# version 22
# baseline (speedup 1.0000x reference)
import numpy as np
from contextlib import ExitStack

import concourse.bass as bass
import concourse.mybir as mybir
from concourse.bass_utils import run_bass_kernel_spmd

F32 = mybir.dt.float32
BF16 = mybir.dt.bfloat16
AF = mybir.ActivationFunctionType
ALU = mybir.AluOpType
AX = mybir.AxisListType

P = 128
D = 1024
DC = D // P
TOWN = 2048
HALO = 64
TX = TOWN + 2 * HALO
NCH = TX // P
NE = 15
NH = 16
HD = 64
EPS = 1e-5
NEG = -1.0e9
NCORE = 8
SEQ = 16384


def tiles(width, maxn=512):
    out = []
    o = 0
    while o < width:
        n = min(maxn, width - o)
        out.append((o, n))
        o += n
    return out


class Ctr:
    def __init__(self, es, nc, name):
        self.h = es.enter_context(nc.semaphore(name))
        self.n = 0


def build_program(upto=99, taps=(), aq=4, anit=16, aout=True, acut=99):
    nc = bass.Bass("TRN2", target_bir_lowering=False)
    taps = set(taps)
    tapnames = []
    uid = [0]

    xT = nc.dram_tensor("xT", [D, TX], F32, kind="ExternalInput").ap()
    masks = nc.dram_tensor("masks", [3, P, 1024], F32, kind="ExternalInput").ap()
    WqT = nc.dram_tensor("WqT", [D, D], F32, kind="ExternalInput").ap()
    WkT = nc.dram_tensor("WkT", [D, D], F32, kind="ExternalInput").ap()
    WvT = nc.dram_tensor("WvT", [D, D], F32, kind="ExternalInput").ap()
    WoT = nc.dram_tensor("WoT", [D, D], F32, kind="ExternalInput").ap()
    bq = nc.dram_tensor("bq", [P, DC], F32, kind="ExternalInput").ap()
    bk = nc.dram_tensor("bk", [P, DC], F32, kind="ExternalInput").ap()
    bv = nc.dram_tensor("bv", [D], F32, kind="ExternalInput").ap()
    bo = nc.dram_tensor("bo", [P, DC], F32, kind="ExternalInput").ap()
    rW = nc.dram_tensor("rW", [D, NE], F32, kind="ExternalInput").ap()
    rbias = nc.dram_tensor("rbias", [NE], F32, kind="ExternalInput").ap()
    sW1 = nc.dram_tensor("sW1", [D, D], F32, kind="ExternalInput").ap()
    sW2 = nc.dram_tensor("sW2", [D, D], F32, kind="ExternalInput").ap()
    sb1 = nc.dram_tensor("sb1", [P, DC], F32, kind="ExternalInput").ap()
    sb2 = nc.dram_tensor("sb2", [D], F32, kind="ExternalInput").ap()
    eW1 = nc.dram_tensor("eW1", [NE, D, D], F32, kind="ExternalInput").ap()
    eW2 = nc.dram_tensor("eW2", [NE, D, D], F32, kind="ExternalInput").ap()
    eb1T = nc.dram_tensor("eb1T", [D, NE], F32, kind="ExternalInput").ap()
    eb2 = nc.dram_tensor("eb2", [NE, D], F32, kind="ExternalInput").ap()

    outT = nc.dram_tensor("outT", [D, TOWN], F32, kind="ExternalOutput").ap()

    nbro = nc.dram_tensor("nbro", [2], mybir.dt.int32, kind="ExternalInput").ap()
    cc_in = nc.dram_tensor("cc_in", [2, D, HALO], F32).ap()
    cc_out_h = nc.dram_tensor("cc_out", [2 * NCORE, D, HALO], F32)
    cc_out = cc_out_h.ap()
    scr_rows = nc.dram_tensor("scr_rows", [2, TX], F32).ap()
    scr_cwt = nc.dram_tensor("scr_cwt", [NE, TX], BF16).ap()

    def mktap(name, shape, dtype=F32):
        if name in taps:
            tapnames.append(name)
            return nc.dram_tensor("tap_" + name, shape, dtype, kind="ExternalOutput").ap()
        return None

    t_z1 = mktap("z1", [P, DC, TX], BF16)
    t_cw1 = mktap("cw1", [P, NCH, NE], F32)
    t_y1 = mktap("y1", [P, DC, TX], F32)
    t_z2 = mktap("z2", [P, DC, TX], BF16)
    t_kT = mktap("kT", [P, DC, 640], BF16)
    t_v = mktap("v", [P, 5, D], BF16)
    t_qT = mktap("qT", [P, DC, 512], BF16)
    t_y2 = mktap("y2", [P, DC, TX], F32)
    t_z3 = mktap("z3", [P, DC, TX], BF16)
    t_cw3 = mktap("cw3", [P, NCH, NE], F32)

    es = ExitStack()
    with es:
        R = es.enter_context(nc.sbuf_tensor("R", [P, DC, TX], F32))
        zT = es.enter_context(nc.sbuf_tensor("zT", [P, DC, TX], BF16))
        bqc = es.enter_context(nc.sbuf_tensor("bqc", [P, DC], F32))
        bkc = es.enter_context(nc.sbuf_tensor("bkc", [P, DC], F32))
        boc = es.enter_context(nc.sbuf_tensor("boc", [P, DC], F32))
        bvb = es.enter_context(nc.sbuf_tensor("bvb", [P, D], BF16))
        sb1c = es.enter_context(nc.sbuf_tensor("sb1c", [P, DC], F32))
        eb1c = es.enter_context(nc.sbuf_tensor("eb1c", [P, DC, NE], F32))
        eb2s = es.enter_context(nc.sbuf_tensor("eb2s", [16, D], BF16))
        rWb = es.enter_context(nc.sbuf_tensor("rWb", [P, DC, NE], BF16))
        rbb = es.enter_context(nc.sbuf_tensor("rbb", [P, NE], F32))
        maskb = es.enter_context(nc.sbuf_tensor("maskb", [P, 3, 1024], BF16))
        identf = es.enter_context(nc.sbuf_tensor("identf", [P, P], F32))
        identb = es.enter_context(nc.sbuf_tensor("identb", [P, P], BF16))
        ones_col = es.enter_context(nc.sbuf_tensor("ones_col", [P, 1], BF16))
        iota15 = es.enter_context(nc.sbuf_tensor("iota15", [P, NE], F32))
        cwf = es.enter_context(nc.sbuf_tensor("cwf", [P, NCH, NE], F32))
        cst = es.enter_context(nc.sbuf_tensor("cst", [P, 3], F32))
        nbrs = es.enter_context(nc.sbuf_tensor("nbrs", [1, 2], mybir.dt.int32))
        rg1 = es.enter_context(nc.gpsimd.register("rg1"))
        rg2 = es.enter_context(nc.gpsimd.register("rg2"))
        s_cc = Ctr(es, nc, "s_cc")
        onesr = es.enter_context(nc.sbuf_tensor("onesr", [1, TX], BF16))
        cwx = es.enter_context(nc.sbuf_tensor("cwx", [16, TX], BF16))

        s_in = Ctr(es, nc, "s_in")
        s_wm = [Ctr(es, nc, f"s_wm{i}") for i in range(3)]
        s_hw = Ctr(es, nc, "s_hw")
        s_cb = [Ctr(es, nc, f"s_cb{i}") for i in range(2)]
        s_pe = Ctr(es, nc, "s_pe")
        s_act = Ctr(es, nc, "s_act")
        s_dve = Ctr(es, nc, "s_dve")
        s_cfree = Ctr(es, nc, "s_cfree")
        s_gp = Ctr(es, nc, "s_gp")
        s_rpe = Ctr(es, nc, "s_rpe")
        s_rdv = Ctr(es, nc, "s_rdv")
        s_rac = Ctr(es, nc, "s_rac")

        def tap3(tapap, sbap):
            with nc.Block() as blk:
                @blk.sync
                def _(sy):
                    sy.dma_start(tapap, sbap).then_inc(s_hw.h, 16)
                    s_hw.n += 16
                    sy.wait_ge(s_hw.h, s_hw.n)

        with ExitStack() as ps:
            ioA = ps.enter_context(nc.sbuf_tensor("ioA", [P, P], F32))
            ioB = ps.enter_context(nc.sbuf_tensor("ioB", [P, P], F32))
            sb2row = ps.enter_context(nc.sbuf_tensor("sb2row", [1, D], F32))
            sb2h = ps.enter_context(nc.sbuf_tensor("sb2h", [1, D], BF16))
            with nc.Block() as blk:
                @blk.gpsimd
                def _(g):
                    def d(out, in_):
                        g.dma_start(out, in_).then_inc(s_in.h, 16)
                        s_in.n += 16
                    d(R[:, 4:8, :], xT.rearrange("(c p) t -> p c t", p=P)[:, 4:8, :])
                    d(maskb[:, :, :], masks.rearrange("m p t -> p m t"))
                    d(nbrs[:1, :], nbro.rearrange("(a b) -> a b", a=1))
                    d(sb2row[:1, :], sb2.rearrange("(a d) -> a d", a=1))
                    d(eb2s[0:NE, :], eb2[:, :])
                    d(bvb[:, :], bv.partition_broadcast(P))
                    d(rWb[:, :, :], rW.rearrange("(c p) e -> p c e", p=P))
                    g.iota(iota15[:, :], [[1, NE]], channel_multiplier=0,
                           allow_small_or_imprecise_dtypes=True)
                    g.iota(ioA[:, :], [[1, P]], channel_multiplier=0,
                           allow_small_or_imprecise_dtypes=True)
                    g.iota(ioB[:, :], [[0, P]], channel_multiplier=1,
                           allow_small_or_imprecise_dtypes=True)
                    g.memset(ones_col[:, :], 1.0)
                    g.memset(onesr[:1, :], 1.0)
                    g.memset(cst[:, 0:1], EPS)
                    g.memset(cst[:, 1:2], 0.5)
                    g.memset(cst[:, 2:3], 0.125)
                    g.wait_ge(s_in.h, s_in.n)

                @blk.sync
                def _(sy):
                    def d2_(out, in_):
                        sy.dma_start(out, in_).then_inc(s_hw.h, 16)
                        s_hw.n += 16
                    d2_(R[:, 0:4, :], xT.rearrange("(c p) t -> p c t", p=P)[:, 0:4, :])
                    d2_(bqc[:, :], bq[:, :])
                    d2_(bkc[:, :], bk[:, :])
                    d2_(boc[:, :], bo[:, :])
                    d2_(sb1c[:, :], sb1[:, :])
                    d2_(eb1c[:, :, :], eb1T.rearrange("(c p) e -> p c e", p=P))
                    d2_(rbb[:, :], rbias.partition_broadcast(P))
                    sy.wait_ge(s_hw.h, s_hw.n)
            with nc.Block() as blk:
                @blk.vector
                def _(v):
                    v.tensor_tensor(out=identf[:, :], in0=ioA[:, :], in1=ioB[:, :], op=ALU.is_equal)
                    v.drain()
                    v.tensor_copy(identb[:, :], identf[:, :])
                    v.tensor_scalar_mul(bqc[:, :], bqc[:, :], 0.125)
                    v.tensor_scalar_mul(sb1c[:, :], sb1c[:, :], 0.5)
                @blk.scalar
                def _(s):
                    s.activation(sb2h[:1, :], sb2row[:1, :], AF.Copy, scale=cst[0:1, 1:2]).then_inc(s_act.h, 1)
                    s_act.n += 1

            with nc.Block() as blk:
                @blk.gpsimd
                def _(g):
                    g.wait_ge(s_act.h, s_act.n)
                    g.dma_start(eb2s[NE:16, :], sb2h[:1, :]).then_inc(s_in.h, 16)
                    s_in.n += 16
                    g.wait_ge(s_in.h, s_in.n)

        def phase_ln(cl0, cln):
            uid[0] += 1
            u = uid[0]
            sl = slice(cl0, cl0 + cln)
            hw_w = cln // 2
            with ExitStack() as ps:
                ssum = ps.enter_context(nc.sbuf_tensor(f"ln_su{u}", [1, TX], F32))
                ssq = ps.enter_context(nc.sbuf_tensor(f"ln_sq{u}", [1, TX], F32))
                srstd = ps.enter_context(nc.sbuf_tensor(f"ln_sr{u}", [1, TX], F32))
                sc0r = ps.enter_context(nc.sbuf_tensor(f"ln_sc{u}", [1, TX], F32))
                rbB = ps.enter_context(nc.sbuf_tensor(f"ln_rb{u}", [P, TX], F32))
                c0B = ps.enter_context(nc.sbuf_tensor(f"ln_c0{u}", [P, TX], F32))
                gsq = ps.enter_context(nc.sbuf_tensor(f"ln_gsq{u}", [P, DC, TX // 2], BF16))
                psA = ps.enter_context(nc.psum_tensor(f"ln_psA{u}", [1, 512], F32))
                psB = ps.enter_context(nc.psum_tensor(f"ln_psB{u}", [1, 512], F32))
                act0, pe0, dve0 = s_act.n, s_pe.n, s_dve.n
                hw1 = s_hw.n
                tl = tiles(hw_w, 512)
                nt = len(tl)
                dve1 = dve0 + 2 + 4 * nt
                act1 = act0 + 2
                NGC = 2
                with nc.Block() as blk:
                    @blk.scalar
                    def _(s):
                        for hw in range(2):
                            if hw == 1:
                                s.wait_ge(s_pe.h, pe0 + nt)
                            c0 = cl0 + hw * hw_w
                            for c in range(DC):
                                mm = s.activation(gsq[:, c, 0:hw_w], R[:, c, c0:c0 + hw_w], AF.Square)
                                if c == DC - 1:
                                    mm.then_inc(s_act.h, 1)
                        s.wait_ge(s_dve.h, dve1 + 1)
                        s.activation(srstd[:, 0:cln], ssq[:, 0:cln], AF.Sqrt,
                                     bias=cst[0:1, 0:1]).then_inc(s_act.h, 1)
                    s_act.n += 3

                    @blk.tensor
                    def _(t):
                        gt = 0
                        for hw in range(2):
                            t.wait_ge(s_act.h, act0 + hw + 1)
                            t.wait_ge(s_dve.h, dve0 + hw + 1)
                            c0 = cl0 + hw * hw_w
                            for ti, (to, tn) in enumerate(tl):
                                if gt >= 1:
                                    t.wait_ge(s_dve.h, dve0 + 2 + 2 * gt)
                                for c in range(DC):
                                    st, sp = (c == 0), (c == DC - 1)
                                    t.matmul(psA[0:1, 0:tn], ones_col[:, :],
                                             zT[:, c, c0 + to: c0 + to + tn], start=st, stop=sp)
                                    mm = t.matmul(psB[0:1, 0:tn], ones_col[:, :],
                                                  gsq[:, c, to:to + tn], start=st, stop=sp)
                                    if sp:
                                        mm.then_inc(s_pe.h, 1)
                                gt += 1
                    s_pe.n += 2 * nt

                    @blk.vector
                    def _(v):
                        for hw in range(2):
                            c0 = cl0 + hw * hw_w
                            for c in range(DC):
                                mm = v.tensor_copy(zT[:, c, c0:c0 + hw_w], R[:, c, c0:c0 + hw_w])
                                if c == DC - 1:
                                    v.drain()
                                    v.nop().then_inc(s_dve.h, 1)
                        gt = 0
                        for hw in range(2):
                            c0r = hw * hw_w
                            for ti, (to, tn) in enumerate(tl):
                                v.wait_ge(s_pe.h, pe0 + gt + 1)
                                v.tensor_copy(ssum[:, c0r + to:c0r + to + tn], psA[0:1, 0:tn])
                                v.drain()
                                v.nop().then_inc(s_dve.h, 1)
                                v.tensor_copy(ssq[:, c0r + to:c0r + to + tn], psB[0:1, 0:tn])
                                v.drain()
                                v.nop().then_inc(s_dve.h, 1)
                                gt += 1
                        cn = slice(0, cln)
                        v.tensor_scalar_mul(ssum[:, cn], ssum[:, cn], -1.0 / D)
                        v.drain()
                        v.tensor_scalar_mul(ssq[:, cn], ssq[:, cn], 1.0 / D)
                        v.drain()
                        v.tensor_tensor(out=sc0r[:, cn], in0=ssum[:, cn], in1=ssum[:, cn], op=ALU.mult)
                        v.drain()
                        v.tensor_tensor(out=ssq[:, cn], in0=ssq[:, cn], in1=sc0r[:, cn], op=ALU.subtract)
                        v.drain()
                        v.nop().then_inc(s_dve.h, 1)
                        v.wait_ge(s_act.h, act1 + 1)
                        v.reciprocal(srstd[:, cn], srstd[:, cn])
                        v.drain()
                        v.tensor_tensor(out=sc0r[:, cn], in0=ssum[:, cn], in1=srstd[:, cn], op=ALU.mult)
                        v.drain()
                        v.nop().then_inc(s_dve.h, 1)
                        v.wait_ge(s_hw.h, hw1 + 64)
                        for c in range(NGC, DC):
                            v.tensor_tensor(out=zT[:, c, sl], in0=R[:, c, sl], in1=rbB[:, 0:cln], op=ALU.mult)
                            v.drain()
                            v.tensor_tensor(out=zT[:, c, sl], in0=zT[:, c, sl], in1=c0B[:, 0:cln], op=ALU.add)
                            v.drain()
                    s_dve.n += 4 * nt + 4

                    @blk.sync
                    def _(sy):
                        sy.wait_ge(s_dve.h, dve1 + 2)
                        sy.dma_start(scr_rows[0:1, 0:cln], srstd[:, 0:cln]).then_inc(s_hw.h, 16)
                        sy.dma_start(scr_rows[1:2, 0:cln], sc0r[:, 0:cln]).then_inc(s_hw.h, 16)
                        s_hw.n += 32
                        sy.wait_ge(s_hw.h, s_hw.n)
                        sy.dma_start(rbB[:, 0:cln], scr_rows[0, 0:cln].partition_broadcast(P)).then_inc(s_hw.h, 16)
                        sy.dma_start(c0B[:, 0:cln], scr_rows[1, 0:cln].partition_broadcast(P)).then_inc(s_hw.h, 16)
                        s_hw.n += 32
                        sy.wait_ge(s_hw.h, s_hw.n)

                    @blk.gpsimd
                    def _(g):
                        g.wait_ge(s_hw.h, hw1 + 64)
                        for c in range(NGC):
                            g.tensor_tensor(out=zT[:, c, sl], in0=R[:, c, sl], in1=rbB[:, 0:cln], op=ALU.mult)
                            g.drain()
                            g.tensor_tensor(out=zT[:, c, sl], in0=zT[:, c, sl], in1=c0B[:, 0:cln], op=ALU.add)
                            g.drain()

        def make_router(cl0, cln, cwf, cwx, ps):
            uid[0] += 1
            u = uid[0]
            nch = cln // P
            nf = nch * NE
            prt = ps.enter_context(nc.psum_tensor(f"rt_p{u}", [P, NE], F32))
            ptw = ps.enter_context(nc.psum_tensor(f"rt_t{u}", [NE, P], F32))
            logt = ps.enter_context(nc.sbuf_tensor(f"rt_lg{u}", [P, NCH, NE], F32))
            probs = ps.enter_context(nc.sbuf_tensor(f"rt_pr{u}", [P, NCH, NE], F32))
            tt1 = ps.enter_context(nc.sbuf_tensor(f"rt_t1{u}", [P, NCH, NE], F32))
            nmx = ps.enter_context(nc.sbuf_tensor(f"rt_nm{u}", [P, 1], F32))
            f17 = ps.enter_context(nc.sbuf_tensor(f"rt_f17{u}", [P, NCH], F32))
            r17 = ps.enter_context(nc.sbuf_tensor(f"rt_r17{u}", [P, NCH], F32))
            iota15m = ps.enter_context(nc.sbuf_tensor(f"rt_iom{u}", [P, NE], F32))
            pe0r, dve0r, act0r = s_rpe.n, s_rdv.n, s_rac.n
            dve1r = dve0r + nch
            dve2r = dve1r + 2
            pe1r = pe0r + nch
            st = {"cwx_ev": dve2r + nch + 1}

            L = logt.ap().rearrange("p a b -> p (a b)")[:, 0:nf].rearrange("p (a b) -> p a b", b=NE)
            PR = probs.ap().rearrange("p a b -> p (a b)")[:, 0:nf].rearrange("p (a b) -> p a b", b=NE)
            T1 = tt1.ap().rearrange("p a b -> p (a b)")[:, 0:nf].rearrange("p (a b) -> p a b", b=NE)
            CW = cwf.ap().rearrange("p a b -> p (a b)")[:, 0:nf].rearrange("p (a b) -> p a b", b=NE)
            Lf = logt.ap().rearrange("p a b -> p (a b)")[:, 0:nf]
            PRf = probs.ap().rearrange("p a b -> p (a b)")[:, 0:nf]
            T1f = tt1.ap().rearrange("p a b -> p (a b)")[:, 0:nf]
            CWf = cwf.ap().rearrange("p a b -> p (a b)")[:, 0:nf]
            io_b = iota15.ap().unsqueeze(1).broadcast_to((P, nch, NE))
            iom_b = iota15m.ap().unsqueeze(1).broadcast_to((P, nch, NE))

            def pe_logits(t):
                for k in range(nch):
                    if k >= 1:
                        t.wait_ge(s_rdv.h, dve0r + k)
                    for c in range(DC):
                        mm = t.matmul(prt[:, :], zT[:, c, cl0 + k * P: cl0 + (k + 1) * P],
                                      rWb[:, c, :], start=(c == 0), stop=(c == DC - 1))
                        if c == DC - 1:
                            mm.then_inc(s_rpe.h, 1)
                s_rpe.n += nch

            def pe_trans(t):
                t.wait_ge(s_rdv.h, dve1r + 2)
                for k in range(nch):
                    if k >= 1:
                        t.wait_ge(s_rdv.h, dve2r + k)
                    t.transpose(ptw[0:NE, :], cwf[:, k, :], identf[:, :]).then_inc(s_rpe.h, 1)
                s_rpe.n += nch

            def dve(v):
                v.tensor_scalar_mul(iota15m[:, :], iota15[:, :], 1.0)
                v.drain()
                v.tensor_scalar_add(out=iota15m[:, :], in0=iota15m[:, :], scalar1=-100.0)
                v.drain()
                for k in range(nch):
                    v.wait_ge(s_rpe.h, pe0r + k + 1)
                    v.tensor_tensor(out=logt[:, k, :], in0=prt[:, :], in1=rbb[:, :], op=ALU.add)
                    v.drain()
                    v.nop().then_inc(s_rdv.h, 1)
                v.tensor_reduce(out=nmx[:, 0:1], in_=Lf, axis=AX.X, op=ALU.max, negate=True)
                v.drain()
                v.nop().then_inc(s_rdv.h, 1)
                v.wait_ge(s_rac.h, act0r + 1)
                v.tensor_reduce(out=f17[:, 0:nch], in_=PR, axis=AX.X, op=ALU.add)
                v.drain()
                v.reciprocal(r17[:, 0:nch], f17[:, 0:nch])
                v.drain()
                v.tensor_tensor(out=PR, in0=PR,
                                in1=r17[:, 0:nch].unsqueeze(2).broadcast_to((P, nch, NE)), op=ALU.mult)
                v.drain()
                v.memset(CWf, 0.0)
                v.drain()
                for it in range(3):
                    v.tensor_reduce(out=f17[:, 0:nch], in_=PR, axis=AX.X, op=ALU.max, negate=True)
                    v.drain()
                    fb = f17[:, 0:nch].unsqueeze(2).broadcast_to((P, nch, NE))
                    v.tensor_tensor(out=T1, in0=PR, in1=fb, op=ALU.add)
                    v.drain()
                    v.tensor_single_scalar(out=T1f, in_=T1f, scalar=0.0, op=ALU.is_equal)
                    v.drain()
                    v.tensor_tensor(out=T1, in0=T1, in1=iom_b, op=ALU.mult)
                    v.drain()
                    v.tensor_scalar_add(out=T1f, in0=T1f, scalar1=100.0)
                    v.drain()
                    v.tensor_reduce(out=f17[:, 0:nch], in_=T1, axis=AX.X, op=ALU.min)
                    v.drain()
                    fb = f17[:, 0:nch].unsqueeze(2).broadcast_to((P, nch, NE))
                    v.tensor_tensor(out=T1, in0=io_b, in1=fb, op=ALU.subtract)
                    v.drain()
                    v.tensor_single_scalar(out=T1f, in_=T1f, scalar=0.0, op=ALU.is_equal)
                    v.drain()
                    v.tensor_reduce(out=r17[:, 0:nch], in_=PR, axis=AX.X, op=ALU.max, negate=True)
                    v.drain()
                    rb2 = r17[:, 0:nch].unsqueeze(2).broadcast_to((P, nch, NE))
                    v.tensor_tensor(out=T1, in0=T1, in1=rb2, op=ALU.mult)
                    v.drain()
                    v.tensor_tensor(out=CW, in0=CW, in1=T1, op=ALU.subtract)
                    v.drain()
                    v.tensor_tensor(out=PR, in0=PR, in1=T1, op=ALU.add)
                    v.drain()
                v.tensor_scalar_mul(CWf, CWf, 0.5)
                v.drain()
                v.nop().then_inc(s_rdv.h, 1)
                for k in range(nch):
                    v.wait_ge(s_rpe.h, pe1r + k + 1)
                    v.tensor_copy(cwx[0:NE, cl0 + k * P: cl0 + (k + 1) * P], ptw[0:NE, :])
                    v.drain()
                    v.nop().then_inc(s_rdv.h, 1)
                v.nop().then_inc(s_rdv.h, 1)
                s_rdv.n += nch + 2 + nch + 1

            def act(sc):
                sc.wait_ge(s_rdv.h, dve1r + 1)
                sc.activation(PRf, Lf, AF.Exp, bias=nmx[:, 0:1]).then_inc(s_rac.h, 1)
                s_rac.n += 1

            def sync(sy):
                sy.wait_ge(s_rdv.h, st["cwx_ev"])
                sy.dma_start(scr_cwt[:, cl0:cl0 + cln], cwx[0:NE, cl0:cl0 + cln]).then_inc(s_hw.h, 16)
                s_hw.n += 16
                sy.wait_ge(s_hw.h, s_hw.n)

            return {"pe_logits": pe_logits, "pe_trans": pe_trans, "dve": dve,
                    "act": act, "sync": sync, "cwx_ev": st["cwx_ev"]}

        def phase_ffn(cl0, cln, halves, cwf, cwx, stream_out=False):
            uid[0] += 1
            u = uid[0]
            nmat = 2 * (NE + 1)
            nh = len(halves)
            with ExitStack() as ps:
                pA = ps.enter_context(nc.psum_tensor(f"ff_pA{u}", [P, 1152], F32))
                pB = ps.enter_context(nc.psum_tensor(f"ff_pB{u}", [P, 1152], F32))
                w0 = ps.enter_context(nc.sbuf_tensor(f"ff_w0{u}", [P, DC, D], BF16))
                w1 = ps.enter_context(nc.sbuf_tensor(f"ff_w1{u}", [P, DC, D], BF16))
                w2 = ps.enter_context(nc.sbuf_tensor(f"ff_w2{u}", [P, DC, D], BF16))
                wb = [w0, w1, w2]
                gbuf = ps.enter_context(nc.sbuf_tensor(f"ff_g{u}", [P, DC, 1152], BF16))
                rb0 = ps.enter_context(nc.sbuf_tensor(f"ff_r0{u}", [P, 1152], BF16))
                rb1 = ps.enter_context(nc.sbuf_tensor(f"ff_r1{u}", [P, 1152], BF16))
                rbufs = [rb0, rb1]
                cb0 = ps.enter_context(nc.sbuf_tensor(f"ff_c0{u}", [P, 1152], BF16))
                cb1 = ps.enter_context(nc.sbuf_tensor(f"ff_c1{u}", [P, 1152], BF16))
                cbufs = [cb0, cb1]

                rt = make_router(cl0, cln, cwf, cwx, ps)
                pe0, act0, dve0 = s_pe.n, s_act.n, s_dve.n
                cf0 = s_cfree.n
                wtgt = []
                for m in range(nmat):
                    c = s_wm[m % 3]
                    c.n += 16
                    wtgt.append(c.n)
                ctgt = {}
                for e in range(NE):
                    for hf in range(nh):
                        c = s_cb[e % 2]
                        c.n += 16
                        ctgt[(e, hf)] = c.n

                def pe_h(ei, hf, dh):
                    return pe0 + ei * 32 + hf * 16 + dh + 1

                def pe_ye(ei, hf, do):
                    return pe0 + ei * 32 + hf * 16 + 8 + do + 1

                def act_relu(ei, hf, dh):
                    return act0 + ei * 16 + hf * 8 + dh + 1

                def dve_g(ei, hf, dh):
                    return dve0 + 16 + (ei - 1) * 32 + hf * 16 + dh + 1

                def dve_R(ei, hf, do):
                    if ei == 0:
                        return dve0 + hf * 8 + do + 1
                    return dve0 + 16 + (ei - 1) * 32 + hf * 16 + 8 + do + 1

                GSEQ = []
                for ei in range(NE + 1):
                    for hf in range(nh):
                        for dh in range(8):
                            GSEQ.append(("h", ei, hf, dh))
                        for do in range(8):
                            GSEQ.append(("ye", ei, hf, do))

                def free_count(gidx):
                    kind, ei, hf, x = GSEQ[gidx]
                    if kind == "h":
                        return (s_act.h, act_relu(ei, hf, x))
                    return (s_dve.h, dve_R(ei, hf, x))

                with nc.Block() as blk:
                    @blk.gpsimd
                    def _(g):
                        for m in range(nmat):
                            ei, is2 = divmod(m, 2)
                            if ei == 0:
                                src = sW2 if is2 else sW1
                            else:
                                src = (eW2 if is2 else eW1)[ei - 1]
                            if m >= 3:
                                mp = m - 3
                                eip, is2p = divmod(mp, 2)
                                if is2p == 0:
                                    g.wait_ge(s_pe.h, pe_h(eip, nh - 1, 7))
                                else:
                                    g.wait_ge(s_pe.h, pe_ye(eip, nh - 1, 7))
                            g.dma_start(wb[m % 3][:, :, :],
                                        src.rearrange("(c p) n -> p c n", p=P)).then_inc(s_wm[m % 3].h, 16)
                        for i in range(3):
                            g.wait_ge(s_wm[i].h, s_wm[i].n)

                    @blk.sync
                    def _(sy):
                        rt["sync"](sy)
                        for e in range(NE):
                            for hf, (ho, hn) in enumerate(halves):
                                if hf >= 1:
                                    sy.wait_ge(s_cfree.h, cf0 + 2 * e + 1)
                                elif e >= 2:
                                    sy.wait_ge(s_cfree.h, cf0 + 2 * (e - 2) + 2)
                                sy.dma_start(cbufs[e % 2][:, 0:hn],
                                             scr_cwt[e, cl0 + ho: cl0 + ho + hn].partition_broadcast(P)
                                             ).then_inc(s_cb[e % 2].h, 16)
                        for i in range(2):
                            sy.wait_ge(s_cb[i].h, s_cb[i].n)
                        if stream_out:
                            for hf, (ho, hn) in enumerate(halves):
                                sy.wait_ge(s_dve.h, dve_R(NE, hf, 7))
                                sy.dma_start(
                                    outT.rearrange("(c p) t -> p c t", p=P)[:, :, ho: ho + hn],
                                    R[:, :, cl0 + ho: cl0 + ho + hn]).then_inc(s_hw.h, 16)
                                s_hw.n += 16
                            sy.wait_ge(s_hw.h, s_hw.n)

                    @blk.tensor
                    def _(t):
                        rt["pe_logits"](t)
                        ebwait = [True]
                        gidx = 0
                        for ei in range(NE + 1):
                            m1, m2 = 2 * ei, 2 * ei + 1
                            for hf, (ho, hn) in enumerate(halves):
                                tl = tiles(hn, 512)
                                if hf == 0:
                                    t.wait_ge(s_wm[m1 % 3].h, wtgt[m1])
                                for dh in range(8):
                                    bank = pA if gidx % 2 == 0 else pB
                                    if gidx >= 2:
                                        sh, cn = free_count(gidx - 2)
                                        t.wait_ge(sh, cn)
                                    for c in range(DC):
                                        for ti, (to, tn) in enumerate(tl):
                                            mm = t.matmul(bank[:, to:to + tn],
                                                          wb[m1 % 3][:, c, dh * P:(dh + 1) * P],
                                                          zT[:, c, cl0 + ho + to: cl0 + ho + to + tn],
                                                          start=(c == 0), stop=(c == DC - 1))
                                            if c == DC - 1 and ti == len(tl) - 1:
                                                mm.then_inc(s_pe.h, 1)
                                    gidx += 1
                                if ei == 0 and hf == 0:
                                    rt["pe_trans"](t)
                                if hf == 0:
                                    t.wait_ge(s_wm[m2 % 3].h, wtgt[m2])
                                for do in range(8):
                                    bank = pA if gidx % 2 == 0 else pB
                                    if gidx >= 2:
                                        sh, cn = free_count(gidx - 2)
                                        t.wait_ge(sh, cn)
                                    for dh in range(8):
                                        if do == 0:
                                            if ei == 0:
                                                t.wait_ge(s_act.h, act_relu(0, hf, dh))
                                            else:
                                                t.wait_ge(s_dve.h, dve_g(ei, hf, dh))
                                        for ti, (to, tn) in enumerate(tl):
                                            last = (dh == DC - 1) and (ei != 0)
                                            mm = t.matmul(bank[:, to:to + tn],
                                                          wb[m2 % 3][:, dh, do * P:(do + 1) * P],
                                                          gbuf[:, dh, to:to + tn],
                                                          start=(dh == 0), stop=last)
                                            if last and ti == len(tl) - 1:
                                                mm.then_inc(s_pe.h, 1)
                                    if ei == 0:
                                        if ebwait[0]:
                                            t.wait_ge(s_rdv.h, rt["cwx_ev"])
                                            ebwait[0] = False
                                        for ti, (to, tn) in enumerate(tl):
                                            mm = t.matmul(bank[:, to:to + tn],
                                                          eb2s[:, do * P:(do + 1) * P],
                                                          cwx[:, cl0 + ho + to: cl0 + ho + to + tn],
                                                          start=False, stop=True)
                                            if ti == len(tl) - 1:
                                                mm.then_inc(s_pe.h, 1)
                                    gidx += 1
                    s_pe.n += (NE + 1) * nh * 16

                    @blk.scalar
                    def _(s):
                        rt["act"](s)
                        for ei in range(NE + 1):
                            for hf, (ho, hn) in enumerate(halves):
                                for dh in range(8):
                                    gidx = ei * 32 + hf * 16 + dh
                                    bank = pA if gidx % 2 == 0 else pB
                                    s.wait_ge(s_pe.h, pe_h(ei, hf, dh))
                                    if ei == 0:
                                        s.activation(gbuf[:, dh, 0:hn], bank[:, 0:hn], AF.Relu,
                                                     bias=sb1c[:, dh:dh + 1],
                                                     scale=cst[:, 1:2]).then_inc(s_act.h, 1)
                                    else:
                                        ridx = (ei - 1) * 16 + hf * 8 + dh
                                        if ridx >= 2:
                                            e2, r2 = divmod(ridx - 2, 16)
                                            hf2, dh2 = divmod(r2, 8)
                                            s.wait_ge(s_dve.h, dve_g(e2 + 1, hf2, dh2))
                                        s.activation(rbufs[ridx % 2][:, 0:hn], bank[:, 0:hn], AF.Relu,
                                                     bias=eb1c[:, dh, ei - 1:ei]).then_inc(s_act.h, 1)
                    s_act.n += (NE + 1) * nh * 8

                    @blk.vector
                    def _(v):
                        rt["dve"](v)
                        for ei in range(NE + 1):
                            for hf, (ho, hn) in enumerate(halves):
                                if ei >= 1:
                                    e = ei - 1
                                    v.wait_ge(s_cb[e % 2].h, ctgt[(e, hf)])
                                    for dh in range(8):
                                        ridx = e * 16 + hf * 8 + dh
                                        v.wait_ge(s_act.h, act_relu(ei, hf, dh))
                                        v.tensor_tensor(out=gbuf[:, dh, 0:hn], in0=rbufs[ridx % 2][:, 0:hn],
                                                        in1=cbufs[e % 2][:, 0:hn], op=ALU.mult)
                                        v.drain()
                                        v.nop().then_inc(s_dve.h, 1)
                                    v.nop().then_inc(s_cfree.h, 1)
                                for do in range(8):
                                    v.wait_ge(s_pe.h, pe_ye(ei, hf, do))
                                    gidx = ei * 32 + hf * 16 + 8 + do
                                    bank = pA if gidx % 2 == 0 else pB
                                    v.tensor_tensor(out=R[:, do, cl0 + ho: cl0 + ho + hn],
                                                    in0=bank[:, 0:hn],
                                                    in1=R[:, do, cl0 + ho: cl0 + ho + hn], op=ALU.add)
                                    v.drain()
                                    v.nop().then_inc(s_dve.h, 1)
                    s_dve.n += NE * nh * 8 + (NE + 1) * nh * 8
                    s_cfree.n += NE * nh

        def phase_attn():
            uid[0] += 1
            u = uid[0]
            with ExitStack() as ps:
                pQ = [[ps.enter_context(nc.psum_tensor(f"at_S{i}{k}{u}", [P, 512], F32))
                       for k in range(2)] for i in range(2)]
                pT2 = [ps.enter_context(nc.psum_tensor(f"at_T{i}{u}", [P, 8, P], BF16)) for i in range(2)]
                pC2 = [ps.enter_context(nc.psum_tensor(f"at_C{i}{u}", [P, 2, P], F32)) for i in range(2)]
                pSa = pQ[0][0].ap()
                pSb = pQ[0][1].ap()
                wA = ps.enter_context(nc.sbuf_tensor(f"at_wA{u}", [P, DC, D], BF16))
                wB = ps.enter_context(nc.sbuf_tensor(f"at_wB{u}", [P, DC, D], BF16))
                kTb = ps.enter_context(nc.sbuf_tensor(f"at_kT{u}", [P, DC, 640], BF16))
                vbuf = ps.enter_context(nc.sbuf_tensor(f"at_v{u}", [P, 5, D], BF16))
                qTb = ps.enter_context(nc.sbuf_tensor(f"at_qT{u}", [P, DC, 512], BF16))
                ctxT = ps.enter_context(nc.sbuf_tensor(f"at_cT{u}", [P, DC, 512], BF16))
                esc = [ps.enter_context(nc.sbuf_tensor(f"at_es{i}{u}", [P, 4, 256], BF16)) for i in range(3)]
                nmx2 = [ps.enter_context(nc.sbuf_tensor(f"at_n2{i}{u}", [P, 2], F32)) for i in range(2)]
                atb = [ps.enter_context(nc.sbuf_tensor(f"at_ab{i}{u}", [P, 4, 256], BF16)) for i in range(2)]
                atT = [ps.enter_context(nc.sbuf_tensor(f"at_aT{i}{u}", [P, 8, P], BF16)) for i in range(2)]
                nmx = [ps.enter_context(nc.sbuf_tensor(f"at_nm{i}{u}", [P, 1], F32)) for i in range(2)]
                sro = [ps.enter_context(nc.sbuf_tensor(f"at_sr{i}{u}", [P, 4], F32)) for i in range(2)]
                rec = [ps.enter_context(nc.sbuf_tensor(f"at_rc{i}{u}", [P, 4], F32)) for i in range(2)]

                prev_e3 = [None]
                prev_e4 = [None]
                prev_pe_s_last = [None]
                prev_pe_c_last = [None]
                prev_dve_out = [None]
                qblk = nc.Block()
                blk = qblk.__enter__()
                for qh in range(aq):
                    kc0 = 512 * qh
                    qc0 = HALO + 512 * qh
                    awt = []
                    for mi in range(4):
                        c = s_wm[mi % 2]
                        c.n += 16
                        awt.append(c.n)
                    pe0, act0, dve0 = s_pe.n, s_act.n, s_dve.n
                    ktl = tiles(640, 512)

                    def prj_free(gi, act0=act0, dve0=dve0):
                        if gi < 16:
                            return (s_act.h, act0 + gi + 1)
                        if gi < 26:
                            return (s_dve.h, dve0 + (gi - 16) + 1)
                        return (s_act.h, act0 + 16 + (gi - 26) + 1)

                    if True:
                        @blk.gpsimd
                        def _(g):
                            waits = [prev_e3[0], prev_e4[0], pe0 + 16, pe0 + 26]
                            for mi, src in enumerate((WkT, WvT, WqT, WoT)):
                                if waits[mi] is not None:
                                    g.wait_ge(s_pe.h, waits[mi])
                                g.dma_start((wA if mi % 2 == 0 else wB)[:, :, :],
                                            src.rearrange("(c p) n -> p c n", p=P)).then_inc(s_wm[mi % 2].h, 16)
                            g.wait_ge(s_wm[0].h, awt[2])
                            g.wait_ge(s_wm[1].h, awt[1])

                        @blk.tensor
                        def _(t):
                            gi = 0
                            t.wait_ge(s_wm[0].h, awt[0])
                            if prev_dve_out[0] is not None:
                                t.wait_ge(s_dve.h, prev_dve_out[0])
                            for do in range(8):
                                for (to, tn) in ktl:
                                    bank = pSa if gi % 2 == 0 else pSb
                                    if gi >= 2:
                                        sh, cn = prj_free(gi - 2)
                                        t.wait_ge(sh, cn)
                                    for c in range(DC):
                                        mm = t.matmul(bank[:, 0:tn], wA[:, c, do * P:(do + 1) * P],
                                                      zT[:, c, kc0 + to: kc0 + to + tn],
                                                      start=(c == 0), stop=(c == DC - 1))
                                        if c == DC - 1:
                                            mm.then_inc(s_pe.h, 1)
                                    gi += 1
                            t.wait_ge(s_wm[1].h, awt[1])
                            for kc in range(5):
                                for nt in range(2):
                                    bank = pSa if gi % 2 == 0 else pSb
                                    if gi >= 2:
                                        sh, cn = prj_free(gi - 2)
                                        t.wait_ge(sh, cn)
                                    for c in range(DC):
                                        mm = t.matmul(bank[:, 0:512],
                                                      zT[:, c, kc0 + kc * P: kc0 + (kc + 1) * P],
                                                      wB[:, c, nt * 512:(nt + 1) * 512],
                                                      start=(c == 0), stop=(c == DC - 1))
                                        if c == DC - 1:
                                            mm.then_inc(s_pe.h, 1)
                                    gi += 1
                            t.wait_ge(s_wm[0].h, awt[2])
                            for do in range(8):
                                bank = pSa if gi % 2 == 0 else pSb
                                if gi >= 2:
                                    sh, cn = prj_free(gi - 2)
                                    t.wait_ge(sh, cn)
                                for c in range(DC):
                                    mm = t.matmul(bank[:, 0:512], wA[:, c, do * P:(do + 1) * P],
                                                  zT[:, c, qc0: qc0 + 512],
                                                  start=(c == 0), stop=(c == DC - 1))
                                    if c == DC - 1:
                                        mm.then_inc(s_pe.h, 1)
                                gi += 1
                        s_pe.n += 34
                        prev_e3[0] = pe0 + 34

                        @blk.scalar
                        def _(s):
                            if prev_pe_s_last[0] is not None:
                                s.wait_ge(s_pe.h, prev_pe_s_last[0])
                            gi = 0
                            for do in range(8):
                                for (to, tn) in ktl:
                                    bank = pSa if gi % 2 == 0 else pSb
                                    s.wait_ge(s_pe.h, pe0 + gi + 1)
                                    s.activation(kTb[:, do, to:to + tn], bank[:, 0:tn], AF.Identity,
                                                 bias=bkc[:, do:do + 1]).then_inc(s_act.h, 1)
                                    gi += 1
                            gi = 26
                            for do in range(8):
                                bank = pSa if gi % 2 == 0 else pSb
                                s.wait_ge(s_pe.h, pe0 + gi + 1)
                                s.activation(qTb[:, do, 0:512], bank[:, 0:512], AF.Identity,
                                             bias=bqc[:, do:do + 1], scale=cst[:, 2:3]).then_inc(s_act.h, 1)
                                gi += 1
                        s_act.n += 24

                        @blk.vector
                        def _(v):
                            if prev_pe_c_last[0] is not None:
                                v.wait_ge(s_pe.h, prev_pe_c_last[0])
                            gi = 16
                            for kc in range(5):
                                for nt in range(2):
                                    bank = pSa if gi % 2 == 0 else pSb
                                    v.wait_ge(s_pe.h, pe0 + gi + 1)
                                    v.tensor_tensor(out=vbuf[:, kc, nt * 512:(nt + 1) * 512],
                                                    in0=bank[:, 0:512],
                                                    in1=bvb[:, nt * 512:(nt + 1) * 512], op=ALU.add)
                                    v.drain()
                                    v.nop().then_inc(s_dve.h, 1)
                                    gi += 1
                        s_dve.n += 10

                    pe1, act1, dve1, gp1 = s_pe.n, s_act.n, s_dve.n, s_gp.n
                    NIT = anit

                    def pe_s(j):
                        return pe1 + 3 * j + 1

                    def pe_t(j):
                        return pe1 + 3 * (j + 3) + 2

                    def pe_c(j):
                        return pe1 + 3 * (j + 4) + 3

                    def d1(j):
                        return dve1 + 2 * j + 1

                    def d2(j):
                        return dve1 + 2 * (j + 1) + 2

                    def g_n(j):
                        return gp1 + j + 3

                    def a2(j):
                        return act1 + 3 * (j + 3) + 1

                    def a3(j):
                        return act1 + 3 * (j + 5) + 2

                    def a1(j):
                        return act1 + 3 * j + 3

                    NC2 = NIT + 5
                    act1v, dve1v = s_act.n, s_dve.n

                    if True:
                        @blk.tensor
                        def _(t):
                            t.wait_ge(s_act.h, act1v)
                            t.wait_ge(s_dve.h, dve1v)
                            for i in range(NC2):
                                if 0 <= i < NIT:
                                    j = i
                                    pl, hq = divmod(j, 4)
                                    p_glob = 4 * qh + pl
                                    midx = 0 if p_glob == 0 else (2 if p_glob == 15 else 1)
                                    if j >= 2:
                                        t.wait_ge(s_act.h, a1(j - 2))

                                    def score_mm(e, stop):
                                        h = 4 * hq + e
                                        hp, dc = h % 2, h // 2
                                        return t.matmul(pQ[j % 2][e % 2][:, (e // 2) * 256:(e // 2 + 1) * 256],
                                                        qTb[hp * HD:(hp + 1) * HD, dc, pl * P:(pl + 1) * P],
                                                        kTb[hp * HD:(hp + 1) * HD, dc, pl * P: pl * P + 256],
                                                        start=False, stop=stop)
                                    t.matmul(pQ[j % 2][0][:, :], identb[:, :],
                                             maskb[:, midx, 0:512], start=True, stop=False)
                                    score_mm(0, False)
                                    score_mm(2, True)
                                    t.matmul(pQ[j % 2][1][:, :], identb[:, :],
                                             maskb[:, midx, 512:1024], start=True, stop=False)
                                    score_mm(1, False)
                                    score_mm(3, True).then_inc(s_pe.h, 1)
                                else:
                                    t.drain()
                                    t.nop().then_inc(s_pe.h, 1)
                                if 0 <= i - 3 < NIT:
                                    j = i - 3
                                    t.wait_ge(s_gp.h, g_n(j))
                                    if j >= 2:
                                        t.wait_ge(s_act.h, a2(j - 2))
                                    pT = pT2[j % 2]
                                    for sidx in range(4):
                                        t.transpose(pT[:, 2 * sidx, :], atb[j % 2][:, sidx, 0:128], identb[:, :])
                                        mm = t.transpose(pT[:, 2 * sidx + 1, :], atb[j % 2][:, sidx, 128:256], identb[:, :])
                                        if sidx == 3:
                                            mm.then_inc(s_pe.h, 1)
                                else:
                                    t.drain()
                                    t.nop().then_inc(s_pe.h, 1)
                                if 0 <= i - 4 < NIT:
                                    j = i - 4
                                    pl, hq = divmod(j, 4)
                                    t.wait_ge(s_act.h, a2(j))
                                    if j >= 2:
                                        t.wait_ge(s_act.h, a3(j - 2))
                                    pC = pC2[j % 2]
                                    for sidx, e in enumerate((0, 2, 1, 3)):
                                        h = 4 * hq + e
                                        hp = h % 2
                                        t.matmul(pC[hp * HD:(hp + 1) * HD, e // 2, :],
                                                 vbuf[:, pl, h * HD:(h + 1) * HD],
                                                 atT[j % 2][:, 2 * sidx, :], start=True, stop=False)
                                        mm = t.matmul(pC[hp * HD:(hp + 1) * HD, e // 2, :],
                                                      vbuf[:, pl + 1, h * HD:(h + 1) * HD],
                                                      atT[j % 2][:, 2 * sidx + 1, :], start=False, stop=True)
                                        if sidx == 3:
                                            mm.then_inc(s_pe.h, 1)
                                else:
                                    t.drain()
                                    t.nop().then_inc(s_pe.h, 1)
                        s_pe.n += 3 * NC2

                        @blk.vector
                        def _(v):
                            for i in range(NC2):
                                if 0 <= i < NIT:
                                    j = i
                                    v.wait_ge(s_pe.h, pe_s(j))
                                    if j >= 2:
                                        v.wait_ge(s_act.h, a1(j - 2))
                                    v.tensor_reduce(out=nmx2[j % 2][:, 0:1], in_=pQ[j % 2][0][:, :],
                                                    axis=AX.X, op=ALU.max, negate=True)
                                    v.tensor_reduce(out=nmx2[j % 2][:, 1:2], in_=pQ[j % 2][1][:, :],
                                                    axis=AX.X, op=ALU.max, negate=True)
                                    v.drain()
                                    v.tensor_reduce(out=nmx[j % 2][:, 0:1], in_=nmx2[j % 2][:, :],
                                                    axis=AX.X, op=ALU.min)
                                    v.drain()
                                    v.nop().then_inc(s_dve.h, 1)
                                else:
                                    v.drain()
                                    v.nop().then_inc(s_dve.h, 1)
                                if 0 <= i - 1 < NIT:
                                    j = i - 1
                                    v.wait_ge(s_act.h, a1(j))
                                    if j >= 2:
                                        v.wait_ge(s_gp.h, g_n(j - 2))
                                    v.tensor_reduce(out=sro[j % 2][:, 0:2], in_=esc[j % 3][:, 0:2, :],
                                                    axis=AX.X, op=ALU.add)
                                    v.tensor_reduce(out=sro[j % 2][:, 2:4], in_=esc[j % 3][:, 2:4, :],
                                                    axis=AX.X, op=ALU.add)
                                    v.drain()
                                    v.reciprocal(rec[j % 2][:, :], sro[j % 2][:, :])
                                    v.drain()
                                    v.nop().then_inc(s_dve.h, 1)
                                else:
                                    v.drain()
                                    v.nop().then_inc(s_dve.h, 1)
                        s_dve.n += 2 * NC2

                        @blk.gpsimd
                        def _(g):
                            for i in range(NC2):
                                if 0 <= i - 2 < NIT:
                                    j = i - 2
                                    g.wait_ge(s_dve.h, d2(j))
                                    if j >= 2:
                                        g.wait_ge(s_pe.h, pe_t(j - 2))
                                    b_ap = rec[j % 2][:, :].unsqueeze(2).broadcast_to((P, 4, 256))
                                    g.tensor_tensor(out=atb[j % 2][:, :, :], in0=esc[j % 3][:, :, :],
                                                    in1=b_ap, op=ALU.mult)
                                    g.drain()
                                    g.nop().then_inc(s_gp.h, 1)
                                else:
                                    g.drain()
                                    g.nop().then_inc(s_gp.h, 1)
                        s_gp.n += NC2

                        @blk.scalar
                        def _(s):
                            for i in range(NC2):
                                if 0 <= i - 3 < NIT:
                                    j = i - 3
                                    s.wait_ge(s_pe.h, pe_t(j))
                                    if j >= 2:
                                        s.wait_ge(s_pe.h, pe_c(j - 2))
                                    s.copy(atT[j % 2][:, :, :], pT2[j % 2][:, :, :]).then_inc(s_act.h, 1)
                                else:
                                    s.drain()
                                    s.nop().then_inc(s_act.h, 1)
                                if 0 <= i - 5 < NIT:
                                    j = i - 5
                                    pl, hq = divmod(j, 4)
                                    s.wait_ge(s_pe.h, pe_c(j))
                                    s.copy(ctxT[:, 2 * hq:2 * hq + 2, pl * P:(pl + 1) * P],
                                           pC2[j % 2][:, :, :]).then_inc(s_act.h, 1)
                                else:
                                    s.drain()
                                    s.nop().then_inc(s_act.h, 1)
                                if 0 <= i < NIT:
                                    j = i
                                    s.wait_ge(s_dve.h, d1(j))
                                    if j >= 3:
                                        s.wait_ge(s_gp.h, g_n(j - 3))
                                        s.wait_ge(s_dve.h, d2(j - 3))
                                    s.activation(esc[j % 3][:, 0:2, :], pQ[j % 2][0][:, :],
                                                 AF.Exp, bias=nmx[j % 2][:, 0:1])
                                    s.activation(esc[j % 3][:, 2:4, :], pQ[j % 2][1][:, :],
                                                 AF.Exp, bias=nmx[j % 2][:, 0:1]).then_inc(s_act.h, 1)
                                else:
                                    s.drain()
                                    s.nop().then_inc(s_act.h, 1)
                        s_act.n += 3 * NC2

                    prev_pe_s_last[0] = pe_s(NIT - 1)
                    prev_pe_c_last[0] = pe_c(NIT - 1)
                    if not aout:
                        prev_e3[0] = None
                        prev_e4[0] = None
                        continue
                    pe2, dve2 = s_pe.n, s_dve.n
                    a3last = a3(NIT - 1)
                    if True:
                        @blk.tensor
                        def _(t):
                            t.wait_ge(s_wm[1].h, awt[3])
                            t.wait_ge(s_act.h, a3last)
                            for do in range(8):
                                bank = pSa if do % 2 == 0 else pSb
                                if do >= 2:
                                    t.wait_ge(s_dve.h, dve2 + do - 1)
                                for c in range(DC):
                                    mm = t.matmul(bank[:, 0:512], wB[:, c, do * P:(do + 1) * P],
                                                  ctxT[:, c, 0:512], start=(c == 0), stop=(c == DC - 1))
                                    if c == DC - 1:
                                        mm.then_inc(s_pe.h, 1)
                        s_pe.n += 8
                        prev_e4[0] = pe2 + 8

                        @blk.vector
                        def _(v):
                            for do in range(8):
                                bank = pSa if do % 2 == 0 else pSb
                                v.wait_ge(s_pe.h, pe2 + do + 1)
                                v.scalar_tensor_tensor(out=R[:, do, qc0:qc0 + 512],
                                                       in0=bank[:, 0:512], scalar=boc[:, do:do + 1],
                                                       in1=R[:, do, qc0:qc0 + 512],
                                                       op0=ALU.add, op1=ALU.add)
                                v.drain()
                                v.nop().then_inc(s_dve.h, 1)
                        s_dve.n += 8
                    prev_dve_out[0] = s_dve.n
                qblk.__exit__(None, None, None)

        if upto >= 1:
            phase_ln(HALO, TOWN)
            if t_z1 is not None:
                tap3(t_z1, zT[:, :, :])
        if upto >= 2:
            with ExitStack() as cs:
                cwf = cs.enter_context(nc.sbuf_tensor("cwf_1", [P, NCH, NE], F32))
                cwx = cs.enter_context(nc.sbuf_tensor("cwx_1", [16, TX], BF16))
                with nc.Block() as blk:
                    @blk.gpsimd
                    def _(g):
                        g.dma_start(cwx[NE:16, :], onesr[:1, :]).then_inc(s_in.h, 16)
                        s_in.n += 16
                        g.wait_ge(s_in.h, s_in.n)
                if upto >= 3:
                    phase_ffn(HALO, TOWN, [(0, 1024), (1024, 1024)], cwf, cwx)
                    if t_cw1 is not None:
                        tap3(t_cw1, cwf[:, :, :])
            if upto >= 3:
                with nc.Block() as blk:
                    @blk.gpsimd
                    def _(g):
                        g.dma_start(cc_in[0].rearrange("(c p) t -> p c t", p=P),
                                    R[:, :, HALO:2 * HALO]).then_inc(s_in.h, 16)
                        s_in.n += 16
                        g.dma_start(cc_in[1].rearrange("(c p) t -> p c t", p=P),
                                    R[:, :, TOWN:TOWN + HALO]).then_inc(s_in.h, 16)
                        s_in.n += 16
                        g.wait_ge(s_in.h, s_in.n)
                        g.collective_compute(
                            "AllGather", ALU.bypass,
                            replica_groups=[list(range(NCORE))],
                            ins=[cc_in.opt()], outs=[cc_out.opt()],
                        ).then_inc(s_cc.h, 1)
                        s_cc.n += 1
                        g.wait_ge(s_cc.h, s_cc.n)
                        g.reg_load(rg1, nbrs[0:1, 0:1])
                        g.reg_load(rg2, nbrs[0:1, 1:2])
                        g.dma_start(R[:, :, 0:HALO],
                                    bass.AP(cc_out_h, rg1, [[HALO, P], [P * HALO, DC], [1, HALO]])
                                    ).then_inc(s_in.h, 16)
                        s_in.n += 16
                        g.dma_start(R[:, :, TOWN + HALO:TX],
                                    bass.AP(cc_out_h, rg2, [[HALO, P], [P * HALO, DC], [1, HALO]])
                                    ).then_inc(s_in.h, 16)
                        s_in.n += 16
                        g.wait_ge(s_in.h, s_in.n)
            if upto >= 3 and t_y1 is not None:
                tap3(t_y1, R[:, :, :])
        if upto >= 4:
            phase_ln(0, TX)
            if t_z2 is not None:
                tap3(t_z2, zT[:, :, :])
        if upto >= 5:
            phase_attn()
            if t_y2 is not None:
                tap3(t_y2, R[:, :, :])
        if upto >= 6:
            phase_ln(HALO, TOWN)
            if t_z3 is not None:
                tap3(t_z3, zT[:, :, :])
            with ExitStack() as cs:
                cwf = cs.enter_context(nc.sbuf_tensor("cwf_2", [P, NCH, NE], F32))
                cwx = cs.enter_context(nc.sbuf_tensor("cwx_2", [16, TX], BF16))
                with nc.Block() as blk:
                    @blk.gpsimd
                    def _(g):
                        g.dma_start(cwx[NE:16, :], onesr[:1, :]).then_inc(s_in.h, 16)
                        s_in.n += 16
                        g.wait_ge(s_in.h, s_in.n)
                phase_ffn(HALO, TOWN, [(0, 1024), (1024, 1024)], cwf, cwx, stream_out=True)
                if t_cw3 is not None:
                    tap3(t_cw3, cwf[:, :, :])

        if upto < 6:
            with nc.Block() as blk:
                @blk.sync
                def _(sy):
                    sy.dma_start(outT.rearrange("(c p) t -> p c t", p=P),
                                 R[:, :, HALO:HALO + TOWN]).then_inc(s_hw.h, 16)
                    s_hw.n += 16
                    sy.wait_ge(s_hw.h, s_hw.n)

    return nc, tapnames


def make_masks(core):
    r = np.arange(P)[:, None]
    jj = np.arange(256)[None, :]
    band = (jj - r >= 32) & (jj - r <= 96)
    mi = np.where(band, 0.0, NEG).astype(np.float32)
    ma = np.where(band & (jj >= 64), 0.0, NEG).astype(np.float32) if core == 0 else mi
    mc = np.where(band & (jj < 192), 0.0, NEG).astype(np.float32) if core == NCORE - 1 else mi
    st = np.stack([ma, mi, mc]).astype(np.float32)
    return np.ascontiguousarray(np.tile(st, (1, 1, 4)))


def prep_inputs(L0_attn_x, L0_attn_Wq, L0_attn_bq, L0_attn_Wk, L0_attn_bk,
                L0_attn_Wv, L0_attn_bv, L0_attn_Wo, L0_attn_bo,
                L0_ffn_router_W, L0_ffn_router_b, L0_ffn_sW1, L0_ffn_sb1,
                L0_ffn_sW2, L0_ffn_sb2, L0_ffn_eW1, L0_ffn_eb1,
                L0_ffn_eW2, L0_ffn_eb2):
    x = np.asarray(L0_attn_x, dtype=np.float32).reshape(SEQ, D)
    xTf = np.ascontiguousarray(x.T)
    xTpad = np.zeros((D, SEQ + 2 * HALO), np.float32)
    xTpad[:, HALO:HALO + SEQ] = xTf
    common = {
        "WqT": np.ascontiguousarray(np.asarray(L0_attn_Wq, np.float32).T),
        "WkT": np.ascontiguousarray(np.asarray(L0_attn_Wk, np.float32).T),
        "WvT": np.ascontiguousarray(np.asarray(L0_attn_Wv, np.float32).T),
        "WoT": np.ascontiguousarray(np.asarray(L0_attn_Wo, np.float32).T),
        "bq": np.ascontiguousarray(np.asarray(L0_attn_bq, np.float32).reshape(DC, P).T),
        "bk": np.ascontiguousarray(np.asarray(L0_attn_bk, np.float32).reshape(DC, P).T),
        "bv": np.asarray(L0_attn_bv, np.float32),
        "bo": np.ascontiguousarray(np.asarray(L0_attn_bo, np.float32).reshape(DC, P).T),
        "rW": np.asarray(L0_ffn_router_W, np.float32),
        "rbias": np.asarray(L0_ffn_router_b, np.float32),
        "sW1": np.asarray(L0_ffn_sW1, np.float32),
        "sW2": np.asarray(L0_ffn_sW2, np.float32),
        "sb1": np.ascontiguousarray(np.asarray(L0_ffn_sb1, np.float32).reshape(DC, P).T),
        "sb2": np.asarray(L0_ffn_sb2, np.float32),
        "eW1": np.asarray(L0_ffn_eW1, np.float32),
        "eW2": np.asarray(L0_ffn_eW2, np.float32),
        "eb1T": np.ascontiguousarray(np.asarray(L0_ffn_eb1, np.float32).T),
        "eb2": np.asarray(L0_ffn_eb2, np.float32),
    }
    in_maps = []
    for c in range(NCORE):
        m = dict(common)
        st = c * TOWN
        m["xT"] = np.ascontiguousarray(xTpad[:, st:st + TX])
        m["masks"] = make_masks(c)
        blk_elems = D * HALO
        left_src = (((c + NCORE - 1) % NCORE) * 2 + 1) * blk_elems
        right_src = (((c + 1) % NCORE) * 2 + 0) * blk_elems
        m["nbro"] = np.array([left_src, right_src], np.int32)
        in_maps.append(m)
    return in_maps


def kernel(**inputs):
    in_maps = prep_inputs(**inputs)
    nc, _ = build_program()
    res = run_bass_kernel_spmd(nc, in_maps, core_ids=list(range(NCORE)))
    parts = [np.asarray(r["outT"]) for r in res.results]
    full = np.concatenate(parts, axis=1)
    return np.ascontiguousarray(full.T).reshape(1, SEQ, D).astype(np.float32)


# version 23
# speedup vs baseline: 1.1595x; 1.1595x over previous
import numpy as np
from contextlib import ExitStack

import concourse.bass as bass
import concourse.mybir as mybir
from concourse.bass_utils import run_bass_kernel_spmd

F32 = mybir.dt.float32
BF16 = mybir.dt.bfloat16
AF = mybir.ActivationFunctionType
ALU = mybir.AluOpType
AX = mybir.AxisListType

P = 128
D = 1024
DC = D // P
TOWN = 2048
HALO = 64
TX = TOWN + 2 * HALO
NCH = TX // P
NE = 15
NH = 16
HD = 64
EPS = 1e-5
NEG = -1.0e9
NCORE = 8
SEQ = 16384


def tiles(width, maxn=512):
    out = []
    o = 0
    while o < width:
        n = min(maxn, width - o)
        out.append((o, n))
        o += n
    return out


class Ctr:
    def __init__(self, es, nc, name):
        self.h = es.enter_context(nc.semaphore(name))
        self.n = 0


def build_program(upto=99, taps=(), aq=4, anit=16, aout=True, acut=99):
    nc = bass.Bass("TRN2", target_bir_lowering=False)
    taps = set(taps)
    tapnames = []
    uid = [0]

    xT = nc.dram_tensor("xT", [D, TX], F32, kind="ExternalInput").ap()
    masks = nc.dram_tensor("masks", [3, P, 1024], F32, kind="ExternalInput").ap()
    WqT = nc.dram_tensor("WqT", [D, D], F32, kind="ExternalInput").ap()
    WkT = nc.dram_tensor("WkT", [D, D], F32, kind="ExternalInput").ap()
    WvT = nc.dram_tensor("WvT", [D, D], F32, kind="ExternalInput").ap()
    WoT = nc.dram_tensor("WoT", [D, D], F32, kind="ExternalInput").ap()
    bq = nc.dram_tensor("bq", [P, DC], F32, kind="ExternalInput").ap()
    bk = nc.dram_tensor("bk", [P, DC], F32, kind="ExternalInput").ap()
    bv = nc.dram_tensor("bv", [D], F32, kind="ExternalInput").ap()
    bo = nc.dram_tensor("bo", [P, DC], F32, kind="ExternalInput").ap()
    rW = nc.dram_tensor("rW", [D, NE], F32, kind="ExternalInput").ap()
    rbias = nc.dram_tensor("rbias", [NE], F32, kind="ExternalInput").ap()
    sW1 = nc.dram_tensor("sW1", [D, D], F32, kind="ExternalInput").ap()
    sW2 = nc.dram_tensor("sW2", [D, D], F32, kind="ExternalInput").ap()
    sb1 = nc.dram_tensor("sb1", [P, DC], F32, kind="ExternalInput").ap()
    sb2 = nc.dram_tensor("sb2", [D], F32, kind="ExternalInput").ap()
    eW1 = nc.dram_tensor("eW1", [NE, D, D], F32, kind="ExternalInput").ap()
    eW2 = nc.dram_tensor("eW2", [NE, D, D], F32, kind="ExternalInput").ap()
    eb1T = nc.dram_tensor("eb1T", [D, NE], F32, kind="ExternalInput").ap()
    eb2 = nc.dram_tensor("eb2", [NE, D], F32, kind="ExternalInput").ap()

    outT = nc.dram_tensor("outT", [D, TOWN], F32, kind="ExternalOutput").ap()

    scr_rows = nc.dram_tensor("scr_rows", [2, TX], F32).ap()
    scr_cwt = nc.dram_tensor("scr_cwt", [NE, TX], BF16).ap()

    def mktap(name, shape, dtype=F32):
        if name in taps:
            tapnames.append(name)
            return nc.dram_tensor("tap_" + name, shape, dtype, kind="ExternalOutput").ap()
        return None

    t_z1 = mktap("z1", [P, DC, TX], BF16)
    t_cw1 = mktap("cw1", [P, NCH, NE], F32)
    t_y1 = mktap("y1", [P, DC, TX], F32)
    t_z2 = mktap("z2", [P, DC, TX], BF16)
    t_kT = mktap("kT", [P, DC, 640], BF16)
    t_v = mktap("v", [P, 5, D], BF16)
    t_qT = mktap("qT", [P, DC, 512], BF16)
    t_y2 = mktap("y2", [P, DC, TX], F32)
    t_z3 = mktap("z3", [P, DC, TX], BF16)
    t_cw3 = mktap("cw3", [P, NCH, NE], F32)

    es = ExitStack()
    with es:
        R = es.enter_context(nc.sbuf_tensor("R", [P, DC, TX], F32))
        zT = es.enter_context(nc.sbuf_tensor("zT", [P, DC, TX], BF16))
        bqc = es.enter_context(nc.sbuf_tensor("bqc", [P, DC], F32))
        bkc = es.enter_context(nc.sbuf_tensor("bkc", [P, DC], F32))
        boc = es.enter_context(nc.sbuf_tensor("boc", [P, DC], F32))
        bvb = es.enter_context(nc.sbuf_tensor("bvb", [P, D], BF16))
        sb1c = es.enter_context(nc.sbuf_tensor("sb1c", [P, DC], F32))
        eb1c = es.enter_context(nc.sbuf_tensor("eb1c", [P, DC, NE], F32))
        eb2s = es.enter_context(nc.sbuf_tensor("eb2s", [16, D], BF16))
        rWb = es.enter_context(nc.sbuf_tensor("rWb", [P, DC, NE], BF16))
        rbb = es.enter_context(nc.sbuf_tensor("rbb", [P, NE], F32))
        maskb = es.enter_context(nc.sbuf_tensor("maskb", [P, 3, 1024], BF16))
        identf = es.enter_context(nc.sbuf_tensor("identf", [P, P], F32))
        identb = es.enter_context(nc.sbuf_tensor("identb", [P, P], BF16))
        ones_col = es.enter_context(nc.sbuf_tensor("ones_col", [P, 1], BF16))
        iota15 = es.enter_context(nc.sbuf_tensor("iota15", [P, NE], F32))
        cwf = es.enter_context(nc.sbuf_tensor("cwf", [P, NCH, NE], F32))
        cst = es.enter_context(nc.sbuf_tensor("cst", [P, 3], F32))
        onesr = es.enter_context(nc.sbuf_tensor("onesr", [1, TX], BF16))
        cwx = es.enter_context(nc.sbuf_tensor("cwx", [16, TX], BF16))

        s_in = Ctr(es, nc, "s_in")
        s_wm = [Ctr(es, nc, f"s_wm{i}") for i in range(3)]
        s_hw = Ctr(es, nc, "s_hw")
        s_cb = [Ctr(es, nc, f"s_cb{i}") for i in range(2)]
        s_pe = Ctr(es, nc, "s_pe")
        s_act = Ctr(es, nc, "s_act")
        s_dve = Ctr(es, nc, "s_dve")
        s_cfree = Ctr(es, nc, "s_cfree")
        s_gp = Ctr(es, nc, "s_gp")
        s_rpe = Ctr(es, nc, "s_rpe")
        s_rdv = Ctr(es, nc, "s_rdv")
        s_rac = Ctr(es, nc, "s_rac")

        def tap3(tapap, sbap):
            with nc.Block() as blk:
                @blk.sync
                def _(sy):
                    sy.dma_start(tapap, sbap).then_inc(s_hw.h, 16)
                    s_hw.n += 16
                    sy.wait_ge(s_hw.h, s_hw.n)

        with ExitStack() as ps:
            ioA = ps.enter_context(nc.sbuf_tensor("ioA", [P, P], F32))
            ioB = ps.enter_context(nc.sbuf_tensor("ioB", [P, P], F32))
            sb2row = ps.enter_context(nc.sbuf_tensor("sb2row", [1, D], F32))
            sb2h = ps.enter_context(nc.sbuf_tensor("sb2h", [1, D], BF16))
            with nc.Block() as blk:
                @blk.gpsimd
                def _(g):
                    def d(out, in_):
                        g.dma_start(out, in_).then_inc(s_in.h, 16)
                        s_in.n += 16
                    d(R[:, 4:8, :], xT.rearrange("(c p) t -> p c t", p=P)[:, 4:8, :])
                    d(maskb[:, :, :], masks.rearrange("m p t -> p m t"))
                    d(sb2row[:1, :], sb2.rearrange("(a d) -> a d", a=1))
                    d(eb2s[0:NE, :], eb2[:, :])
                    d(bvb[:, :], bv.partition_broadcast(P))
                    d(rWb[:, :, :], rW.rearrange("(c p) e -> p c e", p=P))
                    g.iota(iota15[:, :], [[1, NE]], channel_multiplier=0,
                           allow_small_or_imprecise_dtypes=True)
                    g.iota(ioA[:, :], [[1, P]], channel_multiplier=0,
                           allow_small_or_imprecise_dtypes=True)
                    g.iota(ioB[:, :], [[0, P]], channel_multiplier=1,
                           allow_small_or_imprecise_dtypes=True)
                    g.memset(ones_col[:, :], 1.0)
                    g.memset(onesr[:1, :], 1.0)
                    g.memset(cst[:, 0:1], EPS)
                    g.memset(cst[:, 1:2], 0.5)
                    g.memset(cst[:, 2:3], 0.125)
                    g.wait_ge(s_in.h, s_in.n)

                @blk.sync
                def _(sy):
                    def d2_(out, in_):
                        sy.dma_start(out, in_).then_inc(s_hw.h, 16)
                        s_hw.n += 16
                    d2_(R[:, 0:4, :], xT.rearrange("(c p) t -> p c t", p=P)[:, 0:4, :])
                    d2_(bqc[:, :], bq[:, :])
                    d2_(bkc[:, :], bk[:, :])
                    d2_(boc[:, :], bo[:, :])
                    d2_(sb1c[:, :], sb1[:, :])
                    d2_(eb1c[:, :, :], eb1T.rearrange("(c p) e -> p c e", p=P))
                    d2_(rbb[:, :], rbias.partition_broadcast(P))
                    sy.wait_ge(s_hw.h, s_hw.n)
            with nc.Block() as blk:
                @blk.vector
                def _(v):
                    v.tensor_tensor(out=identf[:, :], in0=ioA[:, :], in1=ioB[:, :], op=ALU.is_equal)
                    v.drain()
                    v.tensor_copy(identb[:, :], identf[:, :])
                    v.tensor_scalar_mul(bqc[:, :], bqc[:, :], 0.125)
                    v.tensor_scalar_mul(sb1c[:, :], sb1c[:, :], 0.5)
                @blk.scalar
                def _(s):
                    s.activation(sb2h[:1, :], sb2row[:1, :], AF.Copy, scale=cst[0:1, 1:2]).then_inc(s_act.h, 1)
                    s_act.n += 1

            with nc.Block() as blk:
                @blk.gpsimd
                def _(g):
                    g.wait_ge(s_act.h, s_act.n)
                    g.dma_start(eb2s[NE:16, :], sb2h[:1, :]).then_inc(s_in.h, 16)
                    s_in.n += 16
                    g.wait_ge(s_in.h, s_in.n)

        def phase_ln(cl0, cln):
            uid[0] += 1
            u = uid[0]
            sl = slice(cl0, cl0 + cln)
            hw_w = cln // 2
            with ExitStack() as ps:
                ssum = ps.enter_context(nc.sbuf_tensor(f"ln_su{u}", [1, TX], F32))
                ssq = ps.enter_context(nc.sbuf_tensor(f"ln_sq{u}", [1, TX], F32))
                srstd = ps.enter_context(nc.sbuf_tensor(f"ln_sr{u}", [1, TX], F32))
                sc0r = ps.enter_context(nc.sbuf_tensor(f"ln_sc{u}", [1, TX], F32))
                rbB = ps.enter_context(nc.sbuf_tensor(f"ln_rb{u}", [P, TX], F32))
                c0B = ps.enter_context(nc.sbuf_tensor(f"ln_c0{u}", [P, TX], F32))
                gsq = ps.enter_context(nc.sbuf_tensor(f"ln_gsq{u}", [P, DC, TX // 2], BF16))
                psA = ps.enter_context(nc.psum_tensor(f"ln_psA{u}", [1, 512], F32))
                psB = ps.enter_context(nc.psum_tensor(f"ln_psB{u}", [1, 512], F32))
                act0, pe0, dve0 = s_act.n, s_pe.n, s_dve.n
                hw1 = s_hw.n
                tl = tiles(hw_w, 512)
                nt = len(tl)
                dve1 = dve0 + 2 + 4 * nt
                act1 = act0 + 2
                NGC = 2
                with nc.Block() as blk:
                    @blk.scalar
                    def _(s):
                        for hw in range(2):
                            if hw == 1:
                                s.wait_ge(s_pe.h, pe0 + nt)
                            c0 = cl0 + hw * hw_w
                            for c in range(DC):
                                mm = s.activation(gsq[:, c, 0:hw_w], R[:, c, c0:c0 + hw_w], AF.Square)
                                if c == DC - 1:
                                    mm.then_inc(s_act.h, 1)
                        s.wait_ge(s_dve.h, dve1 + 1)
                        s.activation(srstd[:, 0:cln], ssq[:, 0:cln], AF.Sqrt,
                                     bias=cst[0:1, 0:1]).then_inc(s_act.h, 1)
                    s_act.n += 3

                    @blk.tensor
                    def _(t):
                        gt = 0
                        for hw in range(2):
                            t.wait_ge(s_act.h, act0 + hw + 1)
                            t.wait_ge(s_dve.h, dve0 + hw + 1)
                            c0 = cl0 + hw * hw_w
                            for ti, (to, tn) in enumerate(tl):
                                if gt >= 1:
                                    t.wait_ge(s_dve.h, dve0 + 2 + 2 * gt)
                                for c in range(DC):
                                    st, sp = (c == 0), (c == DC - 1)
                                    t.matmul(psA[0:1, 0:tn], ones_col[:, :],
                                             zT[:, c, c0 + to: c0 + to + tn], start=st, stop=sp)
                                    mm = t.matmul(psB[0:1, 0:tn], ones_col[:, :],
                                                  gsq[:, c, to:to + tn], start=st, stop=sp)
                                    if sp:
                                        mm.then_inc(s_pe.h, 1)
                                gt += 1
                    s_pe.n += 2 * nt

                    @blk.vector
                    def _(v):
                        for hw in range(2):
                            c0 = cl0 + hw * hw_w
                            for c in range(DC):
                                mm = v.tensor_copy(zT[:, c, c0:c0 + hw_w], R[:, c, c0:c0 + hw_w])
                                if c == DC - 1:
                                    v.drain()
                                    v.nop().then_inc(s_dve.h, 1)
                        gt = 0
                        for hw in range(2):
                            c0r = hw * hw_w
                            for ti, (to, tn) in enumerate(tl):
                                v.wait_ge(s_pe.h, pe0 + gt + 1)
                                v.tensor_copy(ssum[:, c0r + to:c0r + to + tn], psA[0:1, 0:tn])
                                v.drain()
                                v.nop().then_inc(s_dve.h, 1)
                                v.tensor_copy(ssq[:, c0r + to:c0r + to + tn], psB[0:1, 0:tn])
                                v.drain()
                                v.nop().then_inc(s_dve.h, 1)
                                gt += 1
                        cn = slice(0, cln)
                        v.tensor_scalar_mul(ssum[:, cn], ssum[:, cn], -1.0 / D)
                        v.drain()
                        v.tensor_scalar_mul(ssq[:, cn], ssq[:, cn], 1.0 / D)
                        v.drain()
                        v.tensor_tensor(out=sc0r[:, cn], in0=ssum[:, cn], in1=ssum[:, cn], op=ALU.mult)
                        v.drain()
                        v.tensor_tensor(out=ssq[:, cn], in0=ssq[:, cn], in1=sc0r[:, cn], op=ALU.subtract)
                        v.drain()
                        v.nop().then_inc(s_dve.h, 1)
                        v.wait_ge(s_act.h, act1 + 1)
                        v.reciprocal(srstd[:, cn], srstd[:, cn])
                        v.drain()
                        v.tensor_tensor(out=sc0r[:, cn], in0=ssum[:, cn], in1=srstd[:, cn], op=ALU.mult)
                        v.drain()
                        v.nop().then_inc(s_dve.h, 1)
                        v.wait_ge(s_hw.h, hw1 + 64)
                        for c in range(NGC, DC):
                            v.tensor_tensor(out=zT[:, c, sl], in0=R[:, c, sl], in1=rbB[:, 0:cln], op=ALU.mult)
                            v.drain()
                            v.tensor_tensor(out=zT[:, c, sl], in0=zT[:, c, sl], in1=c0B[:, 0:cln], op=ALU.add)
                            v.drain()
                    s_dve.n += 4 * nt + 4

                    @blk.sync
                    def _(sy):
                        sy.wait_ge(s_dve.h, dve1 + 2)
                        sy.dma_start(scr_rows[0:1, 0:cln], srstd[:, 0:cln]).then_inc(s_hw.h, 16)
                        sy.dma_start(scr_rows[1:2, 0:cln], sc0r[:, 0:cln]).then_inc(s_hw.h, 16)
                        s_hw.n += 32
                        sy.wait_ge(s_hw.h, s_hw.n)
                        sy.dma_start(rbB[:, 0:cln], scr_rows[0, 0:cln].partition_broadcast(P)).then_inc(s_hw.h, 16)
                        sy.dma_start(c0B[:, 0:cln], scr_rows[1, 0:cln].partition_broadcast(P)).then_inc(s_hw.h, 16)
                        s_hw.n += 32
                        sy.wait_ge(s_hw.h, s_hw.n)

                    @blk.gpsimd
                    def _(g):
                        g.wait_ge(s_hw.h, hw1 + 64)
                        for c in range(NGC):
                            g.tensor_tensor(out=zT[:, c, sl], in0=R[:, c, sl], in1=rbB[:, 0:cln], op=ALU.mult)
                            g.drain()
                            g.tensor_tensor(out=zT[:, c, sl], in0=zT[:, c, sl], in1=c0B[:, 0:cln], op=ALU.add)
                            g.drain()

        def make_router(cl0, cln, cwf, cwx, ps):
            uid[0] += 1
            u = uid[0]
            nch = cln // P
            nf = nch * NE
            prt = ps.enter_context(nc.psum_tensor(f"rt_p{u}", [P, NE], F32))
            ptw = ps.enter_context(nc.psum_tensor(f"rt_t{u}", [NE, P], F32))
            logt = ps.enter_context(nc.sbuf_tensor(f"rt_lg{u}", [P, NCH, NE], F32))
            probs = ps.enter_context(nc.sbuf_tensor(f"rt_pr{u}", [P, NCH, NE], F32))
            tt1 = ps.enter_context(nc.sbuf_tensor(f"rt_t1{u}", [P, NCH, NE], F32))
            nmx = ps.enter_context(nc.sbuf_tensor(f"rt_nm{u}", [P, 1], F32))
            f17 = ps.enter_context(nc.sbuf_tensor(f"rt_f17{u}", [P, NCH], F32))
            r17 = ps.enter_context(nc.sbuf_tensor(f"rt_r17{u}", [P, NCH], F32))
            iota15m = ps.enter_context(nc.sbuf_tensor(f"rt_iom{u}", [P, NE], F32))
            pe0r, dve0r, act0r = s_rpe.n, s_rdv.n, s_rac.n
            dve1r = dve0r + nch
            dve2r = dve1r + 2
            pe1r = pe0r + nch
            st = {"cwx_ev": dve2r + nch + 1}

            L = logt.ap().rearrange("p a b -> p (a b)")[:, 0:nf].rearrange("p (a b) -> p a b", b=NE)
            PR = probs.ap().rearrange("p a b -> p (a b)")[:, 0:nf].rearrange("p (a b) -> p a b", b=NE)
            T1 = tt1.ap().rearrange("p a b -> p (a b)")[:, 0:nf].rearrange("p (a b) -> p a b", b=NE)
            CW = cwf.ap().rearrange("p a b -> p (a b)")[:, 0:nf].rearrange("p (a b) -> p a b", b=NE)
            Lf = logt.ap().rearrange("p a b -> p (a b)")[:, 0:nf]
            PRf = probs.ap().rearrange("p a b -> p (a b)")[:, 0:nf]
            T1f = tt1.ap().rearrange("p a b -> p (a b)")[:, 0:nf]
            CWf = cwf.ap().rearrange("p a b -> p (a b)")[:, 0:nf]
            io_b = iota15.ap().unsqueeze(1).broadcast_to((P, nch, NE))
            iom_b = iota15m.ap().unsqueeze(1).broadcast_to((P, nch, NE))

            def pe_logits(t):
                for k in range(nch):
                    if k >= 1:
                        t.wait_ge(s_rdv.h, dve0r + k)
                    for c in range(DC):
                        mm = t.matmul(prt[:, :], zT[:, c, cl0 + k * P: cl0 + (k + 1) * P],
                                      rWb[:, c, :], start=(c == 0), stop=(c == DC - 1))
                        if c == DC - 1:
                            mm.then_inc(s_rpe.h, 1)
                s_rpe.n += nch

            def pe_trans(t):
                t.wait_ge(s_rdv.h, dve1r + 2)
                for k in range(nch):
                    if k >= 1:
                        t.wait_ge(s_rdv.h, dve2r + k)
                    t.transpose(ptw[0:NE, :], cwf[:, k, :], identf[:, :]).then_inc(s_rpe.h, 1)
                s_rpe.n += nch

            def dve(v):
                v.tensor_scalar_mul(iota15m[:, :], iota15[:, :], 1.0)
                v.drain()
                v.tensor_scalar_add(out=iota15m[:, :], in0=iota15m[:, :], scalar1=-100.0)
                v.drain()
                for k in range(nch):
                    v.wait_ge(s_rpe.h, pe0r + k + 1)
                    v.tensor_tensor(out=logt[:, k, :], in0=prt[:, :], in1=rbb[:, :], op=ALU.add)
                    v.drain()
                    v.nop().then_inc(s_rdv.h, 1)
                v.tensor_reduce(out=nmx[:, 0:1], in_=Lf, axis=AX.X, op=ALU.max, negate=True)
                v.drain()
                v.nop().then_inc(s_rdv.h, 1)
                v.wait_ge(s_rac.h, act0r + 1)
                v.tensor_reduce(out=f17[:, 0:nch], in_=PR, axis=AX.X, op=ALU.add)
                v.drain()
                v.reciprocal(r17[:, 0:nch], f17[:, 0:nch])
                v.drain()
                v.tensor_tensor(out=PR, in0=PR,
                                in1=r17[:, 0:nch].unsqueeze(2).broadcast_to((P, nch, NE)), op=ALU.mult)
                v.drain()
                v.memset(CWf, 0.0)
                v.drain()
                for it in range(3):
                    v.tensor_reduce(out=f17[:, 0:nch], in_=PR, axis=AX.X, op=ALU.max, negate=True)
                    v.drain()
                    fb = f17[:, 0:nch].unsqueeze(2).broadcast_to((P, nch, NE))
                    v.tensor_tensor(out=T1, in0=PR, in1=fb, op=ALU.add)
                    v.drain()
                    v.tensor_single_scalar(out=T1f, in_=T1f, scalar=0.0, op=ALU.is_equal)
                    v.drain()
                    v.tensor_tensor(out=T1, in0=T1, in1=iom_b, op=ALU.mult)
                    v.drain()
                    v.tensor_scalar_add(out=T1f, in0=T1f, scalar1=100.0)
                    v.drain()
                    v.tensor_reduce(out=f17[:, 0:nch], in_=T1, axis=AX.X, op=ALU.min)
                    v.drain()
                    fb = f17[:, 0:nch].unsqueeze(2).broadcast_to((P, nch, NE))
                    v.tensor_tensor(out=T1, in0=io_b, in1=fb, op=ALU.subtract)
                    v.drain()
                    v.tensor_single_scalar(out=T1f, in_=T1f, scalar=0.0, op=ALU.is_equal)
                    v.drain()
                    v.tensor_reduce(out=r17[:, 0:nch], in_=PR, axis=AX.X, op=ALU.max, negate=True)
                    v.drain()
                    rb2 = r17[:, 0:nch].unsqueeze(2).broadcast_to((P, nch, NE))
                    v.tensor_tensor(out=T1, in0=T1, in1=rb2, op=ALU.mult)
                    v.drain()
                    v.tensor_tensor(out=CW, in0=CW, in1=T1, op=ALU.subtract)
                    v.drain()
                    v.tensor_tensor(out=PR, in0=PR, in1=T1, op=ALU.add)
                    v.drain()
                v.tensor_scalar_mul(CWf, CWf, 0.5)
                v.drain()
                v.nop().then_inc(s_rdv.h, 1)
                for k in range(nch):
                    v.wait_ge(s_rpe.h, pe1r + k + 1)
                    v.tensor_copy(cwx[0:NE, cl0 + k * P: cl0 + (k + 1) * P], ptw[0:NE, :])
                    v.drain()
                    v.nop().then_inc(s_rdv.h, 1)
                v.nop().then_inc(s_rdv.h, 1)
                s_rdv.n += nch + 2 + nch + 1

            def act(sc):
                sc.wait_ge(s_rdv.h, dve1r + 1)
                sc.activation(PRf, Lf, AF.Exp, bias=nmx[:, 0:1]).then_inc(s_rac.h, 1)
                s_rac.n += 1

            def sync(sy):
                sy.wait_ge(s_rdv.h, st["cwx_ev"])
                sy.dma_start(scr_cwt[:, cl0:cl0 + cln], cwx[0:NE, cl0:cl0 + cln]).then_inc(s_hw.h, 16)
                s_hw.n += 16
                sy.wait_ge(s_hw.h, s_hw.n)

            return {"pe_logits": pe_logits, "pe_trans": pe_trans, "dve": dve,
                    "act": act, "sync": sync, "cwx_ev": st["cwx_ev"]}

        def phase_ffn(cl0, cln, halves, cwf, cwx, stream_out=False):
            uid[0] += 1
            u = uid[0]
            nmat = 2 * (NE + 1)
            nh = len(halves)
            with ExitStack() as ps:
                pA = ps.enter_context(nc.psum_tensor(f"ff_pA{u}", [P, 1152], F32))
                pB = ps.enter_context(nc.psum_tensor(f"ff_pB{u}", [P, 1152], F32))
                w0 = ps.enter_context(nc.sbuf_tensor(f"ff_w0{u}", [P, DC, D], BF16))
                w1 = ps.enter_context(nc.sbuf_tensor(f"ff_w1{u}", [P, DC, D], BF16))
                w2 = ps.enter_context(nc.sbuf_tensor(f"ff_w2{u}", [P, DC, D], BF16))
                wb = [w0, w1, w2]
                gbuf = ps.enter_context(nc.sbuf_tensor(f"ff_g{u}", [P, DC, 1152], BF16))
                rb0 = ps.enter_context(nc.sbuf_tensor(f"ff_r0{u}", [P, 1152], BF16))
                rb1 = ps.enter_context(nc.sbuf_tensor(f"ff_r1{u}", [P, 1152], BF16))
                rbufs = [rb0, rb1]
                cb0 = ps.enter_context(nc.sbuf_tensor(f"ff_c0{u}", [P, 1152], BF16))
                cb1 = ps.enter_context(nc.sbuf_tensor(f"ff_c1{u}", [P, 1152], BF16))
                cbufs = [cb0, cb1]

                rt = make_router(cl0, cln, cwf, cwx, ps)
                pe0, act0, dve0 = s_pe.n, s_act.n, s_dve.n
                cf0 = s_cfree.n
                wtgt = []
                for m in range(nmat):
                    c = s_wm[m % 3]
                    c.n += 16
                    wtgt.append(c.n)
                ctgt = {}
                for e in range(NE):
                    for hf in range(nh):
                        c = s_cb[e % 2]
                        c.n += 16
                        ctgt[(e, hf)] = c.n

                def pe_h(ei, hf, dh):
                    return pe0 + ei * 32 + hf * 16 + dh + 1

                def pe_ye(ei, hf, do):
                    return pe0 + ei * 32 + hf * 16 + 8 + do + 1

                def act_relu(ei, hf, dh):
                    return act0 + ei * 16 + hf * 8 + dh + 1

                def dve_g(ei, hf, dh):
                    return dve0 + 16 + (ei - 1) * 32 + hf * 16 + dh + 1

                def dve_R(ei, hf, do):
                    if ei == 0:
                        return dve0 + hf * 8 + do + 1
                    return dve0 + 16 + (ei - 1) * 32 + hf * 16 + 8 + do + 1

                GSEQ = []
                for ei in range(NE + 1):
                    for hf in range(nh):
                        for dh in range(8):
                            GSEQ.append(("h", ei, hf, dh))
                        for do in range(8):
                            GSEQ.append(("ye", ei, hf, do))

                def free_count(gidx):
                    kind, ei, hf, x = GSEQ[gidx]
                    if kind == "h":
                        return (s_act.h, act_relu(ei, hf, x))
                    return (s_dve.h, dve_R(ei, hf, x))

                with nc.Block() as blk:
                    @blk.gpsimd
                    def _(g):
                        for m in range(nmat):
                            ei, is2 = divmod(m, 2)
                            if ei == 0:
                                src = sW2 if is2 else sW1
                            else:
                                src = (eW2 if is2 else eW1)[ei - 1]
                            if m >= 3:
                                mp = m - 3
                                eip, is2p = divmod(mp, 2)
                                if is2p == 0:
                                    g.wait_ge(s_pe.h, pe_h(eip, nh - 1, 7))
                                else:
                                    g.wait_ge(s_pe.h, pe_ye(eip, nh - 1, 7))
                            g.dma_start(wb[m % 3][:, :, :],
                                        src.rearrange("(c p) n -> p c n", p=P)).then_inc(s_wm[m % 3].h, 16)
                        for i in range(3):
                            g.wait_ge(s_wm[i].h, s_wm[i].n)

                    @blk.sync
                    def _(sy):
                        rt["sync"](sy)
                        for e in range(NE):
                            for hf, (ho, hn) in enumerate(halves):
                                if hf >= 1:
                                    sy.wait_ge(s_cfree.h, cf0 + 2 * e + 1)
                                elif e >= 2:
                                    sy.wait_ge(s_cfree.h, cf0 + 2 * (e - 2) + 2)
                                sy.dma_start(cbufs[e % 2][:, 0:hn],
                                             scr_cwt[e, cl0 + ho: cl0 + ho + hn].partition_broadcast(P)
                                             ).then_inc(s_cb[e % 2].h, 16)
                        for i in range(2):
                            sy.wait_ge(s_cb[i].h, s_cb[i].n)
                        if stream_out:
                            for hf, (ho, hn) in enumerate(halves):
                                sy.wait_ge(s_dve.h, dve_R(NE, hf, 7))
                                sy.dma_start(
                                    outT.rearrange("(c p) t -> p c t", p=P)[:, :, ho: ho + hn],
                                    R[:, :, cl0 + ho: cl0 + ho + hn]).then_inc(s_hw.h, 16)
                                s_hw.n += 16
                            sy.wait_ge(s_hw.h, s_hw.n)

                    @blk.tensor
                    def _(t):
                        rt["pe_logits"](t)
                        ebwait = [True]
                        gidx = 0
                        for ei in range(NE + 1):
                            m1, m2 = 2 * ei, 2 * ei + 1
                            for hf, (ho, hn) in enumerate(halves):
                                tl = tiles(hn, 512)
                                if hf == 0:
                                    t.wait_ge(s_wm[m1 % 3].h, wtgt[m1])
                                for dh in range(8):
                                    bank = pA if gidx % 2 == 0 else pB
                                    if gidx >= 2:
                                        sh, cn = free_count(gidx - 2)
                                        t.wait_ge(sh, cn)
                                    for c in range(DC):
                                        for ti, (to, tn) in enumerate(tl):
                                            mm = t.matmul(bank[:, to:to + tn],
                                                          wb[m1 % 3][:, c, dh * P:(dh + 1) * P],
                                                          zT[:, c, cl0 + ho + to: cl0 + ho + to + tn],
                                                          start=(c == 0), stop=(c == DC - 1))
                                            if c == DC - 1 and ti == len(tl) - 1:
                                                mm.then_inc(s_pe.h, 1)
                                    gidx += 1
                                if ei == 0 and hf == 0:
                                    rt["pe_trans"](t)
                                if hf == 0:
                                    t.wait_ge(s_wm[m2 % 3].h, wtgt[m2])
                                for do in range(8):
                                    bank = pA if gidx % 2 == 0 else pB
                                    if gidx >= 2:
                                        sh, cn = free_count(gidx - 2)
                                        t.wait_ge(sh, cn)
                                    for dh in range(8):
                                        if do == 0:
                                            if ei == 0:
                                                t.wait_ge(s_act.h, act_relu(0, hf, dh))
                                            else:
                                                t.wait_ge(s_dve.h, dve_g(ei, hf, dh))
                                        for ti, (to, tn) in enumerate(tl):
                                            last = (dh == DC - 1) and (ei != 0)
                                            mm = t.matmul(bank[:, to:to + tn],
                                                          wb[m2 % 3][:, dh, do * P:(do + 1) * P],
                                                          gbuf[:, dh, to:to + tn],
                                                          start=(dh == 0), stop=last)
                                            if last and ti == len(tl) - 1:
                                                mm.then_inc(s_pe.h, 1)
                                    if ei == 0:
                                        if ebwait[0]:
                                            t.wait_ge(s_rdv.h, rt["cwx_ev"])
                                            ebwait[0] = False
                                        for ti, (to, tn) in enumerate(tl):
                                            mm = t.matmul(bank[:, to:to + tn],
                                                          eb2s[:, do * P:(do + 1) * P],
                                                          cwx[:, cl0 + ho + to: cl0 + ho + to + tn],
                                                          start=False, stop=True)
                                            if ti == len(tl) - 1:
                                                mm.then_inc(s_pe.h, 1)
                                    gidx += 1
                    s_pe.n += (NE + 1) * nh * 16

                    @blk.scalar
                    def _(s):
                        rt["act"](s)
                        for ei in range(NE + 1):
                            for hf, (ho, hn) in enumerate(halves):
                                for dh in range(8):
                                    gidx = ei * 32 + hf * 16 + dh
                                    bank = pA if gidx % 2 == 0 else pB
                                    s.wait_ge(s_pe.h, pe_h(ei, hf, dh))
                                    if ei == 0:
                                        s.activation(gbuf[:, dh, 0:hn], bank[:, 0:hn], AF.Relu,
                                                     bias=sb1c[:, dh:dh + 1],
                                                     scale=cst[:, 1:2]).then_inc(s_act.h, 1)
                                    else:
                                        ridx = (ei - 1) * 16 + hf * 8 + dh
                                        if ridx >= 2:
                                            e2, r2 = divmod(ridx - 2, 16)
                                            hf2, dh2 = divmod(r2, 8)
                                            s.wait_ge(s_dve.h, dve_g(e2 + 1, hf2, dh2))
                                        s.activation(rbufs[ridx % 2][:, 0:hn], bank[:, 0:hn], AF.Relu,
                                                     bias=eb1c[:, dh, ei - 1:ei]).then_inc(s_act.h, 1)
                    s_act.n += (NE + 1) * nh * 8

                    @blk.vector
                    def _(v):
                        rt["dve"](v)
                        for ei in range(NE + 1):
                            for hf, (ho, hn) in enumerate(halves):
                                if ei >= 1:
                                    e = ei - 1
                                    v.wait_ge(s_cb[e % 2].h, ctgt[(e, hf)])
                                    for dh in range(8):
                                        ridx = e * 16 + hf * 8 + dh
                                        v.wait_ge(s_act.h, act_relu(ei, hf, dh))
                                        v.tensor_tensor(out=gbuf[:, dh, 0:hn], in0=rbufs[ridx % 2][:, 0:hn],
                                                        in1=cbufs[e % 2][:, 0:hn], op=ALU.mult)
                                        v.drain()
                                        v.nop().then_inc(s_dve.h, 1)
                                    v.nop().then_inc(s_cfree.h, 1)
                                for do in range(8):
                                    v.wait_ge(s_pe.h, pe_ye(ei, hf, do))
                                    gidx = ei * 32 + hf * 16 + 8 + do
                                    bank = pA if gidx % 2 == 0 else pB
                                    v.tensor_tensor(out=R[:, do, cl0 + ho: cl0 + ho + hn],
                                                    in0=bank[:, 0:hn],
                                                    in1=R[:, do, cl0 + ho: cl0 + ho + hn], op=ALU.add)
                                    v.drain()
                                    v.nop().then_inc(s_dve.h, 1)
                    s_dve.n += NE * nh * 8 + (NE + 1) * nh * 8
                    s_cfree.n += NE * nh

        def phase_attn():
            uid[0] += 1
            u = uid[0]
            with ExitStack() as ps:
                pQ = [[ps.enter_context(nc.psum_tensor(f"at_S{i}{k}{u}", [P, 512], F32))
                       for k in range(2)] for i in range(2)]
                pT2 = [ps.enter_context(nc.psum_tensor(f"at_T{i}{u}", [P, 8, P], BF16)) for i in range(2)]
                pC2 = [ps.enter_context(nc.psum_tensor(f"at_C{i}{u}", [P, 2, P], F32)) for i in range(2)]
                pSa = pQ[0][0].ap()
                pSb = pQ[0][1].ap()
                wA = ps.enter_context(nc.sbuf_tensor(f"at_wA{u}", [P, DC, D], BF16))
                wB = ps.enter_context(nc.sbuf_tensor(f"at_wB{u}", [P, DC, D], BF16))
                kTb = ps.enter_context(nc.sbuf_tensor(f"at_kT{u}", [P, DC, 640], BF16))
                vbuf = ps.enter_context(nc.sbuf_tensor(f"at_v{u}", [P, 5, D], BF16))
                qTb = ps.enter_context(nc.sbuf_tensor(f"at_qT{u}", [P, DC, 512], BF16))
                ctxT = ps.enter_context(nc.sbuf_tensor(f"at_cT{u}", [P, DC, 512], BF16))
                esc = [ps.enter_context(nc.sbuf_tensor(f"at_es{i}{u}", [P, 4, 256], BF16)) for i in range(3)]
                nmx2 = [ps.enter_context(nc.sbuf_tensor(f"at_n2{i}{u}", [P, 2], F32)) for i in range(2)]
                atb = [ps.enter_context(nc.sbuf_tensor(f"at_ab{i}{u}", [P, 4, 256], BF16)) for i in range(2)]
                atT = [ps.enter_context(nc.sbuf_tensor(f"at_aT{i}{u}", [P, 8, P], BF16)) for i in range(2)]
                nmx = [ps.enter_context(nc.sbuf_tensor(f"at_nm{i}{u}", [P, 1], F32)) for i in range(2)]
                sro = [ps.enter_context(nc.sbuf_tensor(f"at_sr{i}{u}", [P, 4], F32)) for i in range(2)]
                rec = [ps.enter_context(nc.sbuf_tensor(f"at_rc{i}{u}", [P, 4], F32)) for i in range(2)]

                prev_e3 = [None]
                prev_e4 = [None]
                prev_pe_s_last = [None]
                prev_pe_c_last = [None]
                prev_dve_out = [None]
                qblk = nc.Block()
                blk = qblk.__enter__()
                for qh in range(aq):
                    kc0 = 512 * qh
                    qc0 = HALO + 512 * qh
                    awt = []
                    for mi in range(4):
                        c = s_wm[mi % 2]
                        c.n += 16
                        awt.append(c.n)
                    pe0, act0, dve0 = s_pe.n, s_act.n, s_dve.n
                    ktl = tiles(640, 512)

                    def prj_free(gi, act0=act0, dve0=dve0):
                        if gi < 16:
                            return (s_act.h, act0 + gi + 1)
                        if gi < 26:
                            return (s_dve.h, dve0 + (gi - 16) + 1)
                        return (s_act.h, act0 + 16 + (gi - 26) + 1)

                    if True:
                        @blk.gpsimd
                        def _(g):
                            waits = [prev_e3[0], prev_e4[0], pe0 + 16, pe0 + 26]
                            for mi, src in enumerate((WkT, WvT, WqT, WoT)):
                                if waits[mi] is not None:
                                    g.wait_ge(s_pe.h, waits[mi])
                                g.dma_start((wA if mi % 2 == 0 else wB)[:, :, :],
                                            src.rearrange("(c p) n -> p c n", p=P)).then_inc(s_wm[mi % 2].h, 16)
                            g.wait_ge(s_wm[0].h, awt[2])
                            g.wait_ge(s_wm[1].h, awt[1])

                        @blk.tensor
                        def _(t):
                            gi = 0
                            t.wait_ge(s_wm[0].h, awt[0])
                            if prev_dve_out[0] is not None:
                                t.wait_ge(s_dve.h, prev_dve_out[0])
                            for do in range(8):
                                for (to, tn) in ktl:
                                    bank = pSa if gi % 2 == 0 else pSb
                                    if gi >= 2:
                                        sh, cn = prj_free(gi - 2)
                                        t.wait_ge(sh, cn)
                                    for c in range(DC):
                                        mm = t.matmul(bank[:, 0:tn], wA[:, c, do * P:(do + 1) * P],
                                                      zT[:, c, kc0 + to: kc0 + to + tn],
                                                      start=(c == 0), stop=(c == DC - 1))
                                        if c == DC - 1:
                                            mm.then_inc(s_pe.h, 1)
                                    gi += 1
                            t.wait_ge(s_wm[1].h, awt[1])
                            for kc in range(5):
                                for nt in range(2):
                                    bank = pSa if gi % 2 == 0 else pSb
                                    if gi >= 2:
                                        sh, cn = prj_free(gi - 2)
                                        t.wait_ge(sh, cn)
                                    for c in range(DC):
                                        mm = t.matmul(bank[:, 0:512],
                                                      zT[:, c, kc0 + kc * P: kc0 + (kc + 1) * P],
                                                      wB[:, c, nt * 512:(nt + 1) * 512],
                                                      start=(c == 0), stop=(c == DC - 1))
                                        if c == DC - 1:
                                            mm.then_inc(s_pe.h, 1)
                                    gi += 1
                            t.wait_ge(s_wm[0].h, awt[2])
                            for do in range(8):
                                bank = pSa if gi % 2 == 0 else pSb
                                if gi >= 2:
                                    sh, cn = prj_free(gi - 2)
                                    t.wait_ge(sh, cn)
                                for c in range(DC):
                                    mm = t.matmul(bank[:, 0:512], wA[:, c, do * P:(do + 1) * P],
                                                  zT[:, c, qc0: qc0 + 512],
                                                  start=(c == 0), stop=(c == DC - 1))
                                    if c == DC - 1:
                                        mm.then_inc(s_pe.h, 1)
                                gi += 1
                        s_pe.n += 34
                        prev_e3[0] = pe0 + 34

                        @blk.scalar
                        def _(s):
                            if prev_pe_s_last[0] is not None:
                                s.wait_ge(s_pe.h, prev_pe_s_last[0])
                            gi = 0
                            for do in range(8):
                                for (to, tn) in ktl:
                                    bank = pSa if gi % 2 == 0 else pSb
                                    s.wait_ge(s_pe.h, pe0 + gi + 1)
                                    s.activation(kTb[:, do, to:to + tn], bank[:, 0:tn], AF.Identity,
                                                 bias=bkc[:, do:do + 1]).then_inc(s_act.h, 1)
                                    gi += 1
                            gi = 26
                            for do in range(8):
                                bank = pSa if gi % 2 == 0 else pSb
                                s.wait_ge(s_pe.h, pe0 + gi + 1)
                                s.activation(qTb[:, do, 0:512], bank[:, 0:512], AF.Identity,
                                             bias=bqc[:, do:do + 1], scale=cst[:, 2:3]).then_inc(s_act.h, 1)
                                gi += 1
                        s_act.n += 24

                        @blk.vector
                        def _(v):
                            if prev_pe_c_last[0] is not None:
                                v.wait_ge(s_pe.h, prev_pe_c_last[0])
                            gi = 16
                            for kc in range(5):
                                for nt in range(2):
                                    bank = pSa if gi % 2 == 0 else pSb
                                    v.wait_ge(s_pe.h, pe0 + gi + 1)
                                    v.tensor_tensor(out=vbuf[:, kc, nt * 512:(nt + 1) * 512],
                                                    in0=bank[:, 0:512],
                                                    in1=bvb[:, nt * 512:(nt + 1) * 512], op=ALU.add)
                                    v.drain()
                                    v.nop().then_inc(s_dve.h, 1)
                                    gi += 1
                        s_dve.n += 10

                    pe1, act1, dve1, gp1 = s_pe.n, s_act.n, s_dve.n, s_gp.n
                    NIT = anit

                    def pe_s(j):
                        return pe1 + 3 * j + 1

                    def pe_t(j):
                        return pe1 + 3 * (j + 3) + 2

                    def pe_c(j):
                        return pe1 + 3 * (j + 4) + 3

                    def d1(j):
                        return dve1 + 2 * j + 1

                    def d2(j):
                        return dve1 + 2 * (j + 1) + 2

                    def g_n(j):
                        return gp1 + j + 3

                    def a2(j):
                        return act1 + 3 * (j + 3) + 1

                    def a3(j):
                        return act1 + 3 * (j + 5) + 2

                    def a1(j):
                        return act1 + 3 * j + 3

                    NC2 = NIT + 5
                    act1v, dve1v = s_act.n, s_dve.n

                    if True:
                        @blk.tensor
                        def _(t):
                            t.wait_ge(s_act.h, act1v)
                            t.wait_ge(s_dve.h, dve1v)
                            for i in range(NC2):
                                if 0 <= i < NIT:
                                    j = i
                                    pl, hq = divmod(j, 4)
                                    p_glob = 4 * qh + pl
                                    midx = 0 if p_glob == 0 else (2 if p_glob == 15 else 1)
                                    if j >= 2:
                                        t.wait_ge(s_act.h, a1(j - 2))

                                    def score_mm(e, stop):
                                        h = 4 * hq + e
                                        hp, dc = h % 2, h // 2
                                        return t.matmul(pQ[j % 2][e % 2][:, (e // 2) * 256:(e // 2 + 1) * 256],
                                                        qTb[hp * HD:(hp + 1) * HD, dc, pl * P:(pl + 1) * P],
                                                        kTb[hp * HD:(hp + 1) * HD, dc, pl * P: pl * P + 256],
                                                        start=False, stop=stop)
                                    t.matmul(pQ[j % 2][0][:, :], identb[:, :],
                                             maskb[:, midx, 0:512], start=True, stop=False)
                                    score_mm(0, False)
                                    score_mm(2, True)
                                    t.matmul(pQ[j % 2][1][:, :], identb[:, :],
                                             maskb[:, midx, 512:1024], start=True, stop=False)
                                    score_mm(1, False)
                                    score_mm(3, True).then_inc(s_pe.h, 1)
                                else:
                                    t.drain()
                                    t.nop().then_inc(s_pe.h, 1)
                                if 0 <= i - 3 < NIT:
                                    j = i - 3
                                    t.wait_ge(s_gp.h, g_n(j))
                                    if j >= 2:
                                        t.wait_ge(s_act.h, a2(j - 2))
                                    pT = pT2[j % 2]
                                    for sidx in range(4):
                                        t.transpose(pT[:, 2 * sidx, :], atb[j % 2][:, sidx, 0:128], identb[:, :])
                                        mm = t.transpose(pT[:, 2 * sidx + 1, :], atb[j % 2][:, sidx, 128:256], identb[:, :])
                                        if sidx == 3:
                                            mm.then_inc(s_pe.h, 1)
                                else:
                                    t.drain()
                                    t.nop().then_inc(s_pe.h, 1)
                                if 0 <= i - 4 < NIT:
                                    j = i - 4
                                    pl, hq = divmod(j, 4)
                                    t.wait_ge(s_act.h, a2(j))
                                    if j >= 2:
                                        t.wait_ge(s_act.h, a3(j - 2))
                                    pC = pC2[j % 2]
                                    for sidx, e in enumerate((0, 2, 1, 3)):
                                        h = 4 * hq + e
                                        hp = h % 2
                                        t.matmul(pC[hp * HD:(hp + 1) * HD, e // 2, :],
                                                 vbuf[:, pl, h * HD:(h + 1) * HD],
                                                 atT[j % 2][:, 2 * sidx, :], start=True, stop=False)
                                        mm = t.matmul(pC[hp * HD:(hp + 1) * HD, e // 2, :],
                                                      vbuf[:, pl + 1, h * HD:(h + 1) * HD],
                                                      atT[j % 2][:, 2 * sidx + 1, :], start=False, stop=True)
                                        if sidx == 3:
                                            mm.then_inc(s_pe.h, 1)
                                else:
                                    t.drain()
                                    t.nop().then_inc(s_pe.h, 1)
                        s_pe.n += 3 * NC2

                        @blk.vector
                        def _(v):
                            for i in range(NC2):
                                if 0 <= i < NIT:
                                    j = i
                                    v.wait_ge(s_pe.h, pe_s(j))
                                    if j >= 2:
                                        v.wait_ge(s_act.h, a1(j - 2))
                                    v.tensor_reduce(out=nmx2[j % 2][:, 0:1], in_=pQ[j % 2][0][:, :],
                                                    axis=AX.X, op=ALU.max, negate=True)
                                    mm = v.tensor_reduce(out=nmx2[j % 2][:, 1:2], in_=pQ[j % 2][1][:, :],
                                                         axis=AX.X, op=ALU.max, negate=True)
                                    mm.then_inc(s_dve.h, 1)
                                else:
                                    v.drain()
                                    v.nop().then_inc(s_dve.h, 1)
                                if 0 <= i - 1 < NIT:
                                    j = i - 1
                                    v.wait_ge(s_act.h, a1(j))
                                    if j >= 2:
                                        v.wait_ge(s_gp.h, g_n(j - 2))
                                    v.tensor_reduce(out=sro[j % 2][:, 0:2], in_=esc[j % 3][:, 0:2, :],
                                                    axis=AX.X, op=ALU.add)
                                    v.tensor_reduce(out=sro[j % 2][:, 2:4], in_=esc[j % 3][:, 2:4, :],
                                                    axis=AX.X, op=ALU.add)
                                    v.drain()
                                    v.reciprocal(rec[j % 2][:, :], sro[j % 2][:, :])
                                    v.drain()
                                    v.nop().then_inc(s_dve.h, 1)
                                else:
                                    v.drain()
                                    v.nop().then_inc(s_dve.h, 1)
                        s_dve.n += 2 * NC2

                        @blk.gpsimd
                        def _(g):
                            for i in range(NC2):
                                if 0 <= i - 2 < NIT:
                                    j = i - 2
                                    g.wait_ge(s_dve.h, d2(j))
                                    if j >= 2:
                                        g.wait_ge(s_pe.h, pe_t(j - 2))
                                    b_ap = rec[j % 2][:, :].unsqueeze(2).broadcast_to((P, 4, 256))
                                    g.tensor_tensor(out=atb[j % 2][:, :, :], in0=esc[j % 3][:, :, :],
                                                    in1=b_ap, op=ALU.mult)
                                    g.drain()
                                    g.nop().then_inc(s_gp.h, 1)
                                else:
                                    g.drain()
                                    g.nop().then_inc(s_gp.h, 1)
                        s_gp.n += NC2

                        @blk.scalar
                        def _(s):
                            for i in range(NC2):
                                if 0 <= i - 3 < NIT:
                                    j = i - 3
                                    s.wait_ge(s_pe.h, pe_t(j))
                                    if j >= 2:
                                        s.wait_ge(s_pe.h, pe_c(j - 2))
                                    s.copy(atT[j % 2][:, :, :], pT2[j % 2][:, :, :]).then_inc(s_act.h, 1)
                                else:
                                    s.drain()
                                    s.nop().then_inc(s_act.h, 1)
                                if 0 <= i - 5 < NIT:
                                    j = i - 5
                                    pl, hq = divmod(j, 4)
                                    s.wait_ge(s_pe.h, pe_c(j))
                                    s.copy(ctxT[:, 2 * hq:2 * hq + 2, pl * P:(pl + 1) * P],
                                           pC2[j % 2][:, :, :]).then_inc(s_act.h, 1)
                                else:
                                    s.drain()
                                    s.nop().then_inc(s_act.h, 1)
                                if 0 <= i < NIT:
                                    j = i
                                    s.wait_ge(s_dve.h, d1(j))
                                    if j >= 3:
                                        s.wait_ge(s_gp.h, g_n(j - 3))
                                        s.wait_ge(s_dve.h, d2(j - 3))
                                    s.activation(esc[j % 3][:, 0:2, :], pQ[j % 2][0][:, :],
                                                 AF.Exp, bias=nmx2[j % 2][:, 0:1])
                                    s.activation(esc[j % 3][:, 2:4, :], pQ[j % 2][1][:, :],
                                                 AF.Exp, bias=nmx2[j % 2][:, 1:2]).then_inc(s_act.h, 1)
                                else:
                                    s.drain()
                                    s.nop().then_inc(s_act.h, 1)
                        s_act.n += 3 * NC2

                    prev_pe_s_last[0] = pe_s(NIT - 1)
                    prev_pe_c_last[0] = pe_c(NIT - 1)
                    if not aout:
                        prev_e3[0] = None
                        prev_e4[0] = None
                        continue
                    pe2, dve2 = s_pe.n, s_dve.n
                    a3last = a3(NIT - 1)
                    if True:
                        @blk.tensor
                        def _(t):
                            t.wait_ge(s_wm[1].h, awt[3])
                            t.wait_ge(s_act.h, a3last)
                            for do in range(8):
                                bank = pSa if do % 2 == 0 else pSb
                                if do >= 2:
                                    t.wait_ge(s_dve.h, dve2 + do - 1)
                                for c in range(DC):
                                    mm = t.matmul(bank[:, 0:512], wB[:, c, do * P:(do + 1) * P],
                                                  ctxT[:, c, 0:512], start=(c == 0), stop=(c == DC - 1))
                                    if c == DC - 1:
                                        mm.then_inc(s_pe.h, 1)
                        s_pe.n += 8
                        prev_e4[0] = pe2 + 8

                        @blk.vector
                        def _(v):
                            for do in range(8):
                                bank = pSa if do % 2 == 0 else pSb
                                v.wait_ge(s_pe.h, pe2 + do + 1)
                                v.scalar_tensor_tensor(out=R[:, do, qc0:qc0 + 512],
                                                       in0=bank[:, 0:512], scalar=boc[:, do:do + 1],
                                                       in1=R[:, do, qc0:qc0 + 512],
                                                       op0=ALU.add, op1=ALU.add)
                                v.drain()
                                v.nop().then_inc(s_dve.h, 1)
                        s_dve.n += 8
                    prev_dve_out[0] = s_dve.n
                qblk.__exit__(None, None, None)

        if upto >= 1:
            phase_ln(0, TX)
            if t_z1 is not None:
                tap3(t_z1, zT[:, :, :])
        if upto >= 2:
            with ExitStack() as cs:
                cwf = cs.enter_context(nc.sbuf_tensor("cwf_1", [P, NCH, NE], F32))
                cwx = cs.enter_context(nc.sbuf_tensor("cwx_1", [16, TX], BF16))
                with nc.Block() as blk:
                    @blk.gpsimd
                    def _(g):
                        g.dma_start(cwx[NE:16, :], onesr[:1, :]).then_inc(s_in.h, 16)
                        s_in.n += 16
                        g.wait_ge(s_in.h, s_in.n)
                if upto >= 3:
                    phase_ffn(0, TX, [(0, 1152), (1152, 1024)], cwf, cwx)
                    if t_cw1 is not None:
                        tap3(t_cw1, cwf[:, :, :])
            if upto >= 3 and t_y1 is not None:
                tap3(t_y1, R[:, :, :])
        if upto >= 4:
            phase_ln(0, TX)
            if t_z2 is not None:
                tap3(t_z2, zT[:, :, :])
        if upto >= 5:
            phase_attn()
            if t_y2 is not None:
                tap3(t_y2, R[:, :, :])
        if upto >= 6:
            phase_ln(HALO, TOWN)
            if t_z3 is not None:
                tap3(t_z3, zT[:, :, :])
            with ExitStack() as cs:
                cwf = cs.enter_context(nc.sbuf_tensor("cwf_2", [P, NCH, NE], F32))
                cwx = cs.enter_context(nc.sbuf_tensor("cwx_2", [16, TX], BF16))
                with nc.Block() as blk:
                    @blk.gpsimd
                    def _(g):
                        g.dma_start(cwx[NE:16, :], onesr[:1, :]).then_inc(s_in.h, 16)
                        s_in.n += 16
                        g.wait_ge(s_in.h, s_in.n)
                phase_ffn(HALO, TOWN, [(0, 1024), (1024, 1024)], cwf, cwx, stream_out=True)
                if t_cw3 is not None:
                    tap3(t_cw3, cwf[:, :, :])

        if upto < 6:
            with nc.Block() as blk:
                @blk.sync
                def _(sy):
                    sy.dma_start(outT.rearrange("(c p) t -> p c t", p=P),
                                 R[:, :, HALO:HALO + TOWN]).then_inc(s_hw.h, 16)
                    s_hw.n += 16
                    sy.wait_ge(s_hw.h, s_hw.n)

    return nc, tapnames


def make_masks(core):
    r = np.arange(P)[:, None]
    jj = np.arange(256)[None, :]
    band = (jj - r >= 32) & (jj - r <= 96)
    mi = np.where(band, 0.0, NEG).astype(np.float32)
    ma = np.where(band & (jj >= 64), 0.0, NEG).astype(np.float32) if core == 0 else mi
    mc = np.where(band & (jj < 192), 0.0, NEG).astype(np.float32) if core == NCORE - 1 else mi
    st = np.stack([ma, mi, mc]).astype(np.float32)
    return np.ascontiguousarray(np.tile(st, (1, 1, 4)))


def prep_inputs(L0_attn_x, L0_attn_Wq, L0_attn_bq, L0_attn_Wk, L0_attn_bk,
                L0_attn_Wv, L0_attn_bv, L0_attn_Wo, L0_attn_bo,
                L0_ffn_router_W, L0_ffn_router_b, L0_ffn_sW1, L0_ffn_sb1,
                L0_ffn_sW2, L0_ffn_sb2, L0_ffn_eW1, L0_ffn_eb1,
                L0_ffn_eW2, L0_ffn_eb2):
    x = np.asarray(L0_attn_x, dtype=np.float32).reshape(SEQ, D)
    xTf = np.ascontiguousarray(x.T)
    xTpad = np.zeros((D, SEQ + 2 * HALO), np.float32)
    xTpad[:, HALO:HALO + SEQ] = xTf
    common = {
        "WqT": np.ascontiguousarray(np.asarray(L0_attn_Wq, np.float32).T),
        "WkT": np.ascontiguousarray(np.asarray(L0_attn_Wk, np.float32).T),
        "WvT": np.ascontiguousarray(np.asarray(L0_attn_Wv, np.float32).T),
        "WoT": np.ascontiguousarray(np.asarray(L0_attn_Wo, np.float32).T),
        "bq": np.ascontiguousarray(np.asarray(L0_attn_bq, np.float32).reshape(DC, P).T),
        "bk": np.ascontiguousarray(np.asarray(L0_attn_bk, np.float32).reshape(DC, P).T),
        "bv": np.asarray(L0_attn_bv, np.float32),
        "bo": np.ascontiguousarray(np.asarray(L0_attn_bo, np.float32).reshape(DC, P).T),
        "rW": np.asarray(L0_ffn_router_W, np.float32),
        "rbias": np.asarray(L0_ffn_router_b, np.float32),
        "sW1": np.asarray(L0_ffn_sW1, np.float32),
        "sW2": np.asarray(L0_ffn_sW2, np.float32),
        "sb1": np.ascontiguousarray(np.asarray(L0_ffn_sb1, np.float32).reshape(DC, P).T),
        "sb2": np.asarray(L0_ffn_sb2, np.float32),
        "eW1": np.asarray(L0_ffn_eW1, np.float32),
        "eW2": np.asarray(L0_ffn_eW2, np.float32),
        "eb1T": np.ascontiguousarray(np.asarray(L0_ffn_eb1, np.float32).T),
        "eb2": np.asarray(L0_ffn_eb2, np.float32),
    }
    in_maps = []
    for c in range(NCORE):
        m = dict(common)
        st = c * TOWN
        m["xT"] = np.ascontiguousarray(xTpad[:, st:st + TX])
        m["masks"] = make_masks(c)
        in_maps.append(m)
    return in_maps


def kernel(**inputs):
    in_maps = prep_inputs(**inputs)
    nc, _ = build_program()
    res = run_bass_kernel_spmd(nc, in_maps, core_ids=list(range(NCORE)))
    parts = [np.asarray(r["outT"]) for r in res.results]
    full = np.concatenate(parts, axis=1)
    return np.ascontiguousarray(full.T).reshape(1, SEQ, D).astype(np.float32)


# version 24
# speedup vs baseline: 1.2003x; 1.0353x over previous
import numpy as np
from contextlib import ExitStack

import concourse.bass as bass
import concourse.mybir as mybir
from concourse.bass_utils import run_bass_kernel_spmd

F32 = mybir.dt.float32
BF16 = mybir.dt.bfloat16
AF = mybir.ActivationFunctionType
ALU = mybir.AluOpType
AX = mybir.AxisListType

P = 128
D = 1024
DC = D // P
TOWN = 2048
HALO = 64
TX = TOWN + 2 * HALO
NCH = TX // P
NE = 15
NH = 16
HD = 64
EPS = 1e-5
NEG = -1.0e9
NCORE = 8
SEQ = 16384


def tiles(width, maxn=512):
    out = []
    o = 0
    while o < width:
        n = min(maxn, width - o)
        out.append((o, n))
        o += n
    return out


class Ctr:
    def __init__(self, es, nc, name):
        self.h = es.enter_context(nc.semaphore(name))
        self.n = 0


def build_program(upto=99, taps=(), aq=4, anit=16, aout=True, acut=99):
    nc = bass.Bass("TRN2", target_bir_lowering=False)
    taps = set(taps)
    tapnames = []
    uid = [0]

    xT = nc.dram_tensor("xT", [D, TX], F32, kind="ExternalInput").ap()
    masks = nc.dram_tensor("masks", [3, P, 1024], F32, kind="ExternalInput").ap()
    WqT = nc.dram_tensor("WqT", [D, D], F32, kind="ExternalInput").ap()
    WkT = nc.dram_tensor("WkT", [D, D], F32, kind="ExternalInput").ap()
    WvT = nc.dram_tensor("WvT", [D, D], F32, kind="ExternalInput").ap()
    WoT = nc.dram_tensor("WoT", [D, D], F32, kind="ExternalInput").ap()
    bq = nc.dram_tensor("bq", [P, DC], F32, kind="ExternalInput").ap()
    bk = nc.dram_tensor("bk", [P, DC], F32, kind="ExternalInput").ap()
    bv = nc.dram_tensor("bv", [D], F32, kind="ExternalInput").ap()
    bo = nc.dram_tensor("bo", [P, DC], F32, kind="ExternalInput").ap()
    rW = nc.dram_tensor("rW", [D, NE], F32, kind="ExternalInput").ap()
    rbias = nc.dram_tensor("rbias", [NE], F32, kind="ExternalInput").ap()
    sW1 = nc.dram_tensor("sW1", [D, D], F32, kind="ExternalInput").ap()
    sW2 = nc.dram_tensor("sW2", [D, D], F32, kind="ExternalInput").ap()
    sb1 = nc.dram_tensor("sb1", [P, DC], F32, kind="ExternalInput").ap()
    sb2 = nc.dram_tensor("sb2", [D], F32, kind="ExternalInput").ap()
    eW1 = nc.dram_tensor("eW1", [NE, D, D], F32, kind="ExternalInput").ap()
    eW2 = nc.dram_tensor("eW2", [NE, D, D], F32, kind="ExternalInput").ap()
    eb1T = nc.dram_tensor("eb1T", [D, NE], F32, kind="ExternalInput").ap()
    eb2 = nc.dram_tensor("eb2", [NE, D], F32, kind="ExternalInput").ap()

    outT = nc.dram_tensor("outT", [D, TOWN], F32, kind="ExternalOutput").ap()

    scr_rows = nc.dram_tensor("scr_rows", [2, TX], F32).ap()
    scr_cwt = nc.dram_tensor("scr_cwt", [NE, TX], BF16).ap()

    def mktap(name, shape, dtype=F32):
        if name in taps:
            tapnames.append(name)
            return nc.dram_tensor("tap_" + name, shape, dtype, kind="ExternalOutput").ap()
        return None

    t_z1 = mktap("z1", [P, DC, TX], BF16)
    t_cw1 = mktap("cw1", [P, NCH, NE], F32)
    t_y1 = mktap("y1", [P, DC, TX], F32)
    t_z2 = mktap("z2", [P, DC, TX], BF16)
    t_kT = mktap("kT", [P, DC, 640], BF16)
    t_v = mktap("v", [P, 5, D], BF16)
    t_qT = mktap("qT", [P, DC, 512], BF16)
    t_y2 = mktap("y2", [P, DC, TX], F32)
    t_z3 = mktap("z3", [P, DC, TX], BF16)
    t_cw3 = mktap("cw3", [P, NCH, NE], F32)

    es = ExitStack()
    with es:
        R = es.enter_context(nc.sbuf_tensor("R", [P, DC, TX], F32))
        zT = es.enter_context(nc.sbuf_tensor("zT", [P, DC, TX], BF16))
        bqc = es.enter_context(nc.sbuf_tensor("bqc", [P, DC], F32))
        bkc = es.enter_context(nc.sbuf_tensor("bkc", [P, DC], F32))
        boc = es.enter_context(nc.sbuf_tensor("boc", [P, DC], F32))
        bvb = es.enter_context(nc.sbuf_tensor("bvb", [P, D], BF16))
        sb1c = es.enter_context(nc.sbuf_tensor("sb1c", [P, DC], F32))
        eb1c = es.enter_context(nc.sbuf_tensor("eb1c", [P, DC, NE], F32))
        eb2s = es.enter_context(nc.sbuf_tensor("eb2s", [16, D], BF16))
        rWb = es.enter_context(nc.sbuf_tensor("rWb", [P, DC, NE], BF16))
        rbb = es.enter_context(nc.sbuf_tensor("rbb", [P, NE], F32))
        maskb = es.enter_context(nc.sbuf_tensor("maskb", [P, 3, 1024], BF16))
        identf = es.enter_context(nc.sbuf_tensor("identf", [P, P], F32))
        identb = es.enter_context(nc.sbuf_tensor("identb", [P, P], BF16))
        ones_col = es.enter_context(nc.sbuf_tensor("ones_col", [P, 1], BF16))
        ncol = es.enter_context(nc.sbuf_tensor("ncol", [P, 1], F32))
        qcol = es.enter_context(nc.sbuf_tensor("qcol", [P, 1], BF16))
        iota15 = es.enter_context(nc.sbuf_tensor("iota15", [P, NE], F32))
        cwf = es.enter_context(nc.sbuf_tensor("cwf", [P, NCH, NE], F32))
        cst = es.enter_context(nc.sbuf_tensor("cst", [P, 4], F32))
        onesr = es.enter_context(nc.sbuf_tensor("onesr", [1, TX], BF16))
        cwx = es.enter_context(nc.sbuf_tensor("cwx", [16, TX], BF16))

        s_in = Ctr(es, nc, "s_in")
        s_wm = [Ctr(es, nc, f"s_wm{i}") for i in range(3)]
        s_hw = Ctr(es, nc, "s_hw")
        s_cb = [Ctr(es, nc, f"s_cb{i}") for i in range(2)]
        s_pe = Ctr(es, nc, "s_pe")
        s_act = Ctr(es, nc, "s_act")
        s_dve = Ctr(es, nc, "s_dve")
        s_cfree = Ctr(es, nc, "s_cfree")
        s_gp = Ctr(es, nc, "s_gp")
        s_rpe = Ctr(es, nc, "s_rpe")
        s_rdv = Ctr(es, nc, "s_rdv")
        s_rac = Ctr(es, nc, "s_rac")

        def tap3(tapap, sbap):
            with nc.Block() as blk:
                @blk.sync
                def _(sy):
                    sy.dma_start(tapap, sbap).then_inc(s_hw.h, 16)
                    s_hw.n += 16
                    sy.wait_ge(s_hw.h, s_hw.n)

        with ExitStack() as ps:
            ioA = ps.enter_context(nc.sbuf_tensor("ioA", [P, P], F32))
            ioB = ps.enter_context(nc.sbuf_tensor("ioB", [P, P], F32))
            sb2row = ps.enter_context(nc.sbuf_tensor("sb2row", [1, D], F32))
            sb2h = ps.enter_context(nc.sbuf_tensor("sb2h", [1, D], BF16))
            with nc.Block() as blk:
                @blk.gpsimd
                def _(g):
                    def d(out, in_):
                        g.dma_start(out, in_).then_inc(s_in.h, 16)
                        s_in.n += 16
                    d(R[:, 4:8, :], xT.rearrange("(c p) t -> p c t", p=P)[:, 4:8, :])
                    d(maskb[:, :, :], masks.rearrange("m p t -> p m t"))
                    d(sb2row[:1, :], sb2.rearrange("(a d) -> a d", a=1))
                    d(eb2s[0:NE, :], eb2[:, :])
                    d(bvb[:, :], bv.partition_broadcast(P))
                    d(rWb[:, :, :], rW.rearrange("(c p) e -> p c e", p=P))
                    g.iota(iota15[:, :], [[1, NE]], channel_multiplier=0,
                           allow_small_or_imprecise_dtypes=True)
                    g.iota(ioA[:, :], [[1, P]], channel_multiplier=0,
                           allow_small_or_imprecise_dtypes=True)
                    g.iota(ioB[:, :], [[0, P]], channel_multiplier=1,
                           allow_small_or_imprecise_dtypes=True)
                    g.memset(ones_col[:, :], 1.0)
                    g.memset(ncol[:, :], -1.0 / D)
                    g.memset(qcol[:, :], 1.0 / D)
                    g.memset(cst[:, 3:4], -0.5)
                    g.memset(onesr[:1, :], 1.0)
                    g.memset(cst[:, 0:1], EPS)
                    g.memset(cst[:, 1:2], 0.5)
                    g.memset(cst[:, 2:3], 0.125)
                    g.wait_ge(s_in.h, s_in.n)

                @blk.sync
                def _(sy):
                    def d2_(out, in_):
                        sy.dma_start(out, in_).then_inc(s_hw.h, 16)
                        s_hw.n += 16
                    d2_(R[:, 0:4, :], xT.rearrange("(c p) t -> p c t", p=P)[:, 0:4, :])
                    d2_(bqc[:, :], bq[:, :])
                    d2_(bkc[:, :], bk[:, :])
                    d2_(boc[:, :], bo[:, :])
                    d2_(sb1c[:, :], sb1[:, :])
                    d2_(eb1c[:, :, :], eb1T.rearrange("(c p) e -> p c e", p=P))
                    d2_(rbb[:, :], rbias.partition_broadcast(P))
                    sy.wait_ge(s_hw.h, s_hw.n)
            with nc.Block() as blk:
                @blk.vector
                def _(v):
                    v.tensor_tensor(out=identf[:, :], in0=ioA[:, :], in1=ioB[:, :], op=ALU.is_equal)
                    v.drain()
                    v.tensor_copy(identb[:, :], identf[:, :])
                    v.tensor_scalar_mul(bqc[:, :], bqc[:, :], 0.125)
                    v.tensor_scalar_mul(sb1c[:, :], sb1c[:, :], 0.5)
                @blk.scalar
                def _(s):
                    s.activation(sb2h[:1, :], sb2row[:1, :], AF.Copy, scale=cst[0:1, 1:2]).then_inc(s_act.h, 1)
                    s_act.n += 1

            with nc.Block() as blk:
                @blk.gpsimd
                def _(g):
                    g.wait_ge(s_act.h, s_act.n)
                    g.dma_start(eb2s[NE:16, :], sb2h[:1, :]).then_inc(s_in.h, 16)
                    s_in.n += 16
                    g.wait_ge(s_in.h, s_in.n)

        def phase_ln(cl0, cln):
            uid[0] += 1
            u = uid[0]
            hw_w = cln // 2
            NDV = 5
            with ExitStack() as ps:
                ssum = ps.enter_context(nc.sbuf_tensor(f"ln_su{u}", [1, TX], F32))
                ssq = ps.enter_context(nc.sbuf_tensor(f"ln_sq{u}", [1, TX], F32))
                svar = ps.enter_context(nc.sbuf_tensor(f"ln_va{u}", [1, TX], F32))
                slog = ps.enter_context(nc.sbuf_tensor(f"ln_lg{u}", [1, TX], F32))
                srstd = ps.enter_context(nc.sbuf_tensor(f"ln_sr{u}", [1, TX], F32))
                mB = ps.enter_context(nc.sbuf_tensor(f"ln_mb{u}", [P, TX], F32))
                rB = ps.enter_context(nc.sbuf_tensor(f"ln_rb{u}", [P, TX], F32))
                gsq = ps.enter_context(nc.sbuf_tensor(f"ln_gsq{u}", [P, DC, cln // 2], BF16))
                tl = tiles(hw_w, 512)
                nt = len(tl)
                psA = [ps.enter_context(nc.psum_tensor(f"ln_psA{u}_{i}", [1, 512], F32))
                       for i in range(nt)]
                psB = [ps.enter_context(nc.psum_tensor(f"ln_psB{u}_{i}", [1, 512], F32))
                       for i in range(nt)]
                act0, pe0, dve0 = s_act.n, s_pe.n, s_dve.n
                hw0n = s_hw.n
                with nc.Block() as blk:
                    @blk.scalar
                    def _(s):
                        for hw in range(2):
                            c0 = cl0 + hw * hw_w
                            if hw == 1:
                                s.wait_ge(s_pe.h, pe0 + 2 * nt)
                            for c in range(DC):
                                mm = s.activation(gsq[:, c, 0:hw_w],
                                                  R[:, c, c0:c0 + hw_w], AF.Square)
                                if c == DC - 1:
                                    mm.then_inc(s_act.h, 1)
                        for hw in range(2):
                            o0 = hw * hw_w
                            s.wait_ge(s_dve.h, dve0 + 3 * (hw + 1))
                            s.activation(slog[:1, o0:o0 + hw_w], svar[:1, o0:o0 + hw_w],
                                         AF.Ln, bias=cst[0:1, 0:1])
                            s.drain()
                            s.activation(srstd[:1, o0:o0 + hw_w], slog[:1, o0:o0 + hw_w],
                                         AF.Exp, scale=cst[0:1, 3:4]).then_inc(s_act.h, 1)
                    s_act.n += 4

                    @blk.tensor
                    def _(t):
                        for hw in range(2):
                            c0 = cl0 + hw * hw_w
                            o0 = hw * hw_w
                            for ti, (to, tn) in enumerate(tl):
                                if hw == 1 and ti == 0:
                                    t.wait_ge(s_dve.h, dve0 + 1)
                                for c in range(DC):
                                    st, sp = (c == 0), (c == DC - 1)
                                    mm = t.matmul(psA[ti][0:1, 0:tn], ncol[:, :],
                                                  R[:, c, c0 + to: c0 + to + tn],
                                                  start=st, stop=sp)
                                    if sp:
                                        mm.then_inc(s_pe.h, 1)
                            t.wait_ge(s_act.h, act0 + hw + 1)
                            for ti, (to, tn) in enumerate(tl):
                                if hw == 1 and ti == 0:
                                    t.wait_ge(s_dve.h, dve0 + 2)
                                for c in range(DC):
                                    st, sp = (c == 0), (c == DC - 1)
                                    mm = t.matmul(psB[ti][0:1, 0:tn], qcol[:, :],
                                                  gsq[:, c, to:to + tn],
                                                  start=st, stop=sp)
                                    if sp:
                                        mm.then_inc(s_pe.h, 1)
                    s_pe.n += 4 * nt

                    @blk.vector
                    def _(v):
                        def evacs(hw, sq, ptensors, dst):
                            o0 = hw * hw_w
                            base = pe0 + (2 * hw + sq) * nt
                            for ti, (to, tn) in enumerate(tl):
                                v.wait_ge(s_pe.h, base + ti + 1)
                                v.tensor_copy(dst[:1, o0 + to:o0 + to + tn],
                                              ptensors[ti][0:1, 0:tn])
                            v.drain()
                            v.nop().then_inc(s_dve.h, 1)

                        def rowmath(hw):
                            o0 = hw * hw_w
                            rs = slice(o0, o0 + hw_w)
                            v.tensor_tensor(out=svar[:1, rs], in0=ssum[:1, rs],
                                            in1=ssum[:1, rs], op=ALU.mult)
                            v.drain()
                            v.tensor_tensor(out=svar[:1, rs], in0=ssq[:1, rs],
                                            in1=svar[:1, rs], op=ALU.subtract)
                            v.drain()
                            v.nop().then_inc(s_dve.h, 1)

                        def npass(hw, p2):
                            c0 = cl0 + hw * hw_w
                            o0 = hw * hw_w
                            v.wait_ge(s_hw.h, hw0n + 32 * (2 * hw + p2 + 1))
                            for c in range(NDV):
                                sl = slice(c0, c0 + hw_w)
                                if p2:
                                    v.tensor_tensor(out=zT[:, c, sl], in0=zT[:, c, sl],
                                                    in1=rB[:, o0:o0 + hw_w], op=ALU.mult)
                                else:
                                    v.tensor_tensor(out=zT[:, c, sl], in0=R[:, c, sl],
                                                    in1=mB[:, o0:o0 + hw_w], op=ALU.add)
                            v.drain()

                        evacs(0, 0, psA, ssum)
                        evacs(0, 1, psB, ssq)
                        rowmath(0)
                        npass(0, 0)
                        evacs(1, 0, psA, ssum)
                        evacs(1, 1, psB, ssq)
                        rowmath(1)
                        npass(0, 1)
                        npass(1, 0)
                        npass(1, 1)
                    s_dve.n += 6

                    @blk.sync
                    def _(sy):
                        cnt = hw0n
                        for hw in range(2):
                            o0 = hw * hw_w
                            rs = slice(o0, o0 + hw_w)
                            sy.wait_ge(s_dve.h, dve0 + 1 + 3 * hw)
                            sy.dma_start(scr_rows[0:1, rs], ssum[:1, rs]).then_inc(s_hw.h, 16)
                            cnt += 16
                            sy.wait_ge(s_hw.h, cnt)
                            sy.dma_start(mB[:, rs],
                                         scr_rows[0, rs].partition_broadcast(P)).then_inc(s_hw.h, 16)
                            cnt += 16
                            sy.wait_ge(s_hw.h, cnt)
                            sy.wait_ge(s_act.h, act0 + 3 + hw)
                            sy.dma_start(scr_rows[1:2, rs], srstd[:1, rs]).then_inc(s_hw.h, 16)
                            cnt += 16
                            sy.wait_ge(s_hw.h, cnt)
                            sy.dma_start(rB[:, rs],
                                         scr_rows[1, rs].partition_broadcast(P)).then_inc(s_hw.h, 16)
                            cnt += 16
                            if hw == 0:
                                sy.wait_ge(s_hw.h, cnt)
                        s_hw.n += 128

                    @blk.gpsimd
                    def _(g):
                        for hw in range(2):
                            c0 = cl0 + hw * hw_w
                            o0 = hw * hw_w
                            sl = slice(c0, c0 + hw_w)
                            g.wait_ge(s_hw.h, hw0n + 64 * hw + 32)
                            for c in range(NDV, DC):
                                g.tensor_tensor(out=zT[:, c, sl], in0=R[:, c, sl],
                                                in1=mB[:, o0:o0 + hw_w], op=ALU.add)
                            g.drain()
                            g.wait_ge(s_hw.h, hw0n + 64 * hw + 64)
                            for c in range(NDV, DC):
                                g.tensor_tensor(out=zT[:, c, sl], in0=zT[:, c, sl],
                                                in1=rB[:, o0:o0 + hw_w], op=ALU.mult)
                            g.drain()

        def make_router(cl0, cln, cwf, cwx, ps):
            uid[0] += 1
            u = uid[0]
            nch = cln // P
            nf = nch * NE
            prt = ps.enter_context(nc.psum_tensor(f"rt_p{u}", [P, NE], F32))
            ptw = ps.enter_context(nc.psum_tensor(f"rt_t{u}", [NE, P], F32))
            logt = ps.enter_context(nc.sbuf_tensor(f"rt_lg{u}", [P, NCH, NE], F32))
            probs = ps.enter_context(nc.sbuf_tensor(f"rt_pr{u}", [P, NCH, NE], F32))
            tt1 = ps.enter_context(nc.sbuf_tensor(f"rt_t1{u}", [P, NCH, NE], F32))
            nmx = ps.enter_context(nc.sbuf_tensor(f"rt_nm{u}", [P, 1], F32))
            f17 = ps.enter_context(nc.sbuf_tensor(f"rt_f17{u}", [P, NCH], F32))
            r17 = ps.enter_context(nc.sbuf_tensor(f"rt_r17{u}", [P, NCH], F32))
            iota15m = ps.enter_context(nc.sbuf_tensor(f"rt_iom{u}", [P, NE], F32))
            pe0r, dve0r, act0r = s_rpe.n, s_rdv.n, s_rac.n
            dve1r = dve0r + nch
            dve2r = dve1r + 2
            pe1r = pe0r + nch
            st = {"cwx_ev": dve2r + nch + 1}

            L = logt.ap().rearrange("p a b -> p (a b)")[:, 0:nf].rearrange("p (a b) -> p a b", b=NE)
            PR = probs.ap().rearrange("p a b -> p (a b)")[:, 0:nf].rearrange("p (a b) -> p a b", b=NE)
            T1 = tt1.ap().rearrange("p a b -> p (a b)")[:, 0:nf].rearrange("p (a b) -> p a b", b=NE)
            CW = cwf.ap().rearrange("p a b -> p (a b)")[:, 0:nf].rearrange("p (a b) -> p a b", b=NE)
            Lf = logt.ap().rearrange("p a b -> p (a b)")[:, 0:nf]
            PRf = probs.ap().rearrange("p a b -> p (a b)")[:, 0:nf]
            T1f = tt1.ap().rearrange("p a b -> p (a b)")[:, 0:nf]
            CWf = cwf.ap().rearrange("p a b -> p (a b)")[:, 0:nf]
            io_b = iota15.ap().unsqueeze(1).broadcast_to((P, nch, NE))
            iom_b = iota15m.ap().unsqueeze(1).broadcast_to((P, nch, NE))

            def pe_logits(t):
                for k in range(nch):
                    if k >= 1:
                        t.wait_ge(s_rdv.h, dve0r + k)
                    for c in range(DC):
                        mm = t.matmul(prt[:, :], zT[:, c, cl0 + k * P: cl0 + (k + 1) * P],
                                      rWb[:, c, :], start=(c == 0), stop=(c == DC - 1))
                        if c == DC - 1:
                            mm.then_inc(s_rpe.h, 1)
                s_rpe.n += nch

            def pe_trans(t):
                t.wait_ge(s_rdv.h, dve1r + 2)
                for k in range(nch):
                    if k >= 1:
                        t.wait_ge(s_rdv.h, dve2r + k)
                    t.transpose(ptw[0:NE, :], cwf[:, k, :], identf[:, :]).then_inc(s_rpe.h, 1)
                s_rpe.n += nch

            def dve(v):
                v.tensor_scalar_mul(iota15m[:, :], iota15[:, :], 1.0)
                v.drain()
                v.tensor_scalar_add(out=iota15m[:, :], in0=iota15m[:, :], scalar1=-100.0)
                v.drain()
                for k in range(nch):
                    v.wait_ge(s_rpe.h, pe0r + k + 1)
                    v.tensor_tensor(out=logt[:, k, :], in0=prt[:, :], in1=rbb[:, :], op=ALU.add)
                    v.drain()
                    v.nop().then_inc(s_rdv.h, 1)
                v.tensor_reduce(out=nmx[:, 0:1], in_=Lf, axis=AX.X, op=ALU.max, negate=True)
                v.drain()
                v.nop().then_inc(s_rdv.h, 1)
                v.wait_ge(s_rac.h, act0r + 1)
                v.tensor_reduce(out=f17[:, 0:nch], in_=PR, axis=AX.X, op=ALU.add)
                v.drain()
                v.reciprocal(r17[:, 0:nch], f17[:, 0:nch])
                v.drain()
                v.tensor_tensor(out=PR, in0=PR,
                                in1=r17[:, 0:nch].unsqueeze(2).broadcast_to((P, nch, NE)), op=ALU.mult)
                v.drain()
                v.memset(CWf, 0.0)
                v.drain()
                for it in range(3):
                    v.tensor_reduce(out=f17[:, 0:nch], in_=PR, axis=AX.X, op=ALU.max, negate=True)
                    v.drain()
                    fb = f17[:, 0:nch].unsqueeze(2).broadcast_to((P, nch, NE))
                    v.tensor_tensor(out=T1, in0=PR, in1=fb, op=ALU.add)
                    v.drain()
                    v.tensor_single_scalar(out=T1f, in_=T1f, scalar=0.0, op=ALU.is_equal)
                    v.drain()
                    v.tensor_tensor(out=T1, in0=T1, in1=iom_b, op=ALU.mult)
                    v.drain()
                    v.tensor_scalar_add(out=T1f, in0=T1f, scalar1=100.0)
                    v.drain()
                    v.tensor_reduce(out=f17[:, 0:nch], in_=T1, axis=AX.X, op=ALU.min)
                    v.drain()
                    fb = f17[:, 0:nch].unsqueeze(2).broadcast_to((P, nch, NE))
                    v.tensor_tensor(out=T1, in0=io_b, in1=fb, op=ALU.subtract)
                    v.drain()
                    v.tensor_single_scalar(out=T1f, in_=T1f, scalar=0.0, op=ALU.is_equal)
                    v.drain()
                    v.tensor_reduce(out=r17[:, 0:nch], in_=PR, axis=AX.X, op=ALU.max, negate=True)
                    v.drain()
                    rb2 = r17[:, 0:nch].unsqueeze(2).broadcast_to((P, nch, NE))
                    v.tensor_tensor(out=T1, in0=T1, in1=rb2, op=ALU.mult)
                    v.drain()
                    v.tensor_tensor(out=CW, in0=CW, in1=T1, op=ALU.subtract)
                    v.drain()
                    v.tensor_tensor(out=PR, in0=PR, in1=T1, op=ALU.add)
                    v.drain()
                v.tensor_scalar_mul(CWf, CWf, 0.5)
                v.drain()
                v.nop().then_inc(s_rdv.h, 1)
                for k in range(nch):
                    v.wait_ge(s_rpe.h, pe1r + k + 1)
                    v.tensor_copy(cwx[0:NE, cl0 + k * P: cl0 + (k + 1) * P], ptw[0:NE, :])
                    v.drain()
                    v.nop().then_inc(s_rdv.h, 1)
                v.nop().then_inc(s_rdv.h, 1)
                s_rdv.n += nch + 2 + nch + 1

            def act(sc):
                sc.wait_ge(s_rdv.h, dve1r + 1)
                sc.activation(PRf, Lf, AF.Exp, bias=nmx[:, 0:1]).then_inc(s_rac.h, 1)
                s_rac.n += 1

            def sync(sy):
                sy.wait_ge(s_rdv.h, st["cwx_ev"])
                sy.dma_start(scr_cwt[:, cl0:cl0 + cln], cwx[0:NE, cl0:cl0 + cln]).then_inc(s_hw.h, 16)
                s_hw.n += 16
                sy.wait_ge(s_hw.h, s_hw.n)

            return {"pe_logits": pe_logits, "pe_trans": pe_trans, "dve": dve,
                    "act": act, "sync": sync, "cwx_ev": st["cwx_ev"]}

        def phase_ffn(cl0, cln, halves, cwf, cwx, stream_out=False):
            uid[0] += 1
            u = uid[0]
            nmat = 2 * (NE + 1)
            nh = len(halves)
            with ExitStack() as ps:
                pA = ps.enter_context(nc.psum_tensor(f"ff_pA{u}", [P, 1152], F32))
                pB = ps.enter_context(nc.psum_tensor(f"ff_pB{u}", [P, 1152], F32))
                w0 = ps.enter_context(nc.sbuf_tensor(f"ff_w0{u}", [P, DC, D], BF16))
                w1 = ps.enter_context(nc.sbuf_tensor(f"ff_w1{u}", [P, DC, D], BF16))
                w2 = ps.enter_context(nc.sbuf_tensor(f"ff_w2{u}", [P, DC, D], BF16))
                wb = [w0, w1, w2]
                gbuf = ps.enter_context(nc.sbuf_tensor(f"ff_g{u}", [P, DC, 1152], BF16))
                rb0 = ps.enter_context(nc.sbuf_tensor(f"ff_r0{u}", [P, 1152], BF16))
                rb1 = ps.enter_context(nc.sbuf_tensor(f"ff_r1{u}", [P, 1152], BF16))
                rbufs = [rb0, rb1]
                cb0 = ps.enter_context(nc.sbuf_tensor(f"ff_c0{u}", [P, 1152], BF16))
                cb1 = ps.enter_context(nc.sbuf_tensor(f"ff_c1{u}", [P, 1152], BF16))
                cbufs = [cb0, cb1]

                rt = make_router(cl0, cln, cwf, cwx, ps)
                pe0, act0, dve0 = s_pe.n, s_act.n, s_dve.n
                cf0 = s_cfree.n
                wtgt = []
                for m in range(nmat):
                    c = s_wm[m % 3]
                    c.n += 16
                    wtgt.append(c.n)
                ctgt = {}
                for e in range(NE):
                    for hf in range(nh):
                        c = s_cb[e % 2]
                        c.n += 16
                        ctgt[(e, hf)] = c.n

                def pe_h(ei, hf, dh):
                    return pe0 + ei * 32 + hf * 16 + dh + 1

                def pe_ye(ei, hf, do):
                    return pe0 + ei * 32 + hf * 16 + 8 + do + 1

                def act_relu(ei, hf, dh):
                    return act0 + ei * 16 + hf * 8 + dh + 1

                def dve_g(ei, hf, dh):
                    return dve0 + 16 + (ei - 1) * 32 + hf * 16 + dh + 1

                def dve_R(ei, hf, do):
                    if ei == 0:
                        return dve0 + hf * 8 + do + 1
                    return dve0 + 16 + (ei - 1) * 32 + hf * 16 + 8 + do + 1

                GSEQ = []
                for ei in range(NE + 1):
                    for hf in range(nh):
                        for dh in range(8):
                            GSEQ.append(("h", ei, hf, dh))
                        for do in range(8):
                            GSEQ.append(("ye", ei, hf, do))

                def free_count(gidx):
                    kind, ei, hf, x = GSEQ[gidx]
                    if kind == "h":
                        return (s_act.h, act_relu(ei, hf, x))
                    return (s_dve.h, dve_R(ei, hf, x))

                with nc.Block() as blk:
                    @blk.gpsimd
                    def _(g):
                        for m in range(nmat):
                            ei, is2 = divmod(m, 2)
                            if ei == 0:
                                src = sW2 if is2 else sW1
                            else:
                                src = (eW2 if is2 else eW1)[ei - 1]
                            if m >= 3:
                                mp = m - 3
                                eip, is2p = divmod(mp, 2)
                                if is2p == 0:
                                    g.wait_ge(s_pe.h, pe_h(eip, nh - 1, 7))
                                else:
                                    g.wait_ge(s_pe.h, pe_ye(eip, nh - 1, 7))
                            g.dma_start(wb[m % 3][:, :, :],
                                        src.rearrange("(c p) n -> p c n", p=P)).then_inc(s_wm[m % 3].h, 16)
                        for i in range(3):
                            g.wait_ge(s_wm[i].h, s_wm[i].n)

                    @blk.sync
                    def _(sy):
                        rt["sync"](sy)
                        for e in range(NE):
                            for hf, (ho, hn) in enumerate(halves):
                                if hf >= 1:
                                    sy.wait_ge(s_cfree.h, cf0 + 2 * e + 1)
                                elif e >= 2:
                                    sy.wait_ge(s_cfree.h, cf0 + 2 * (e - 2) + 2)
                                sy.dma_start(cbufs[e % 2][:, 0:hn],
                                             scr_cwt[e, cl0 + ho: cl0 + ho + hn].partition_broadcast(P)
                                             ).then_inc(s_cb[e % 2].h, 16)
                        for i in range(2):
                            sy.wait_ge(s_cb[i].h, s_cb[i].n)
                        if stream_out:
                            for hf, (ho, hn) in enumerate(halves):
                                sy.wait_ge(s_dve.h, dve_R(NE, hf, 7))
                                sy.dma_start(
                                    outT.rearrange("(c p) t -> p c t", p=P)[:, :, ho: ho + hn],
                                    R[:, :, cl0 + ho: cl0 + ho + hn]).then_inc(s_hw.h, 16)
                                s_hw.n += 16
                            sy.wait_ge(s_hw.h, s_hw.n)

                    @blk.tensor
                    def _(t):
                        rt["pe_logits"](t)
                        ebwait = [True]
                        gidx = 0
                        for ei in range(NE + 1):
                            m1, m2 = 2 * ei, 2 * ei + 1
                            for hf, (ho, hn) in enumerate(halves):
                                tl = tiles(hn, 512)
                                if hf == 0:
                                    t.wait_ge(s_wm[m1 % 3].h, wtgt[m1])
                                for dh in range(8):
                                    bank = pA if gidx % 2 == 0 else pB
                                    if gidx >= 2:
                                        sh, cn = free_count(gidx - 2)
                                        t.wait_ge(sh, cn)
                                    for c in range(DC):
                                        for ti, (to, tn) in enumerate(tl):
                                            mm = t.matmul(bank[:, to:to + tn],
                                                          wb[m1 % 3][:, c, dh * P:(dh + 1) * P],
                                                          zT[:, c, cl0 + ho + to: cl0 + ho + to + tn],
                                                          start=(c == 0), stop=(c == DC - 1))
                                            if c == DC - 1 and ti == len(tl) - 1:
                                                mm.then_inc(s_pe.h, 1)
                                    gidx += 1
                                if ei == 0 and hf == 0:
                                    rt["pe_trans"](t)
                                if hf == 0:
                                    t.wait_ge(s_wm[m2 % 3].h, wtgt[m2])
                                for do in range(8):
                                    bank = pA if gidx % 2 == 0 else pB
                                    if gidx >= 2:
                                        sh, cn = free_count(gidx - 2)
                                        t.wait_ge(sh, cn)
                                    for dh in range(8):
                                        if do == 0:
                                            if ei == 0:
                                                t.wait_ge(s_act.h, act_relu(0, hf, dh))
                                            else:
                                                t.wait_ge(s_dve.h, dve_g(ei, hf, dh))
                                        for ti, (to, tn) in enumerate(tl):
                                            last = (dh == DC - 1) and (ei != 0)
                                            mm = t.matmul(bank[:, to:to + tn],
                                                          wb[m2 % 3][:, dh, do * P:(do + 1) * P],
                                                          gbuf[:, dh, to:to + tn],
                                                          start=(dh == 0), stop=last)
                                            if last and ti == len(tl) - 1:
                                                mm.then_inc(s_pe.h, 1)
                                    if ei == 0:
                                        if ebwait[0]:
                                            t.wait_ge(s_rdv.h, rt["cwx_ev"])
                                            ebwait[0] = False
                                        for ti, (to, tn) in enumerate(tl):
                                            mm = t.matmul(bank[:, to:to + tn],
                                                          eb2s[:, do * P:(do + 1) * P],
                                                          cwx[:, cl0 + ho + to: cl0 + ho + to + tn],
                                                          start=False, stop=True)
                                            if ti == len(tl) - 1:
                                                mm.then_inc(s_pe.h, 1)
                                    gidx += 1
                    s_pe.n += (NE + 1) * nh * 16

                    @blk.scalar
                    def _(s):
                        rt["act"](s)
                        for ei in range(NE + 1):
                            for hf, (ho, hn) in enumerate(halves):
                                for dh in range(8):
                                    gidx = ei * 32 + hf * 16 + dh
                                    bank = pA if gidx % 2 == 0 else pB
                                    s.wait_ge(s_pe.h, pe_h(ei, hf, dh))
                                    if ei == 0:
                                        s.activation(gbuf[:, dh, 0:hn], bank[:, 0:hn], AF.Relu,
                                                     bias=sb1c[:, dh:dh + 1],
                                                     scale=cst[:, 1:2]).then_inc(s_act.h, 1)
                                    else:
                                        ridx = (ei - 1) * 16 + hf * 8 + dh
                                        if ridx >= 2:
                                            e2, r2 = divmod(ridx - 2, 16)
                                            hf2, dh2 = divmod(r2, 8)
                                            s.wait_ge(s_dve.h, dve_g(e2 + 1, hf2, dh2))
                                        s.activation(rbufs[ridx % 2][:, 0:hn], bank[:, 0:hn], AF.Relu,
                                                     bias=eb1c[:, dh, ei - 1:ei]).then_inc(s_act.h, 1)
                    s_act.n += (NE + 1) * nh * 8

                    @blk.vector
                    def _(v):
                        rt["dve"](v)
                        for ei in range(NE + 1):
                            for hf, (ho, hn) in enumerate(halves):
                                if ei >= 1:
                                    e = ei - 1
                                    v.wait_ge(s_cb[e % 2].h, ctgt[(e, hf)])
                                    for dh in range(8):
                                        ridx = e * 16 + hf * 8 + dh
                                        v.wait_ge(s_act.h, act_relu(ei, hf, dh))
                                        v.tensor_tensor(out=gbuf[:, dh, 0:hn], in0=rbufs[ridx % 2][:, 0:hn],
                                                        in1=cbufs[e % 2][:, 0:hn], op=ALU.mult)
                                        v.drain()
                                        v.nop().then_inc(s_dve.h, 1)
                                    v.nop().then_inc(s_cfree.h, 1)
                                for do in range(8):
                                    v.wait_ge(s_pe.h, pe_ye(ei, hf, do))
                                    gidx = ei * 32 + hf * 16 + 8 + do
                                    bank = pA if gidx % 2 == 0 else pB
                                    v.tensor_tensor(out=R[:, do, cl0 + ho: cl0 + ho + hn],
                                                    in0=bank[:, 0:hn],
                                                    in1=R[:, do, cl0 + ho: cl0 + ho + hn], op=ALU.add)
                                    v.drain()
                                    v.nop().then_inc(s_dve.h, 1)
                    s_dve.n += NE * nh * 8 + (NE + 1) * nh * 8
                    s_cfree.n += NE * nh

        def phase_attn():
            uid[0] += 1
            u = uid[0]
            with ExitStack() as ps:
                pQ = [[ps.enter_context(nc.psum_tensor(f"at_S{i}{k}{u}", [P, 512], F32))
                       for k in range(2)] for i in range(2)]
                pT2 = [ps.enter_context(nc.psum_tensor(f"at_T{i}{u}", [P, 8, P], BF16)) for i in range(2)]
                pC2 = [ps.enter_context(nc.psum_tensor(f"at_C{i}{u}", [P, 2, P], F32)) for i in range(2)]
                pSa = pQ[0][0].ap()
                pSb = pQ[0][1].ap()
                wA = ps.enter_context(nc.sbuf_tensor(f"at_wA{u}", [P, DC, D], BF16))
                wB = ps.enter_context(nc.sbuf_tensor(f"at_wB{u}", [P, DC, D], BF16))
                kTb = ps.enter_context(nc.sbuf_tensor(f"at_kT{u}", [P, DC, 640], BF16))
                vbuf = ps.enter_context(nc.sbuf_tensor(f"at_v{u}", [P, 5, D], BF16))
                qTb = ps.enter_context(nc.sbuf_tensor(f"at_qT{u}", [P, DC, 512], BF16))
                ctxT = ps.enter_context(nc.sbuf_tensor(f"at_cT{u}", [P, DC, 512], BF16))
                esc = [ps.enter_context(nc.sbuf_tensor(f"at_es{i}{u}", [P, 4, 256], BF16)) for i in range(3)]
                nmx2 = [ps.enter_context(nc.sbuf_tensor(f"at_n2{i}{u}", [P, 2], F32)) for i in range(2)]
                atb = [ps.enter_context(nc.sbuf_tensor(f"at_ab{i}{u}", [P, 4, 256], BF16)) for i in range(2)]
                atT = [ps.enter_context(nc.sbuf_tensor(f"at_aT{i}{u}", [P, 8, P], BF16)) for i in range(2)]
                nmx = [ps.enter_context(nc.sbuf_tensor(f"at_nm{i}{u}", [P, 1], F32)) for i in range(2)]
                sro = [ps.enter_context(nc.sbuf_tensor(f"at_sr{i}{u}", [P, 4], F32)) for i in range(2)]
                rec = [ps.enter_context(nc.sbuf_tensor(f"at_rc{i}{u}", [P, 4], F32)) for i in range(2)]

                prev_e3 = [None]
                prev_e4 = [None]
                prev_pe_s_last = [None]
                prev_pe_c_last = [None]
                prev_dve_out = [None]
                qblk = nc.Block()
                blk = qblk.__enter__()
                for qh in range(aq):
                    kc0 = 512 * qh
                    qc0 = HALO + 512 * qh
                    awt = []
                    for mi in range(4):
                        c = s_wm[mi % 2]
                        c.n += 16
                        awt.append(c.n)
                    pe0, act0, dve0 = s_pe.n, s_act.n, s_dve.n
                    ktl = tiles(640, 512)

                    def prj_free(gi, act0=act0, dve0=dve0):
                        if gi < 16:
                            return (s_act.h, act0 + gi + 1)
                        if gi < 26:
                            return (s_dve.h, dve0 + (gi - 16) + 1)
                        return (s_act.h, act0 + 16 + (gi - 26) + 1)

                    if True:
                        @blk.gpsimd
                        def _(g):
                            waits = [prev_e3[0], prev_e4[0], pe0 + 16, pe0 + 26]
                            for mi, src in enumerate((WkT, WvT, WqT, WoT)):
                                if waits[mi] is not None:
                                    g.wait_ge(s_pe.h, waits[mi])
                                g.dma_start((wA if mi % 2 == 0 else wB)[:, :, :],
                                            src.rearrange("(c p) n -> p c n", p=P)).then_inc(s_wm[mi % 2].h, 16)
                            g.wait_ge(s_wm[0].h, awt[2])
                            g.wait_ge(s_wm[1].h, awt[1])

                        @blk.tensor
                        def _(t):
                            gi = 0
                            t.wait_ge(s_wm[0].h, awt[0])
                            if prev_dve_out[0] is not None:
                                t.wait_ge(s_dve.h, prev_dve_out[0])
                            for do in range(8):
                                for (to, tn) in ktl:
                                    bank = pSa if gi % 2 == 0 else pSb
                                    if gi >= 2:
                                        sh, cn = prj_free(gi - 2)
                                        t.wait_ge(sh, cn)
                                    for c in range(DC):
                                        mm = t.matmul(bank[:, 0:tn], wA[:, c, do * P:(do + 1) * P],
                                                      zT[:, c, kc0 + to: kc0 + to + tn],
                                                      start=(c == 0), stop=(c == DC - 1))
                                        if c == DC - 1:
                                            mm.then_inc(s_pe.h, 1)
                                    gi += 1
                            t.wait_ge(s_wm[1].h, awt[1])
                            for kc in range(5):
                                for nt in range(2):
                                    bank = pSa if gi % 2 == 0 else pSb
                                    if gi >= 2:
                                        sh, cn = prj_free(gi - 2)
                                        t.wait_ge(sh, cn)
                                    for c in range(DC):
                                        mm = t.matmul(bank[:, 0:512],
                                                      zT[:, c, kc0 + kc * P: kc0 + (kc + 1) * P],
                                                      wB[:, c, nt * 512:(nt + 1) * 512],
                                                      start=(c == 0), stop=(c == DC - 1))
                                        if c == DC - 1:
                                            mm.then_inc(s_pe.h, 1)
                                    gi += 1
                            t.wait_ge(s_wm[0].h, awt[2])
                            for do in range(8):
                                bank = pSa if gi % 2 == 0 else pSb
                                if gi >= 2:
                                    sh, cn = prj_free(gi - 2)
                                    t.wait_ge(sh, cn)
                                for c in range(DC):
                                    mm = t.matmul(bank[:, 0:512], wA[:, c, do * P:(do + 1) * P],
                                                  zT[:, c, qc0: qc0 + 512],
                                                  start=(c == 0), stop=(c == DC - 1))
                                    if c == DC - 1:
                                        mm.then_inc(s_pe.h, 1)
                                gi += 1
                        s_pe.n += 34
                        prev_e3[0] = pe0 + 34

                        @blk.scalar
                        def _(s):
                            if prev_pe_s_last[0] is not None:
                                s.wait_ge(s_pe.h, prev_pe_s_last[0])
                            gi = 0
                            for do in range(8):
                                for (to, tn) in ktl:
                                    bank = pSa if gi % 2 == 0 else pSb
                                    s.wait_ge(s_pe.h, pe0 + gi + 1)
                                    s.activation(kTb[:, do, to:to + tn], bank[:, 0:tn], AF.Identity,
                                                 bias=bkc[:, do:do + 1]).then_inc(s_act.h, 1)
                                    gi += 1
                            gi = 26
                            for do in range(8):
                                bank = pSa if gi % 2 == 0 else pSb
                                s.wait_ge(s_pe.h, pe0 + gi + 1)
                                s.activation(qTb[:, do, 0:512], bank[:, 0:512], AF.Identity,
                                             bias=bqc[:, do:do + 1], scale=cst[:, 2:3]).then_inc(s_act.h, 1)
                                gi += 1
                        s_act.n += 24

                        @blk.vector
                        def _(v):
                            if prev_pe_c_last[0] is not None:
                                v.wait_ge(s_pe.h, prev_pe_c_last[0])
                            gi = 16
                            for kc in range(5):
                                for nt in range(2):
                                    bank = pSa if gi % 2 == 0 else pSb
                                    v.wait_ge(s_pe.h, pe0 + gi + 1)
                                    v.tensor_tensor(out=vbuf[:, kc, nt * 512:(nt + 1) * 512],
                                                    in0=bank[:, 0:512],
                                                    in1=bvb[:, nt * 512:(nt + 1) * 512], op=ALU.add)
                                    v.drain()
                                    v.nop().then_inc(s_dve.h, 1)
                                    gi += 1
                        s_dve.n += 10

                    pe1, act1, dve1, gp1 = s_pe.n, s_act.n, s_dve.n, s_gp.n
                    NIT = anit

                    def pe_s(j):
                        return pe1 + 3 * j + 1

                    def pe_t(j):
                        return pe1 + 3 * (j + 3) + 2

                    def pe_c(j):
                        return pe1 + 3 * (j + 4) + 3

                    def d1(j):
                        return dve1 + 2 * j + 1

                    def d2(j):
                        return dve1 + 2 * (j + 1) + 2

                    def g_n(j):
                        return gp1 + j + 3

                    def a2(j):
                        return act1 + 3 * (j + 3) + 1

                    def a3(j):
                        return act1 + 3 * (j + 5) + 2

                    def a1(j):
                        return act1 + 3 * j + 3

                    NC2 = NIT + 5
                    act1v, dve1v = s_act.n, s_dve.n

                    if True:
                        @blk.tensor
                        def _(t):
                            t.wait_ge(s_act.h, act1v)
                            t.wait_ge(s_dve.h, dve1v)
                            for i in range(NC2):
                                if 0 <= i < NIT:
                                    j = i
                                    pl, hq = divmod(j, 4)
                                    p_glob = 4 * qh + pl
                                    midx = 0 if p_glob == 0 else (2 if p_glob == 15 else 1)
                                    if j >= 2:
                                        t.wait_ge(s_act.h, a1(j - 2))

                                    def score_mm(e, stop):
                                        h = 4 * hq + e
                                        hp, dc = h % 2, h // 2
                                        return t.matmul(pQ[j % 2][e % 2][:, (e // 2) * 256:(e // 2 + 1) * 256],
                                                        qTb[hp * HD:(hp + 1) * HD, dc, pl * P:(pl + 1) * P],
                                                        kTb[hp * HD:(hp + 1) * HD, dc, pl * P: pl * P + 256],
                                                        start=False, stop=stop)
                                    t.matmul(pQ[j % 2][0][:, :], identb[:, :],
                                             maskb[:, midx, 0:512], start=True, stop=False)
                                    score_mm(0, False)
                                    score_mm(2, True)
                                    t.matmul(pQ[j % 2][1][:, :], identb[:, :],
                                             maskb[:, midx, 512:1024], start=True, stop=False)
                                    score_mm(1, False)
                                    score_mm(3, True).then_inc(s_pe.h, 1)
                                else:
                                    t.drain()
                                    t.nop().then_inc(s_pe.h, 1)
                                if 0 <= i - 3 < NIT:
                                    j = i - 3
                                    t.wait_ge(s_gp.h, g_n(j))
                                    if j >= 2:
                                        t.wait_ge(s_act.h, a2(j - 2))
                                    pT = pT2[j % 2]
                                    for sidx in range(4):
                                        t.transpose(pT[:, 2 * sidx, :], atb[j % 2][:, sidx, 0:128], identb[:, :])
                                        mm = t.transpose(pT[:, 2 * sidx + 1, :], atb[j % 2][:, sidx, 128:256], identb[:, :])
                                        if sidx == 3:
                                            mm.then_inc(s_pe.h, 1)
                                else:
                                    t.drain()
                                    t.nop().then_inc(s_pe.h, 1)
                                if 0 <= i - 4 < NIT:
                                    j = i - 4
                                    pl, hq = divmod(j, 4)
                                    t.wait_ge(s_act.h, a2(j))
                                    if j >= 2:
                                        t.wait_ge(s_act.h, a3(j - 2))
                                    pC = pC2[j % 2]
                                    for sidx, e in enumerate((0, 2, 1, 3)):
                                        h = 4 * hq + e
                                        hp = h % 2
                                        t.matmul(pC[hp * HD:(hp + 1) * HD, e // 2, :],
                                                 vbuf[:, pl, h * HD:(h + 1) * HD],
                                                 atT[j % 2][:, 2 * sidx, :], start=True, stop=False)
                                        mm = t.matmul(pC[hp * HD:(hp + 1) * HD, e // 2, :],
                                                      vbuf[:, pl + 1, h * HD:(h + 1) * HD],
                                                      atT[j % 2][:, 2 * sidx + 1, :], start=False, stop=True)
                                        if sidx == 3:
                                            mm.then_inc(s_pe.h, 1)
                                else:
                                    t.drain()
                                    t.nop().then_inc(s_pe.h, 1)
                        s_pe.n += 3 * NC2

                        @blk.vector
                        def _(v):
                            for i in range(NC2):
                                if 0 <= i < NIT:
                                    j = i
                                    v.wait_ge(s_pe.h, pe_s(j))
                                    if j >= 2:
                                        v.wait_ge(s_act.h, a1(j - 2))
                                    v.tensor_reduce(out=nmx2[j % 2][:, 0:1], in_=pQ[j % 2][0][:, :],
                                                    axis=AX.X, op=ALU.max, negate=True)
                                    mm = v.tensor_reduce(out=nmx2[j % 2][:, 1:2], in_=pQ[j % 2][1][:, :],
                                                         axis=AX.X, op=ALU.max, negate=True)
                                    mm.then_inc(s_dve.h, 1)
                                else:
                                    v.drain()
                                    v.nop().then_inc(s_dve.h, 1)
                                if 0 <= i - 1 < NIT:
                                    j = i - 1
                                    v.wait_ge(s_act.h, a1(j))
                                    if j >= 2:
                                        v.wait_ge(s_gp.h, g_n(j - 2))
                                    v.tensor_reduce(out=sro[j % 2][:, 0:2], in_=esc[j % 3][:, 0:2, :],
                                                    axis=AX.X, op=ALU.add)
                                    v.tensor_reduce(out=sro[j % 2][:, 2:4], in_=esc[j % 3][:, 2:4, :],
                                                    axis=AX.X, op=ALU.add)
                                    v.drain()
                                    v.reciprocal(rec[j % 2][:, :], sro[j % 2][:, :])
                                    v.drain()
                                    v.nop().then_inc(s_dve.h, 1)
                                else:
                                    v.drain()
                                    v.nop().then_inc(s_dve.h, 1)
                        s_dve.n += 2 * NC2

                        @blk.gpsimd
                        def _(g):
                            for i in range(NC2):
                                if 0 <= i - 2 < NIT:
                                    j = i - 2
                                    g.wait_ge(s_dve.h, d2(j))
                                    if j >= 2:
                                        g.wait_ge(s_pe.h, pe_t(j - 2))
                                    b_ap = rec[j % 2][:, :].unsqueeze(2).broadcast_to((P, 4, 256))
                                    g.tensor_tensor(out=atb[j % 2][:, :, :], in0=esc[j % 3][:, :, :],
                                                    in1=b_ap, op=ALU.mult)
                                    g.drain()
                                    g.nop().then_inc(s_gp.h, 1)
                                else:
                                    g.drain()
                                    g.nop().then_inc(s_gp.h, 1)
                        s_gp.n += NC2

                        @blk.scalar
                        def _(s):
                            for i in range(NC2):
                                if 0 <= i - 3 < NIT:
                                    j = i - 3
                                    s.wait_ge(s_pe.h, pe_t(j))
                                    if j >= 2:
                                        s.wait_ge(s_pe.h, pe_c(j - 2))
                                    s.copy(atT[j % 2][:, :, :], pT2[j % 2][:, :, :]).then_inc(s_act.h, 1)
                                else:
                                    s.drain()
                                    s.nop().then_inc(s_act.h, 1)
                                if 0 <= i - 5 < NIT:
                                    j = i - 5
                                    pl, hq = divmod(j, 4)
                                    s.wait_ge(s_pe.h, pe_c(j))
                                    s.copy(ctxT[:, 2 * hq:2 * hq + 2, pl * P:(pl + 1) * P],
                                           pC2[j % 2][:, :, :]).then_inc(s_act.h, 1)
                                else:
                                    s.drain()
                                    s.nop().then_inc(s_act.h, 1)
                                if 0 <= i < NIT:
                                    j = i
                                    s.wait_ge(s_dve.h, d1(j))
                                    if j >= 3:
                                        s.wait_ge(s_gp.h, g_n(j - 3))
                                        s.wait_ge(s_dve.h, d2(j - 3))
                                    s.activation(esc[j % 3][:, 0:2, :], pQ[j % 2][0][:, :],
                                                 AF.Exp, bias=nmx2[j % 2][:, 0:1])
                                    s.activation(esc[j % 3][:, 2:4, :], pQ[j % 2][1][:, :],
                                                 AF.Exp, bias=nmx2[j % 2][:, 1:2]).then_inc(s_act.h, 1)
                                else:
                                    s.drain()
                                    s.nop().then_inc(s_act.h, 1)
                        s_act.n += 3 * NC2

                    prev_pe_s_last[0] = pe_s(NIT - 1)
                    prev_pe_c_last[0] = pe_c(NIT - 1)
                    if not aout:
                        prev_e3[0] = None
                        prev_e4[0] = None
                        continue
                    pe2, dve2 = s_pe.n, s_dve.n
                    a3last = a3(NIT - 1)
                    if True:
                        @blk.tensor
                        def _(t):
                            t.wait_ge(s_wm[1].h, awt[3])
                            t.wait_ge(s_act.h, a3last)
                            for do in range(8):
                                bank = pSa if do % 2 == 0 else pSb
                                if do >= 2:
                                    t.wait_ge(s_dve.h, dve2 + do - 1)
                                for c in range(DC):
                                    mm = t.matmul(bank[:, 0:512], wB[:, c, do * P:(do + 1) * P],
                                                  ctxT[:, c, 0:512], start=(c == 0), stop=(c == DC - 1))
                                    if c == DC - 1:
                                        mm.then_inc(s_pe.h, 1)
                        s_pe.n += 8
                        prev_e4[0] = pe2 + 8

                        @blk.vector
                        def _(v):
                            for do in range(8):
                                bank = pSa if do % 2 == 0 else pSb
                                v.wait_ge(s_pe.h, pe2 + do + 1)
                                v.scalar_tensor_tensor(out=R[:, do, qc0:qc0 + 512],
                                                       in0=bank[:, 0:512], scalar=boc[:, do:do + 1],
                                                       in1=R[:, do, qc0:qc0 + 512],
                                                       op0=ALU.add, op1=ALU.add)
                                v.drain()
                                v.nop().then_inc(s_dve.h, 1)
                        s_dve.n += 8
                    prev_dve_out[0] = s_dve.n
                qblk.__exit__(None, None, None)

        if upto >= 1:
            phase_ln(0, TX)
            if t_z1 is not None:
                tap3(t_z1, zT[:, :, :])
        if upto >= 2:
            with ExitStack() as cs:
                cwf = cs.enter_context(nc.sbuf_tensor("cwf_1", [P, NCH, NE], F32))
                cwx = cs.enter_context(nc.sbuf_tensor("cwx_1", [16, TX], BF16))
                with nc.Block() as blk:
                    @blk.gpsimd
                    def _(g):
                        g.dma_start(cwx[NE:16, :], onesr[:1, :]).then_inc(s_in.h, 16)
                        s_in.n += 16
                        g.wait_ge(s_in.h, s_in.n)
                if upto >= 3:
                    phase_ffn(0, TX, [(0, 1152), (1152, 1024)], cwf, cwx)
                    if t_cw1 is not None:
                        tap3(t_cw1, cwf[:, :, :])
            if upto >= 3 and t_y1 is not None:
                tap3(t_y1, R[:, :, :])
        if upto >= 4:
            phase_ln(0, TX)
            if t_z2 is not None:
                tap3(t_z2, zT[:, :, :])
        if upto >= 5:
            phase_attn()
            if t_y2 is not None:
                tap3(t_y2, R[:, :, :])
        if upto >= 6:
            phase_ln(HALO, TOWN)
            if t_z3 is not None:
                tap3(t_z3, zT[:, :, :])
            with ExitStack() as cs:
                cwf = cs.enter_context(nc.sbuf_tensor("cwf_2", [P, NCH, NE], F32))
                cwx = cs.enter_context(nc.sbuf_tensor("cwx_2", [16, TX], BF16))
                with nc.Block() as blk:
                    @blk.gpsimd
                    def _(g):
                        g.dma_start(cwx[NE:16, :], onesr[:1, :]).then_inc(s_in.h, 16)
                        s_in.n += 16
                        g.wait_ge(s_in.h, s_in.n)
                phase_ffn(HALO, TOWN, [(0, 1024), (1024, 1024)], cwf, cwx, stream_out=True)
                if t_cw3 is not None:
                    tap3(t_cw3, cwf[:, :, :])

        if upto < 6:
            with nc.Block() as blk:
                @blk.sync
                def _(sy):
                    sy.dma_start(outT.rearrange("(c p) t -> p c t", p=P),
                                 R[:, :, HALO:HALO + TOWN]).then_inc(s_hw.h, 16)
                    s_hw.n += 16
                    sy.wait_ge(s_hw.h, s_hw.n)

    return nc, tapnames


def make_masks(core):
    r = np.arange(P)[:, None]
    jj = np.arange(256)[None, :]
    band = (jj - r >= 32) & (jj - r <= 96)
    mi = np.where(band, 0.0, NEG).astype(np.float32)
    ma = np.where(band & (jj >= 64), 0.0, NEG).astype(np.float32) if core == 0 else mi
    mc = np.where(band & (jj < 192), 0.0, NEG).astype(np.float32) if core == NCORE - 1 else mi
    st = np.stack([ma, mi, mc]).astype(np.float32)
    return np.ascontiguousarray(np.tile(st, (1, 1, 4)))


def prep_inputs(L0_attn_x, L0_attn_Wq, L0_attn_bq, L0_attn_Wk, L0_attn_bk,
                L0_attn_Wv, L0_attn_bv, L0_attn_Wo, L0_attn_bo,
                L0_ffn_router_W, L0_ffn_router_b, L0_ffn_sW1, L0_ffn_sb1,
                L0_ffn_sW2, L0_ffn_sb2, L0_ffn_eW1, L0_ffn_eb1,
                L0_ffn_eW2, L0_ffn_eb2):
    x = np.asarray(L0_attn_x, dtype=np.float32).reshape(SEQ, D)
    xTf = np.ascontiguousarray(x.T)
    xTpad = np.zeros((D, SEQ + 2 * HALO), np.float32)
    xTpad[:, HALO:HALO + SEQ] = xTf
    common = {
        "WqT": np.ascontiguousarray(np.asarray(L0_attn_Wq, np.float32).T),
        "WkT": np.ascontiguousarray(np.asarray(L0_attn_Wk, np.float32).T),
        "WvT": np.ascontiguousarray(np.asarray(L0_attn_Wv, np.float32).T),
        "WoT": np.ascontiguousarray(np.asarray(L0_attn_Wo, np.float32).T),
        "bq": np.ascontiguousarray(np.asarray(L0_attn_bq, np.float32).reshape(DC, P).T),
        "bk": np.ascontiguousarray(np.asarray(L0_attn_bk, np.float32).reshape(DC, P).T),
        "bv": np.asarray(L0_attn_bv, np.float32),
        "bo": np.ascontiguousarray(np.asarray(L0_attn_bo, np.float32).reshape(DC, P).T),
        "rW": np.asarray(L0_ffn_router_W, np.float32),
        "rbias": np.asarray(L0_ffn_router_b, np.float32),
        "sW1": np.asarray(L0_ffn_sW1, np.float32),
        "sW2": np.asarray(L0_ffn_sW2, np.float32),
        "sb1": np.ascontiguousarray(np.asarray(L0_ffn_sb1, np.float32).reshape(DC, P).T),
        "sb2": np.asarray(L0_ffn_sb2, np.float32),
        "eW1": np.asarray(L0_ffn_eW1, np.float32),
        "eW2": np.asarray(L0_ffn_eW2, np.float32),
        "eb1T": np.ascontiguousarray(np.asarray(L0_ffn_eb1, np.float32).T),
        "eb2": np.asarray(L0_ffn_eb2, np.float32),
    }
    in_maps = []
    for c in range(NCORE):
        m = dict(common)
        st = c * TOWN
        m["xT"] = np.ascontiguousarray(xTpad[:, st:st + TX])
        m["masks"] = make_masks(c)
        in_maps.append(m)
    return in_maps


def kernel(**inputs):
    in_maps = prep_inputs(**inputs)
    nc, _ = build_program()
    res = run_bass_kernel_spmd(nc, in_maps, core_ids=list(range(NCORE)))
    parts = [np.asarray(r["outT"]) for r in res.results]
    full = np.concatenate(parts, axis=1)
    return np.ascontiguousarray(full.T).reshape(1, SEQ, D).astype(np.float32)


# version 25
# speedup vs baseline: 1.2007x; 1.0003x over previous
import numpy as np
from contextlib import ExitStack

import concourse.bass as bass
import concourse.mybir as mybir
from concourse.bass_utils import run_bass_kernel_spmd

F32 = mybir.dt.float32
BF16 = mybir.dt.bfloat16
AF = mybir.ActivationFunctionType
ALU = mybir.AluOpType
AX = mybir.AxisListType

P = 128
D = 1024
DC = D // P
TOWN = 2048
HALO = 64
TX = TOWN + 2 * HALO
NCH = TX // P
NE = 15
NH = 16
HD = 64
EPS = 1e-5
NEG = -1.0e9
NCORE = 8
SEQ = 16384


def tiles(width, maxn=512):
    out = []
    o = 0
    while o < width:
        n = min(maxn, width - o)
        out.append((o, n))
        o += n
    return out


class Ctr:
    def __init__(self, es, nc, name):
        self.h = es.enter_context(nc.semaphore(name))
        self.n = 0


def build_program(upto=99, taps=(), aq=4, anit=16, aout=True, acut=99):
    nc = bass.Bass("TRN2", target_bir_lowering=False)
    taps = set(taps)
    tapnames = []
    uid = [0]

    xT = nc.dram_tensor("xT", [D, TX], F32, kind="ExternalInput").ap()
    masks = nc.dram_tensor("masks", [3, P, 1024], F32, kind="ExternalInput").ap()
    WqT = nc.dram_tensor("WqT", [D, D], F32, kind="ExternalInput").ap()
    WkT = nc.dram_tensor("WkT", [D, D], F32, kind="ExternalInput").ap()
    WvT = nc.dram_tensor("WvT", [D, D], F32, kind="ExternalInput").ap()
    WoT = nc.dram_tensor("WoT", [D, D], F32, kind="ExternalInput").ap()
    bq = nc.dram_tensor("bq", [P, DC], F32, kind="ExternalInput").ap()
    bk = nc.dram_tensor("bk", [P, DC], F32, kind="ExternalInput").ap()
    bv = nc.dram_tensor("bv", [D], F32, kind="ExternalInput").ap()
    bo = nc.dram_tensor("bo", [P, DC], F32, kind="ExternalInput").ap()
    rW = nc.dram_tensor("rW", [D, NE], F32, kind="ExternalInput").ap()
    rbias = nc.dram_tensor("rbias", [NE], F32, kind="ExternalInput").ap()
    sW1 = nc.dram_tensor("sW1", [D, D], F32, kind="ExternalInput").ap()
    sW2 = nc.dram_tensor("sW2", [D, D], F32, kind="ExternalInput").ap()
    sb1 = nc.dram_tensor("sb1", [P, DC], F32, kind="ExternalInput").ap()
    sb2 = nc.dram_tensor("sb2", [D], F32, kind="ExternalInput").ap()
    eW1 = nc.dram_tensor("eW1", [NE, D, D], F32, kind="ExternalInput").ap()
    eW2 = nc.dram_tensor("eW2", [NE, D, D], F32, kind="ExternalInput").ap()
    eb1T = nc.dram_tensor("eb1T", [D, NE], F32, kind="ExternalInput").ap()
    eb2 = nc.dram_tensor("eb2", [NE, D], F32, kind="ExternalInput").ap()

    outT = nc.dram_tensor("outT", [D, TOWN], F32, kind="ExternalOutput").ap()

    scr_rows = nc.dram_tensor("scr_rows", [2, TX], F32).ap()
    scr_cwt = nc.dram_tensor("scr_cwt", [NE, TX], BF16).ap()

    def mktap(name, shape, dtype=F32):
        if name in taps:
            tapnames.append(name)
            return nc.dram_tensor("tap_" + name, shape, dtype, kind="ExternalOutput").ap()
        return None

    t_z1 = mktap("z1", [P, DC, TX], BF16)
    t_cw1 = mktap("cw1", [P, NCH, NE], F32)
    t_y1 = mktap("y1", [P, DC, TX], F32)
    t_z2 = mktap("z2", [P, DC, TX], BF16)
    t_kT = mktap("kT", [P, DC, 640], BF16)
    t_v = mktap("v", [P, 5, D], BF16)
    t_qT = mktap("qT", [P, DC, 512], BF16)
    t_y2 = mktap("y2", [P, DC, TX], F32)
    t_z3 = mktap("z3", [P, DC, TX], BF16)
    t_cw3 = mktap("cw3", [P, NCH, NE], F32)

    es = ExitStack()
    with es:
        R = es.enter_context(nc.sbuf_tensor("R", [P, DC, TX], F32))
        zT = es.enter_context(nc.sbuf_tensor("zT", [P, DC, TX], BF16))
        bqc = es.enter_context(nc.sbuf_tensor("bqc", [P, DC], F32))
        bkc = es.enter_context(nc.sbuf_tensor("bkc", [P, DC], F32))
        boc = es.enter_context(nc.sbuf_tensor("boc", [P, DC], F32))
        bvb = es.enter_context(nc.sbuf_tensor("bvb", [P, D], BF16))
        sb1c = es.enter_context(nc.sbuf_tensor("sb1c", [P, DC], F32))
        eb1c = es.enter_context(nc.sbuf_tensor("eb1c", [P, DC, NE], F32))
        eb2s = es.enter_context(nc.sbuf_tensor("eb2s", [16, D], BF16))
        rWb = es.enter_context(nc.sbuf_tensor("rWb", [P, DC, NE], BF16))
        rbb = es.enter_context(nc.sbuf_tensor("rbb", [P, NE], F32))
        maskb = es.enter_context(nc.sbuf_tensor("maskb", [P, 3, 1024], BF16))
        identf = es.enter_context(nc.sbuf_tensor("identf", [P, P], F32))
        identb = es.enter_context(nc.sbuf_tensor("identb", [P, P], BF16))
        ones_col = es.enter_context(nc.sbuf_tensor("ones_col", [P, 1], BF16))
        ncol = es.enter_context(nc.sbuf_tensor("ncol", [P, 1], F32))
        qcol = es.enter_context(nc.sbuf_tensor("qcol", [P, 1], BF16))
        iota15 = es.enter_context(nc.sbuf_tensor("iota15", [P, NE], F32))
        cwf = es.enter_context(nc.sbuf_tensor("cwf", [P, NCH, NE], F32))
        cst = es.enter_context(nc.sbuf_tensor("cst", [P, 4], F32))
        onesr = es.enter_context(nc.sbuf_tensor("onesr", [1, TX], BF16))
        cwx = es.enter_context(nc.sbuf_tensor("cwx", [16, TX], BF16))

        s_in = Ctr(es, nc, "s_in")
        s_wm = [Ctr(es, nc, f"s_wm{i}") for i in range(3)]
        s_hw = Ctr(es, nc, "s_hw")
        s_cb = [Ctr(es, nc, f"s_cb{i}") for i in range(2)]
        s_pe = Ctr(es, nc, "s_pe")
        s_act = Ctr(es, nc, "s_act")
        s_dve = Ctr(es, nc, "s_dve")
        s_cfree = Ctr(es, nc, "s_cfree")
        s_gp = Ctr(es, nc, "s_gp")
        s_rpe = Ctr(es, nc, "s_rpe")
        s_rdv = Ctr(es, nc, "s_rdv")
        s_rac = Ctr(es, nc, "s_rac")

        def tap3(tapap, sbap):
            with nc.Block() as blk:
                @blk.sync
                def _(sy):
                    sy.dma_start(tapap, sbap).then_inc(s_hw.h, 16)
                    s_hw.n += 16
                    sy.wait_ge(s_hw.h, s_hw.n)

        with ExitStack() as ps:
            ioA = ps.enter_context(nc.sbuf_tensor("ioA", [P, P], F32))
            ioB = ps.enter_context(nc.sbuf_tensor("ioB", [P, P], F32))
            sb2row = ps.enter_context(nc.sbuf_tensor("sb2row", [1, D], F32))
            sb2h = ps.enter_context(nc.sbuf_tensor("sb2h", [1, D], BF16))
            with nc.Block() as blk:
                @blk.gpsimd
                def _(g):
                    def d(out, in_):
                        g.dma_start(out, in_).then_inc(s_in.h, 16)
                        s_in.n += 16
                    d(R[:, 3:5, :], xT.rearrange("(c p) t -> p c t", p=P)[:, 3:5, :])
                    d(maskb[:, :, :], masks.rearrange("m p t -> p m t"))
                    d(sb2row[:1, :], sb2.rearrange("(a d) -> a d", a=1))
                    d(eb2s[0:NE, :], eb2[:, :])
                    d(bvb[:, :], bv.partition_broadcast(P))
                    d(rWb[:, :, :], rW.rearrange("(c p) e -> p c e", p=P))
                    g.iota(iota15[:, :], [[1, NE]], channel_multiplier=0,
                           allow_small_or_imprecise_dtypes=True)
                    g.iota(ioA[:, :], [[1, P]], channel_multiplier=0,
                           allow_small_or_imprecise_dtypes=True)
                    g.iota(ioB[:, :], [[0, P]], channel_multiplier=1,
                           allow_small_or_imprecise_dtypes=True)
                    g.memset(ones_col[:, :], 1.0)
                    g.memset(ncol[:, :], -1.0 / D)
                    g.memset(qcol[:, :], 1.0 / D)
                    g.memset(cst[:, 3:4], -0.5)
                    g.memset(onesr[:1, :], 1.0)
                    g.memset(cst[:, 0:1], EPS)
                    g.memset(cst[:, 1:2], 0.5)
                    g.memset(cst[:, 2:3], 0.125)
                    g.wait_ge(s_in.h, s_in.n)

                @blk.sync
                def _(sy):
                    def d2_(out, in_):
                        sy.dma_start(out, in_).then_inc(s_hw.h, 16)
                        s_hw.n += 16
                    d2_(R[:, 0:3, :], xT.rearrange("(c p) t -> p c t", p=P)[:, 0:3, :])
                    d2_(bqc[:, :], bq[:, :])
                    d2_(bkc[:, :], bk[:, :])
                    d2_(boc[:, :], bo[:, :])
                    d2_(sb1c[:, :], sb1[:, :])
                    d2_(eb1c[:, :, :], eb1T.rearrange("(c p) e -> p c e", p=P))
                    d2_(rbb[:, :], rbias.partition_broadcast(P))
                    sy.wait_ge(s_hw.h, s_hw.n + 16)
                    s_hw.n += 16

                @blk.scalar
                def _(s):
                    s.dma_start(R[:, 5:8, :],
                                xT.rearrange("(c p) t -> p c t", p=P)[:, 5:8, :]).then_inc(s_hw.h, 16)
            with nc.Block() as blk:
                @blk.vector
                def _(v):
                    v.tensor_tensor(out=identf[:, :], in0=ioA[:, :], in1=ioB[:, :], op=ALU.is_equal)
                    v.drain()
                    v.tensor_copy(identb[:, :], identf[:, :])
                    v.tensor_scalar_mul(bqc[:, :], bqc[:, :], 0.125)
                    v.tensor_scalar_mul(sb1c[:, :], sb1c[:, :], 0.5)
                @blk.scalar
                def _(s):
                    s.activation(sb2h[:1, :], sb2row[:1, :], AF.Copy, scale=cst[0:1, 1:2]).then_inc(s_act.h, 1)
                    s_act.n += 1

            with nc.Block() as blk:
                @blk.gpsimd
                def _(g):
                    g.wait_ge(s_act.h, s_act.n)
                    g.dma_start(eb2s[NE:16, :], sb2h[:1, :]).then_inc(s_in.h, 16)
                    s_in.n += 16
                    g.wait_ge(s_in.h, s_in.n)

        def phase_ln(cl0, cln):
            uid[0] += 1
            u = uid[0]
            hw_w = cln // 2
            NDV = 5
            with ExitStack() as ps:
                ssum = ps.enter_context(nc.sbuf_tensor(f"ln_su{u}", [1, TX], F32))
                ssq = ps.enter_context(nc.sbuf_tensor(f"ln_sq{u}", [1, TX], F32))
                svar = ps.enter_context(nc.sbuf_tensor(f"ln_va{u}", [1, TX], F32))
                slog = ps.enter_context(nc.sbuf_tensor(f"ln_lg{u}", [1, TX], F32))
                srstd = ps.enter_context(nc.sbuf_tensor(f"ln_sr{u}", [1, TX], F32))
                mB = ps.enter_context(nc.sbuf_tensor(f"ln_mb{u}", [P, TX], F32))
                rB = ps.enter_context(nc.sbuf_tensor(f"ln_rb{u}", [P, TX], F32))
                gsq = ps.enter_context(nc.sbuf_tensor(f"ln_gsq{u}", [P, DC, cln // 2], BF16))
                tl = tiles(hw_w, 512)
                nt = len(tl)
                psA = [ps.enter_context(nc.psum_tensor(f"ln_psA{u}_{i}", [1, 512], F32))
                       for i in range(nt)]
                psB = [ps.enter_context(nc.psum_tensor(f"ln_psB{u}_{i}", [1, 512], F32))
                       for i in range(nt)]
                act0, pe0, dve0 = s_act.n, s_pe.n, s_dve.n
                hw0n = s_hw.n
                with nc.Block() as blk:
                    @blk.scalar
                    def _(s):
                        for hw in range(2):
                            c0 = cl0 + hw * hw_w
                            if hw == 1:
                                s.wait_ge(s_pe.h, pe0 + 2 * nt)
                            for c in range(DC):
                                mm = s.activation(gsq[:, c, 0:hw_w],
                                                  R[:, c, c0:c0 + hw_w], AF.Square)
                                if c == DC - 1:
                                    mm.then_inc(s_act.h, 1)
                        for hw in range(2):
                            o0 = hw * hw_w
                            s.wait_ge(s_dve.h, dve0 + 3 * (hw + 1))
                            s.activation(slog[:1, o0:o0 + hw_w], svar[:1, o0:o0 + hw_w],
                                         AF.Ln, bias=cst[0:1, 0:1])
                            s.drain()
                            s.activation(srstd[:1, o0:o0 + hw_w], slog[:1, o0:o0 + hw_w],
                                         AF.Exp, scale=cst[0:1, 3:4]).then_inc(s_act.h, 1)
                    s_act.n += 4

                    @blk.tensor
                    def _(t):
                        for hw in range(2):
                            c0 = cl0 + hw * hw_w
                            o0 = hw * hw_w
                            for ti, (to, tn) in enumerate(tl):
                                if hw == 1 and ti == 0:
                                    t.wait_ge(s_dve.h, dve0 + 1)
                                for c in range(DC):
                                    st, sp = (c == 0), (c == DC - 1)
                                    mm = t.matmul(psA[ti][0:1, 0:tn], ncol[:, :],
                                                  R[:, c, c0 + to: c0 + to + tn],
                                                  start=st, stop=sp)
                                    if sp:
                                        mm.then_inc(s_pe.h, 1)
                            t.wait_ge(s_act.h, act0 + hw + 1)
                            for ti, (to, tn) in enumerate(tl):
                                if hw == 1 and ti == 0:
                                    t.wait_ge(s_dve.h, dve0 + 2)
                                for c in range(DC):
                                    st, sp = (c == 0), (c == DC - 1)
                                    mm = t.matmul(psB[ti][0:1, 0:tn], qcol[:, :],
                                                  gsq[:, c, to:to + tn],
                                                  start=st, stop=sp)
                                    if sp:
                                        mm.then_inc(s_pe.h, 1)
                    s_pe.n += 4 * nt

                    @blk.vector
                    def _(v):
                        def evacs(hw, sq, ptensors, dst):
                            o0 = hw * hw_w
                            base = pe0 + (2 * hw + sq) * nt
                            for ti, (to, tn) in enumerate(tl):
                                v.wait_ge(s_pe.h, base + ti + 1)
                                v.tensor_copy(dst[:1, o0 + to:o0 + to + tn],
                                              ptensors[ti][0:1, 0:tn])
                            v.drain()
                            v.nop().then_inc(s_dve.h, 1)

                        def rowmath(hw):
                            o0 = hw * hw_w
                            rs = slice(o0, o0 + hw_w)
                            v.tensor_tensor(out=svar[:1, rs], in0=ssum[:1, rs],
                                            in1=ssum[:1, rs], op=ALU.mult)
                            v.drain()
                            v.tensor_tensor(out=svar[:1, rs], in0=ssq[:1, rs],
                                            in1=svar[:1, rs], op=ALU.subtract)
                            v.drain()
                            v.nop().then_inc(s_dve.h, 1)

                        def npass(hw, p2):
                            c0 = cl0 + hw * hw_w
                            o0 = hw * hw_w
                            v.wait_ge(s_hw.h, hw0n + 32 * (2 * hw + p2 + 1))
                            for c in range(NDV):
                                sl = slice(c0, c0 + hw_w)
                                if p2:
                                    v.tensor_tensor(out=zT[:, c, sl], in0=zT[:, c, sl],
                                                    in1=rB[:, o0:o0 + hw_w], op=ALU.mult)
                                else:
                                    v.tensor_tensor(out=zT[:, c, sl], in0=R[:, c, sl],
                                                    in1=mB[:, o0:o0 + hw_w], op=ALU.add)
                            v.drain()

                        evacs(0, 0, psA, ssum)
                        evacs(0, 1, psB, ssq)
                        rowmath(0)
                        npass(0, 0)
                        evacs(1, 0, psA, ssum)
                        evacs(1, 1, psB, ssq)
                        rowmath(1)
                        npass(0, 1)
                        npass(1, 0)
                        npass(1, 1)
                    s_dve.n += 6

                    @blk.sync
                    def _(sy):
                        cnt = hw0n
                        for hw in range(2):
                            o0 = hw * hw_w
                            rs = slice(o0, o0 + hw_w)
                            sy.wait_ge(s_dve.h, dve0 + 1 + 3 * hw)
                            sy.dma_start(scr_rows[0:1, rs], ssum[:1, rs]).then_inc(s_hw.h, 16)
                            cnt += 16
                            sy.wait_ge(s_hw.h, cnt)
                            sy.dma_start(mB[:, rs],
                                         scr_rows[0, rs].partition_broadcast(P)).then_inc(s_hw.h, 16)
                            cnt += 16
                            sy.wait_ge(s_hw.h, cnt)
                            sy.wait_ge(s_act.h, act0 + 3 + hw)
                            sy.dma_start(scr_rows[1:2, rs], srstd[:1, rs]).then_inc(s_hw.h, 16)
                            cnt += 16
                            sy.wait_ge(s_hw.h, cnt)
                            sy.dma_start(rB[:, rs],
                                         scr_rows[1, rs].partition_broadcast(P)).then_inc(s_hw.h, 16)
                            cnt += 16
                            if hw == 0:
                                sy.wait_ge(s_hw.h, cnt)
                        s_hw.n += 128

                    @blk.gpsimd
                    def _(g):
                        for hw in range(2):
                            c0 = cl0 + hw * hw_w
                            o0 = hw * hw_w
                            sl = slice(c0, c0 + hw_w)
                            g.wait_ge(s_hw.h, hw0n + 64 * hw + 32)
                            for c in range(NDV, DC):
                                g.tensor_tensor(out=zT[:, c, sl], in0=R[:, c, sl],
                                                in1=mB[:, o0:o0 + hw_w], op=ALU.add)
                            g.drain()
                            g.wait_ge(s_hw.h, hw0n + 64 * hw + 64)
                            for c in range(NDV, DC):
                                g.tensor_tensor(out=zT[:, c, sl], in0=zT[:, c, sl],
                                                in1=rB[:, o0:o0 + hw_w], op=ALU.mult)
                            g.drain()

        def make_router(cl0, cln, cwf, cwx, ps):
            uid[0] += 1
            u = uid[0]
            nch = cln // P
            nf = nch * NE
            prt = ps.enter_context(nc.psum_tensor(f"rt_p{u}", [P, NE], F32))
            ptw = ps.enter_context(nc.psum_tensor(f"rt_t{u}", [NE, P], F32))
            logt = ps.enter_context(nc.sbuf_tensor(f"rt_lg{u}", [P, NCH, NE], F32))
            probs = ps.enter_context(nc.sbuf_tensor(f"rt_pr{u}", [P, NCH, NE], F32))
            tt1 = ps.enter_context(nc.sbuf_tensor(f"rt_t1{u}", [P, NCH, NE], F32))
            nmx = ps.enter_context(nc.sbuf_tensor(f"rt_nm{u}", [P, 1], F32))
            f17 = ps.enter_context(nc.sbuf_tensor(f"rt_f17{u}", [P, NCH], F32))
            r17 = ps.enter_context(nc.sbuf_tensor(f"rt_r17{u}", [P, NCH], F32))
            iota15m = ps.enter_context(nc.sbuf_tensor(f"rt_iom{u}", [P, NE], F32))
            pe0r, dve0r, act0r = s_rpe.n, s_rdv.n, s_rac.n
            dve1r = dve0r + nch
            dve2r = dve1r + 2
            pe1r = pe0r + nch
            st = {"cwx_ev": dve2r + nch + 1}

            L = logt.ap().rearrange("p a b -> p (a b)")[:, 0:nf].rearrange("p (a b) -> p a b", b=NE)
            PR = probs.ap().rearrange("p a b -> p (a b)")[:, 0:nf].rearrange("p (a b) -> p a b", b=NE)
            T1 = tt1.ap().rearrange("p a b -> p (a b)")[:, 0:nf].rearrange("p (a b) -> p a b", b=NE)
            CW = cwf.ap().rearrange("p a b -> p (a b)")[:, 0:nf].rearrange("p (a b) -> p a b", b=NE)
            Lf = logt.ap().rearrange("p a b -> p (a b)")[:, 0:nf]
            PRf = probs.ap().rearrange("p a b -> p (a b)")[:, 0:nf]
            T1f = tt1.ap().rearrange("p a b -> p (a b)")[:, 0:nf]
            CWf = cwf.ap().rearrange("p a b -> p (a b)")[:, 0:nf]
            io_b = iota15.ap().unsqueeze(1).broadcast_to((P, nch, NE))
            iom_b = iota15m.ap().unsqueeze(1).broadcast_to((P, nch, NE))

            def pe_logits(t):
                for k in range(nch):
                    if k >= 1:
                        t.wait_ge(s_rdv.h, dve0r + k)
                    for c in range(DC):
                        mm = t.matmul(prt[:, :], zT[:, c, cl0 + k * P: cl0 + (k + 1) * P],
                                      rWb[:, c, :], start=(c == 0), stop=(c == DC - 1))
                        if c == DC - 1:
                            mm.then_inc(s_rpe.h, 1)
                s_rpe.n += nch

            def pe_trans(t):
                t.wait_ge(s_rdv.h, dve1r + 2)
                for k in range(nch):
                    if k >= 1:
                        t.wait_ge(s_rdv.h, dve2r + k)
                    t.transpose(ptw[0:NE, :], cwf[:, k, :], identf[:, :]).then_inc(s_rpe.h, 1)
                s_rpe.n += nch

            def dve(v):
                v.tensor_scalar_mul(iota15m[:, :], iota15[:, :], 1.0)
                v.drain()
                v.tensor_scalar_add(out=iota15m[:, :], in0=iota15m[:, :], scalar1=-100.0)
                v.drain()
                for k in range(nch):
                    v.wait_ge(s_rpe.h, pe0r + k + 1)
                    v.tensor_tensor(out=logt[:, k, :], in0=prt[:, :], in1=rbb[:, :], op=ALU.add)
                    v.drain()
                    v.nop().then_inc(s_rdv.h, 1)
                v.tensor_reduce(out=nmx[:, 0:1], in_=Lf, axis=AX.X, op=ALU.max, negate=True)
                v.drain()
                v.nop().then_inc(s_rdv.h, 1)
                v.wait_ge(s_rac.h, act0r + 1)
                v.tensor_reduce(out=f17[:, 0:nch], in_=PR, axis=AX.X, op=ALU.add)
                v.drain()
                v.reciprocal(r17[:, 0:nch], f17[:, 0:nch])
                v.drain()
                v.tensor_tensor(out=PR, in0=PR,
                                in1=r17[:, 0:nch].unsqueeze(2).broadcast_to((P, nch, NE)), op=ALU.mult)
                v.drain()
                v.memset(CWf, 0.0)
                v.drain()
                for it in range(3):
                    v.tensor_reduce(out=f17[:, 0:nch], in_=PR, axis=AX.X, op=ALU.max, negate=True)
                    v.drain()
                    fb = f17[:, 0:nch].unsqueeze(2).broadcast_to((P, nch, NE))
                    v.tensor_tensor(out=T1, in0=PR, in1=fb, op=ALU.add)
                    v.drain()
                    v.tensor_single_scalar(out=T1f, in_=T1f, scalar=0.0, op=ALU.is_equal)
                    v.drain()
                    v.tensor_tensor(out=T1, in0=T1, in1=iom_b, op=ALU.mult)
                    v.drain()
                    v.tensor_scalar_add(out=T1f, in0=T1f, scalar1=100.0)
                    v.drain()
                    v.tensor_reduce(out=f17[:, 0:nch], in_=T1, axis=AX.X, op=ALU.min)
                    v.drain()
                    fb = f17[:, 0:nch].unsqueeze(2).broadcast_to((P, nch, NE))
                    v.tensor_tensor(out=T1, in0=io_b, in1=fb, op=ALU.subtract)
                    v.drain()
                    v.tensor_single_scalar(out=T1f, in_=T1f, scalar=0.0, op=ALU.is_equal)
                    v.drain()
                    v.tensor_reduce(out=r17[:, 0:nch], in_=PR, axis=AX.X, op=ALU.max, negate=True)
                    v.drain()
                    rb2 = r17[:, 0:nch].unsqueeze(2).broadcast_to((P, nch, NE))
                    v.tensor_tensor(out=T1, in0=T1, in1=rb2, op=ALU.mult)
                    v.drain()
                    v.tensor_tensor(out=CW, in0=CW, in1=T1, op=ALU.subtract)
                    v.drain()
                    v.tensor_tensor(out=PR, in0=PR, in1=T1, op=ALU.add)
                    v.drain()
                v.tensor_scalar_mul(CWf, CWf, 0.5)
                v.drain()
                v.nop().then_inc(s_rdv.h, 1)
                for k in range(nch):
                    v.wait_ge(s_rpe.h, pe1r + k + 1)
                    v.tensor_copy(cwx[0:NE, cl0 + k * P: cl0 + (k + 1) * P], ptw[0:NE, :])
                    v.drain()
                    v.nop().then_inc(s_rdv.h, 1)
                v.nop().then_inc(s_rdv.h, 1)
                s_rdv.n += nch + 2 + nch + 1

            def act(sc):
                sc.wait_ge(s_rdv.h, dve1r + 1)
                sc.activation(PRf, Lf, AF.Exp, bias=nmx[:, 0:1]).then_inc(s_rac.h, 1)
                s_rac.n += 1

            def sync(sy):
                sy.wait_ge(s_rdv.h, st["cwx_ev"])
                sy.dma_start(scr_cwt[:, cl0:cl0 + cln], cwx[0:NE, cl0:cl0 + cln]).then_inc(s_hw.h, 16)
                s_hw.n += 16
                sy.wait_ge(s_hw.h, s_hw.n)

            return {"pe_logits": pe_logits, "pe_trans": pe_trans, "dve": dve,
                    "act": act, "sync": sync, "cwx_ev": st["cwx_ev"]}

        def phase_ffn(cl0, cln, halves, cwf, cwx, stream_out=False):
            uid[0] += 1
            u = uid[0]
            nmat = 2 * (NE + 1)
            nh = len(halves)
            with ExitStack() as ps:
                pA = ps.enter_context(nc.psum_tensor(f"ff_pA{u}", [P, 1152], F32))
                pB = ps.enter_context(nc.psum_tensor(f"ff_pB{u}", [P, 1152], F32))
                w0 = ps.enter_context(nc.sbuf_tensor(f"ff_w0{u}", [P, DC, D], BF16))
                w1 = ps.enter_context(nc.sbuf_tensor(f"ff_w1{u}", [P, DC, D], BF16))
                w2 = ps.enter_context(nc.sbuf_tensor(f"ff_w2{u}", [P, DC, D], BF16))
                wb = [w0, w1, w2]
                gbuf = ps.enter_context(nc.sbuf_tensor(f"ff_g{u}", [P, DC, 1152], BF16))
                rb0 = ps.enter_context(nc.sbuf_tensor(f"ff_r0{u}", [P, 1152], BF16))
                rb1 = ps.enter_context(nc.sbuf_tensor(f"ff_r1{u}", [P, 1152], BF16))
                rbufs = [rb0, rb1]
                cb0 = ps.enter_context(nc.sbuf_tensor(f"ff_c0{u}", [P, 1152], BF16))
                cb1 = ps.enter_context(nc.sbuf_tensor(f"ff_c1{u}", [P, 1152], BF16))
                cbufs = [cb0, cb1]

                rt = make_router(cl0, cln, cwf, cwx, ps)
                pe0, act0, dve0 = s_pe.n, s_act.n, s_dve.n
                cf0 = s_cfree.n
                wtgt = []
                for m in range(nmat):
                    c = s_wm[m % 3]
                    c.n += 16
                    wtgt.append(c.n)
                ctgt = {}
                for e in range(NE):
                    for hf in range(nh):
                        c = s_cb[e % 2]
                        c.n += 16
                        ctgt[(e, hf)] = c.n

                def pe_h(ei, hf, dh):
                    return pe0 + ei * 32 + hf * 16 + dh + 1

                def pe_ye(ei, hf, do):
                    return pe0 + ei * 32 + hf * 16 + 8 + do + 1

                def act_relu(ei, hf, dh):
                    return act0 + ei * 16 + hf * 8 + dh + 1

                def dve_g(ei, hf, dh):
                    return dve0 + 16 + (ei - 1) * 32 + hf * 16 + dh + 1

                def dve_R(ei, hf, do):
                    if ei == 0:
                        return dve0 + hf * 8 + do + 1
                    return dve0 + 16 + (ei - 1) * 32 + hf * 16 + 8 + do + 1

                GSEQ = []
                for ei in range(NE + 1):
                    for hf in range(nh):
                        for dh in range(8):
                            GSEQ.append(("h", ei, hf, dh))
                        for do in range(8):
                            GSEQ.append(("ye", ei, hf, do))

                def free_count(gidx):
                    kind, ei, hf, x = GSEQ[gidx]
                    if kind == "h":
                        return (s_act.h, act_relu(ei, hf, x))
                    return (s_dve.h, dve_R(ei, hf, x))

                with nc.Block() as blk:
                    @blk.gpsimd
                    def _(g):
                        for m in range(nmat):
                            ei, is2 = divmod(m, 2)
                            if ei == 0:
                                src = sW2 if is2 else sW1
                            else:
                                src = (eW2 if is2 else eW1)[ei - 1]
                            if m >= 3:
                                mp = m - 3
                                eip, is2p = divmod(mp, 2)
                                if is2p == 0:
                                    g.wait_ge(s_pe.h, pe_h(eip, nh - 1, 7))
                                else:
                                    g.wait_ge(s_pe.h, pe_ye(eip, nh - 1, 7))
                            g.dma_start(wb[m % 3][:, :, :],
                                        src.rearrange("(c p) n -> p c n", p=P)).then_inc(s_wm[m % 3].h, 16)
                        for i in range(3):
                            g.wait_ge(s_wm[i].h, s_wm[i].n)

                    @blk.sync
                    def _(sy):
                        rt["sync"](sy)
                        for e in range(NE):
                            for hf, (ho, hn) in enumerate(halves):
                                if hf >= 1:
                                    sy.wait_ge(s_cfree.h, cf0 + 2 * e + 1)
                                elif e >= 2:
                                    sy.wait_ge(s_cfree.h, cf0 + 2 * (e - 2) + 2)
                                sy.dma_start(cbufs[e % 2][:, 0:hn],
                                             scr_cwt[e, cl0 + ho: cl0 + ho + hn].partition_broadcast(P)
                                             ).then_inc(s_cb[e % 2].h, 16)
                        for i in range(2):
                            sy.wait_ge(s_cb[i].h, s_cb[i].n)
                        if stream_out:
                            for hf, (ho, hn) in enumerate(halves):
                                sy.wait_ge(s_dve.h, dve_R(NE, hf, 7))
                                sy.dma_start(
                                    outT.rearrange("(c p) t -> p c t", p=P)[:, :, ho: ho + hn],
                                    R[:, :, cl0 + ho: cl0 + ho + hn]).then_inc(s_hw.h, 16)
                                s_hw.n += 16
                            sy.wait_ge(s_hw.h, s_hw.n)

                    @blk.tensor
                    def _(t):
                        rt["pe_logits"](t)
                        ebwait = [True]
                        gidx = 0
                        for ei in range(NE + 1):
                            m1, m2 = 2 * ei, 2 * ei + 1
                            for hf, (ho, hn) in enumerate(halves):
                                tl = tiles(hn, 512)
                                if hf == 0:
                                    t.wait_ge(s_wm[m1 % 3].h, wtgt[m1])
                                for dh in range(8):
                                    bank = pA if gidx % 2 == 0 else pB
                                    if gidx >= 2:
                                        sh, cn = free_count(gidx - 2)
                                        t.wait_ge(sh, cn)
                                    for c in range(DC):
                                        for ti, (to, tn) in enumerate(tl):
                                            mm = t.matmul(bank[:, to:to + tn],
                                                          wb[m1 % 3][:, c, dh * P:(dh + 1) * P],
                                                          zT[:, c, cl0 + ho + to: cl0 + ho + to + tn],
                                                          start=(c == 0), stop=(c == DC - 1))
                                            if c == DC - 1 and ti == len(tl) - 1:
                                                mm.then_inc(s_pe.h, 1)
                                    gidx += 1
                                if ei == 0 and hf == 0:
                                    rt["pe_trans"](t)
                                if hf == 0:
                                    t.wait_ge(s_wm[m2 % 3].h, wtgt[m2])
                                for do in range(8):
                                    bank = pA if gidx % 2 == 0 else pB
                                    if gidx >= 2:
                                        sh, cn = free_count(gidx - 2)
                                        t.wait_ge(sh, cn)
                                    for dh in range(8):
                                        if do == 0:
                                            if ei == 0:
                                                t.wait_ge(s_act.h, act_relu(0, hf, dh))
                                            else:
                                                t.wait_ge(s_dve.h, dve_g(ei, hf, dh))
                                        for ti, (to, tn) in enumerate(tl):
                                            last = (dh == DC - 1) and (ei != 0)
                                            mm = t.matmul(bank[:, to:to + tn],
                                                          wb[m2 % 3][:, dh, do * P:(do + 1) * P],
                                                          gbuf[:, dh, to:to + tn],
                                                          start=(dh == 0), stop=last)
                                            if last and ti == len(tl) - 1:
                                                mm.then_inc(s_pe.h, 1)
                                    if ei == 0:
                                        if ebwait[0]:
                                            t.wait_ge(s_rdv.h, rt["cwx_ev"])
                                            ebwait[0] = False
                                        for ti, (to, tn) in enumerate(tl):
                                            mm = t.matmul(bank[:, to:to + tn],
                                                          eb2s[:, do * P:(do + 1) * P],
                                                          cwx[:, cl0 + ho + to: cl0 + ho + to + tn],
                                                          start=False, stop=True)
                                            if ti == len(tl) - 1:
                                                mm.then_inc(s_pe.h, 1)
                                    gidx += 1
                    s_pe.n += (NE + 1) * nh * 16

                    @blk.scalar
                    def _(s):
                        rt["act"](s)
                        for ei in range(NE + 1):
                            for hf, (ho, hn) in enumerate(halves):
                                for dh in range(8):
                                    gidx = ei * 32 + hf * 16 + dh
                                    bank = pA if gidx % 2 == 0 else pB
                                    s.wait_ge(s_pe.h, pe_h(ei, hf, dh))
                                    if ei == 0:
                                        s.activation(gbuf[:, dh, 0:hn], bank[:, 0:hn], AF.Relu,
                                                     bias=sb1c[:, dh:dh + 1],
                                                     scale=cst[:, 1:2]).then_inc(s_act.h, 1)
                                    else:
                                        ridx = (ei - 1) * 16 + hf * 8 + dh
                                        if ridx >= 2:
                                            e2, r2 = divmod(ridx - 2, 16)
                                            hf2, dh2 = divmod(r2, 8)
                                            s.wait_ge(s_dve.h, dve_g(e2 + 1, hf2, dh2))
                                        s.activation(rbufs[ridx % 2][:, 0:hn], bank[:, 0:hn], AF.Relu,
                                                     bias=eb1c[:, dh, ei - 1:ei]).then_inc(s_act.h, 1)
                    s_act.n += (NE + 1) * nh * 8

                    @blk.vector
                    def _(v):
                        rt["dve"](v)
                        for ei in range(NE + 1):
                            for hf, (ho, hn) in enumerate(halves):
                                if ei >= 1:
                                    e = ei - 1
                                    v.wait_ge(s_cb[e % 2].h, ctgt[(e, hf)])
                                    for dh in range(8):
                                        ridx = e * 16 + hf * 8 + dh
                                        v.wait_ge(s_act.h, act_relu(ei, hf, dh))
                                        v.tensor_tensor(out=gbuf[:, dh, 0:hn], in0=rbufs[ridx % 2][:, 0:hn],
                                                        in1=cbufs[e % 2][:, 0:hn], op=ALU.mult)
                                        v.drain()
                                        v.nop().then_inc(s_dve.h, 1)
                                    v.nop().then_inc(s_cfree.h, 1)
                                for do in range(8):
                                    v.wait_ge(s_pe.h, pe_ye(ei, hf, do))
                                    gidx = ei * 32 + hf * 16 + 8 + do
                                    bank = pA if gidx % 2 == 0 else pB
                                    v.tensor_tensor(out=R[:, do, cl0 + ho: cl0 + ho + hn],
                                                    in0=bank[:, 0:hn],
                                                    in1=R[:, do, cl0 + ho: cl0 + ho + hn], op=ALU.add)
                                    v.drain()
                                    v.nop().then_inc(s_dve.h, 1)
                    s_dve.n += NE * nh * 8 + (NE + 1) * nh * 8
                    s_cfree.n += NE * nh

        def phase_attn():
            uid[0] += 1
            u = uid[0]
            with ExitStack() as ps:
                pQ = [[ps.enter_context(nc.psum_tensor(f"at_S{i}{k}{u}", [P, 512], F32))
                       for k in range(2)] for i in range(2)]
                pT2 = [ps.enter_context(nc.psum_tensor(f"at_T{i}{u}", [P, 8, P], BF16)) for i in range(2)]
                pC2 = [ps.enter_context(nc.psum_tensor(f"at_C{i}{u}", [P, 2, P], F32)) for i in range(2)]
                pSa = pQ[0][0].ap()
                pSb = pQ[0][1].ap()
                wA = ps.enter_context(nc.sbuf_tensor(f"at_wA{u}", [P, DC, D], BF16))
                wB = ps.enter_context(nc.sbuf_tensor(f"at_wB{u}", [P, DC, D], BF16))
                kTb = ps.enter_context(nc.sbuf_tensor(f"at_kT{u}", [P, DC, 640], BF16))
                vbuf = ps.enter_context(nc.sbuf_tensor(f"at_v{u}", [P, 5, D], BF16))
                qTb = ps.enter_context(nc.sbuf_tensor(f"at_qT{u}", [P, DC, 512], BF16))
                ctxT = ps.enter_context(nc.sbuf_tensor(f"at_cT{u}", [P, DC, 512], BF16))
                esc = [ps.enter_context(nc.sbuf_tensor(f"at_es{i}{u}", [P, 4, 256], BF16)) for i in range(3)]
                nmx2 = [ps.enter_context(nc.sbuf_tensor(f"at_n2{i}{u}", [P, 2], F32)) for i in range(2)]
                atb = [ps.enter_context(nc.sbuf_tensor(f"at_ab{i}{u}", [P, 4, 256], BF16)) for i in range(2)]
                atT = [ps.enter_context(nc.sbuf_tensor(f"at_aT{i}{u}", [P, 8, P], BF16)) for i in range(2)]
                nmx = [ps.enter_context(nc.sbuf_tensor(f"at_nm{i}{u}", [P, 1], F32)) for i in range(2)]
                sro = [ps.enter_context(nc.sbuf_tensor(f"at_sr{i}{u}", [P, 4], F32)) for i in range(2)]
                rec = [ps.enter_context(nc.sbuf_tensor(f"at_rc{i}{u}", [P, 4], F32)) for i in range(2)]

                prev_e3 = [None]
                prev_e4 = [None]
                prev_pe_s_last = [None]
                prev_pe_c_last = [None]
                prev_dve_out = [None]
                qblk = nc.Block()
                blk = qblk.__enter__()
                for qh in range(aq):
                    kc0 = 512 * qh
                    qc0 = HALO + 512 * qh
                    awt = []
                    for mi in range(4):
                        c = s_wm[mi % 2]
                        c.n += 16
                        awt.append(c.n)
                    pe0, act0, dve0 = s_pe.n, s_act.n, s_dve.n
                    ktl = tiles(640, 512)

                    def prj_free(gi, act0=act0, dve0=dve0):
                        if gi < 16:
                            return (s_act.h, act0 + gi + 1)
                        if gi < 26:
                            return (s_dve.h, dve0 + (gi - 16) + 1)
                        return (s_act.h, act0 + 16 + (gi - 26) + 1)

                    if True:
                        @blk.gpsimd
                        def _(g):
                            waits = [prev_e3[0], prev_e4[0], pe0 + 16, pe0 + 26]
                            for mi, src in enumerate((WkT, WvT, WqT, WoT)):
                                if waits[mi] is not None:
                                    g.wait_ge(s_pe.h, waits[mi])
                                g.dma_start((wA if mi % 2 == 0 else wB)[:, :, :],
                                            src.rearrange("(c p) n -> p c n", p=P)).then_inc(s_wm[mi % 2].h, 16)
                            g.wait_ge(s_wm[0].h, awt[2])
                            g.wait_ge(s_wm[1].h, awt[1])

                        @blk.tensor
                        def _(t):
                            gi = 0
                            t.wait_ge(s_wm[0].h, awt[0])
                            if prev_dve_out[0] is not None:
                                t.wait_ge(s_dve.h, prev_dve_out[0])
                            for do in range(8):
                                for (to, tn) in ktl:
                                    bank = pSa if gi % 2 == 0 else pSb
                                    if gi >= 2:
                                        sh, cn = prj_free(gi - 2)
                                        t.wait_ge(sh, cn)
                                    for c in range(DC):
                                        mm = t.matmul(bank[:, 0:tn], wA[:, c, do * P:(do + 1) * P],
                                                      zT[:, c, kc0 + to: kc0 + to + tn],
                                                      start=(c == 0), stop=(c == DC - 1))
                                        if c == DC - 1:
                                            mm.then_inc(s_pe.h, 1)
                                    gi += 1
                            t.wait_ge(s_wm[1].h, awt[1])
                            for kc in range(5):
                                for nt in range(2):
                                    bank = pSa if gi % 2 == 0 else pSb
                                    if gi >= 2:
                                        sh, cn = prj_free(gi - 2)
                                        t.wait_ge(sh, cn)
                                    for c in range(DC):
                                        mm = t.matmul(bank[:, 0:512],
                                                      zT[:, c, kc0 + kc * P: kc0 + (kc + 1) * P],
                                                      wB[:, c, nt * 512:(nt + 1) * 512],
                                                      start=(c == 0), stop=(c == DC - 1))
                                        if c == DC - 1:
                                            mm.then_inc(s_pe.h, 1)
                                    gi += 1
                            t.wait_ge(s_wm[0].h, awt[2])
                            for do in range(8):
                                bank = pSa if gi % 2 == 0 else pSb
                                if gi >= 2:
                                    sh, cn = prj_free(gi - 2)
                                    t.wait_ge(sh, cn)
                                for c in range(DC):
                                    mm = t.matmul(bank[:, 0:512], wA[:, c, do * P:(do + 1) * P],
                                                  zT[:, c, qc0: qc0 + 512],
                                                  start=(c == 0), stop=(c == DC - 1))
                                    if c == DC - 1:
                                        mm.then_inc(s_pe.h, 1)
                                gi += 1
                        s_pe.n += 34
                        prev_e3[0] = pe0 + 34

                        @blk.scalar
                        def _(s):
                            if prev_pe_s_last[0] is not None:
                                s.wait_ge(s_pe.h, prev_pe_s_last[0])
                            gi = 0
                            for do in range(8):
                                for (to, tn) in ktl:
                                    bank = pSa if gi % 2 == 0 else pSb
                                    s.wait_ge(s_pe.h, pe0 + gi + 1)
                                    s.activation(kTb[:, do, to:to + tn], bank[:, 0:tn], AF.Identity,
                                                 bias=bkc[:, do:do + 1]).then_inc(s_act.h, 1)
                                    gi += 1
                            gi = 26
                            for do in range(8):
                                bank = pSa if gi % 2 == 0 else pSb
                                s.wait_ge(s_pe.h, pe0 + gi + 1)
                                s.activation(qTb[:, do, 0:512], bank[:, 0:512], AF.Identity,
                                             bias=bqc[:, do:do + 1], scale=cst[:, 2:3]).then_inc(s_act.h, 1)
                                gi += 1
                        s_act.n += 24

                        @blk.vector
                        def _(v):
                            if prev_pe_c_last[0] is not None:
                                v.wait_ge(s_pe.h, prev_pe_c_last[0])
                            gi = 16
                            for kc in range(5):
                                for nt in range(2):
                                    bank = pSa if gi % 2 == 0 else pSb
                                    v.wait_ge(s_pe.h, pe0 + gi + 1)
                                    v.tensor_tensor(out=vbuf[:, kc, nt * 512:(nt + 1) * 512],
                                                    in0=bank[:, 0:512],
                                                    in1=bvb[:, nt * 512:(nt + 1) * 512], op=ALU.add)
                                    v.drain()
                                    v.nop().then_inc(s_dve.h, 1)
                                    gi += 1
                        s_dve.n += 10

                    pe1, act1, dve1, gp1 = s_pe.n, s_act.n, s_dve.n, s_gp.n
                    NIT = anit

                    def pe_s(j):
                        return pe1 + 3 * j + 1

                    def pe_t(j):
                        return pe1 + 3 * (j + 3) + 2

                    def pe_c(j):
                        return pe1 + 3 * (j + 4) + 3

                    def d1(j):
                        return dve1 + 2 * j + 1

                    def d2(j):
                        return dve1 + 2 * (j + 1) + 2

                    def g_n(j):
                        return gp1 + j + 3

                    def a2(j):
                        return act1 + 3 * (j + 3) + 1

                    def a3(j):
                        return act1 + 3 * (j + 5) + 2

                    def a1(j):
                        return act1 + 3 * j + 3

                    NC2 = NIT + 5
                    act1v, dve1v = s_act.n, s_dve.n

                    if True:
                        @blk.tensor
                        def _(t):
                            t.wait_ge(s_act.h, act1v)
                            t.wait_ge(s_dve.h, dve1v)
                            for i in range(NC2):
                                if 0 <= i < NIT:
                                    j = i
                                    pl, hq = divmod(j, 4)
                                    p_glob = 4 * qh + pl
                                    midx = 0 if p_glob == 0 else (2 if p_glob == 15 else 1)
                                    if j >= 2:
                                        t.wait_ge(s_act.h, a1(j - 2))

                                    def score_mm(e, stop):
                                        h = 4 * hq + e
                                        hp, dc = h % 2, h // 2
                                        return t.matmul(pQ[j % 2][e % 2][:, (e // 2) * 256:(e // 2 + 1) * 256],
                                                        qTb[hp * HD:(hp + 1) * HD, dc, pl * P:(pl + 1) * P],
                                                        kTb[hp * HD:(hp + 1) * HD, dc, pl * P: pl * P + 256],
                                                        start=False, stop=stop)
                                    t.matmul(pQ[j % 2][0][:, :], identb[:, :],
                                             maskb[:, midx, 0:512], start=True, stop=False)
                                    score_mm(0, False)
                                    score_mm(2, True)
                                    t.matmul(pQ[j % 2][1][:, :], identb[:, :],
                                             maskb[:, midx, 512:1024], start=True, stop=False)
                                    score_mm(1, False)
                                    score_mm(3, True).then_inc(s_pe.h, 1)
                                else:
                                    t.drain()
                                    t.nop().then_inc(s_pe.h, 1)
                                if 0 <= i - 3 < NIT:
                                    j = i - 3
                                    t.wait_ge(s_gp.h, g_n(j))
                                    if j >= 2:
                                        t.wait_ge(s_act.h, a2(j - 2))
                                    pT = pT2[j % 2]
                                    for sidx in range(4):
                                        t.transpose(pT[:, 2 * sidx, :], atb[j % 2][:, sidx, 0:128], identb[:, :])
                                        mm = t.transpose(pT[:, 2 * sidx + 1, :], atb[j % 2][:, sidx, 128:256], identb[:, :])
                                        if sidx == 3:
                                            mm.then_inc(s_pe.h, 1)
                                else:
                                    t.drain()
                                    t.nop().then_inc(s_pe.h, 1)
                                if 0 <= i - 4 < NIT:
                                    j = i - 4
                                    pl, hq = divmod(j, 4)
                                    t.wait_ge(s_act.h, a2(j))
                                    if j >= 2:
                                        t.wait_ge(s_act.h, a3(j - 2))
                                    pC = pC2[j % 2]
                                    for sidx, e in enumerate((0, 2, 1, 3)):
                                        h = 4 * hq + e
                                        hp = h % 2
                                        t.matmul(pC[hp * HD:(hp + 1) * HD, e // 2, :],
                                                 vbuf[:, pl, h * HD:(h + 1) * HD],
                                                 atT[j % 2][:, 2 * sidx, :], start=True, stop=False)
                                        mm = t.matmul(pC[hp * HD:(hp + 1) * HD, e // 2, :],
                                                      vbuf[:, pl + 1, h * HD:(h + 1) * HD],
                                                      atT[j % 2][:, 2 * sidx + 1, :], start=False, stop=True)
                                        if sidx == 3:
                                            mm.then_inc(s_pe.h, 1)
                                else:
                                    t.drain()
                                    t.nop().then_inc(s_pe.h, 1)
                        s_pe.n += 3 * NC2

                        @blk.vector
                        def _(v):
                            for i in range(NC2):
                                if 0 <= i < NIT:
                                    j = i
                                    v.wait_ge(s_pe.h, pe_s(j))
                                    if j >= 2:
                                        v.wait_ge(s_act.h, a1(j - 2))
                                    v.tensor_reduce(out=nmx2[j % 2][:, 0:1], in_=pQ[j % 2][0][:, :],
                                                    axis=AX.X, op=ALU.max, negate=True)
                                    mm = v.tensor_reduce(out=nmx2[j % 2][:, 1:2], in_=pQ[j % 2][1][:, :],
                                                         axis=AX.X, op=ALU.max, negate=True)
                                    mm.then_inc(s_dve.h, 1)
                                else:
                                    v.drain()
                                    v.nop().then_inc(s_dve.h, 1)
                                if 0 <= i - 1 < NIT:
                                    j = i - 1
                                    v.wait_ge(s_act.h, a1(j))
                                    if j >= 2:
                                        v.wait_ge(s_gp.h, g_n(j - 2))
                                    v.tensor_reduce(out=sro[j % 2][:, 0:2], in_=esc[j % 3][:, 0:2, :],
                                                    axis=AX.X, op=ALU.add)
                                    v.tensor_reduce(out=sro[j % 2][:, 2:4], in_=esc[j % 3][:, 2:4, :],
                                                    axis=AX.X, op=ALU.add)
                                    v.drain()
                                    v.reciprocal(rec[j % 2][:, :], sro[j % 2][:, :])
                                    v.drain()
                                    v.nop().then_inc(s_dve.h, 1)
                                else:
                                    v.drain()
                                    v.nop().then_inc(s_dve.h, 1)
                        s_dve.n += 2 * NC2

                        @blk.gpsimd
                        def _(g):
                            for i in range(NC2):
                                if 0 <= i - 2 < NIT:
                                    j = i - 2
                                    g.wait_ge(s_dve.h, d2(j))
                                    if j >= 2:
                                        g.wait_ge(s_pe.h, pe_t(j - 2))
                                    b_ap = rec[j % 2][:, :].unsqueeze(2).broadcast_to((P, 4, 256))
                                    g.tensor_tensor(out=atb[j % 2][:, :, :], in0=esc[j % 3][:, :, :],
                                                    in1=b_ap, op=ALU.mult)
                                    g.drain()
                                    g.nop().then_inc(s_gp.h, 1)
                                else:
                                    g.drain()
                                    g.nop().then_inc(s_gp.h, 1)
                        s_gp.n += NC2

                        @blk.scalar
                        def _(s):
                            for i in range(NC2):
                                if 0 <= i - 3 < NIT:
                                    j = i - 3
                                    s.wait_ge(s_pe.h, pe_t(j))
                                    if j >= 2:
                                        s.wait_ge(s_pe.h, pe_c(j - 2))
                                    s.copy(atT[j % 2][:, :, :], pT2[j % 2][:, :, :]).then_inc(s_act.h, 1)
                                else:
                                    s.drain()
                                    s.nop().then_inc(s_act.h, 1)
                                if 0 <= i - 5 < NIT:
                                    j = i - 5
                                    pl, hq = divmod(j, 4)
                                    s.wait_ge(s_pe.h, pe_c(j))
                                    s.copy(ctxT[:, 2 * hq:2 * hq + 2, pl * P:(pl + 1) * P],
                                           pC2[j % 2][:, :, :]).then_inc(s_act.h, 1)
                                else:
                                    s.drain()
                                    s.nop().then_inc(s_act.h, 1)
                                if 0 <= i < NIT:
                                    j = i
                                    s.wait_ge(s_dve.h, d1(j))
                                    if j >= 3:
                                        s.wait_ge(s_gp.h, g_n(j - 3))
                                        s.wait_ge(s_dve.h, d2(j - 3))
                                    s.activation(esc[j % 3][:, 0:2, :], pQ[j % 2][0][:, :],
                                                 AF.Exp, bias=nmx2[j % 2][:, 0:1])
                                    s.activation(esc[j % 3][:, 2:4, :], pQ[j % 2][1][:, :],
                                                 AF.Exp, bias=nmx2[j % 2][:, 1:2]).then_inc(s_act.h, 1)
                                else:
                                    s.drain()
                                    s.nop().then_inc(s_act.h, 1)
                        s_act.n += 3 * NC2

                    prev_pe_s_last[0] = pe_s(NIT - 1)
                    prev_pe_c_last[0] = pe_c(NIT - 1)
                    if not aout:
                        prev_e3[0] = None
                        prev_e4[0] = None
                        continue
                    pe2, dve2 = s_pe.n, s_dve.n
                    a3last = a3(NIT - 1)
                    if True:
                        @blk.tensor
                        def _(t):
                            t.wait_ge(s_wm[1].h, awt[3])
                            t.wait_ge(s_act.h, a3last)
                            for do in range(8):
                                bank = pSa if do % 2 == 0 else pSb
                                if do >= 2:
                                    t.wait_ge(s_dve.h, dve2 + do - 1)
                                for c in range(DC):
                                    mm = t.matmul(bank[:, 0:512], wB[:, c, do * P:(do + 1) * P],
                                                  ctxT[:, c, 0:512], start=(c == 0), stop=(c == DC - 1))
                                    if c == DC - 1:
                                        mm.then_inc(s_pe.h, 1)
                        s_pe.n += 8
                        prev_e4[0] = pe2 + 8

                        @blk.vector
                        def _(v):
                            for do in range(8):
                                bank = pSa if do % 2 == 0 else pSb
                                v.wait_ge(s_pe.h, pe2 + do + 1)
                                v.scalar_tensor_tensor(out=R[:, do, qc0:qc0 + 512],
                                                       in0=bank[:, 0:512], scalar=boc[:, do:do + 1],
                                                       in1=R[:, do, qc0:qc0 + 512],
                                                       op0=ALU.add, op1=ALU.add)
                                v.drain()
                                v.nop().then_inc(s_dve.h, 1)
                        s_dve.n += 8
                    prev_dve_out[0] = s_dve.n
                qblk.__exit__(None, None, None)

        if upto >= 1:
            phase_ln(0, TX)
            if t_z1 is not None:
                tap3(t_z1, zT[:, :, :])
        if upto >= 2:
            with ExitStack() as cs:
                cwf = cs.enter_context(nc.sbuf_tensor("cwf_1", [P, NCH, NE], F32))
                cwx = cs.enter_context(nc.sbuf_tensor("cwx_1", [16, TX], BF16))
                with nc.Block() as blk:
                    @blk.gpsimd
                    def _(g):
                        g.dma_start(cwx[NE:16, :], onesr[:1, :]).then_inc(s_in.h, 16)
                        s_in.n += 16
                        g.wait_ge(s_in.h, s_in.n)
                if upto >= 3:
                    phase_ffn(0, TX, [(0, 1152), (1152, 1024)], cwf, cwx)
                    if t_cw1 is not None:
                        tap3(t_cw1, cwf[:, :, :])
            if upto >= 3 and t_y1 is not None:
                tap3(t_y1, R[:, :, :])
        if upto >= 4:
            phase_ln(0, TX)
            if t_z2 is not None:
                tap3(t_z2, zT[:, :, :])
        if upto >= 5:
            phase_attn()
            if t_y2 is not None:
                tap3(t_y2, R[:, :, :])
        if upto >= 6:
            phase_ln(HALO, TOWN)
            if t_z3 is not None:
                tap3(t_z3, zT[:, :, :])
            with ExitStack() as cs:
                cwf = cs.enter_context(nc.sbuf_tensor("cwf_2", [P, NCH, NE], F32))
                cwx = cs.enter_context(nc.sbuf_tensor("cwx_2", [16, TX], BF16))
                with nc.Block() as blk:
                    @blk.gpsimd
                    def _(g):
                        g.dma_start(cwx[NE:16, :], onesr[:1, :]).then_inc(s_in.h, 16)
                        s_in.n += 16
                        g.wait_ge(s_in.h, s_in.n)
                phase_ffn(HALO, TOWN, [(0, 1024), (1024, 1024)], cwf, cwx, stream_out=True)
                if t_cw3 is not None:
                    tap3(t_cw3, cwf[:, :, :])

        if upto < 6:
            with nc.Block() as blk:
                @blk.sync
                def _(sy):
                    sy.dma_start(outT.rearrange("(c p) t -> p c t", p=P),
                                 R[:, :, HALO:HALO + TOWN]).then_inc(s_hw.h, 16)
                    s_hw.n += 16
                    sy.wait_ge(s_hw.h, s_hw.n)

    return nc, tapnames


def make_masks(core):
    r = np.arange(P)[:, None]
    jj = np.arange(256)[None, :]
    band = (jj - r >= 32) & (jj - r <= 96)
    mi = np.where(band, 0.0, NEG).astype(np.float32)
    ma = np.where(band & (jj >= 64), 0.0, NEG).astype(np.float32) if core == 0 else mi
    mc = np.where(band & (jj < 192), 0.0, NEG).astype(np.float32) if core == NCORE - 1 else mi
    st = np.stack([ma, mi, mc]).astype(np.float32)
    return np.ascontiguousarray(np.tile(st, (1, 1, 4)))


def prep_inputs(L0_attn_x, L0_attn_Wq, L0_attn_bq, L0_attn_Wk, L0_attn_bk,
                L0_attn_Wv, L0_attn_bv, L0_attn_Wo, L0_attn_bo,
                L0_ffn_router_W, L0_ffn_router_b, L0_ffn_sW1, L0_ffn_sb1,
                L0_ffn_sW2, L0_ffn_sb2, L0_ffn_eW1, L0_ffn_eb1,
                L0_ffn_eW2, L0_ffn_eb2):
    x = np.asarray(L0_attn_x, dtype=np.float32).reshape(SEQ, D)
    xTf = np.ascontiguousarray(x.T)
    xTpad = np.zeros((D, SEQ + 2 * HALO), np.float32)
    xTpad[:, HALO:HALO + SEQ] = xTf
    common = {
        "WqT": np.ascontiguousarray(np.asarray(L0_attn_Wq, np.float32).T),
        "WkT": np.ascontiguousarray(np.asarray(L0_attn_Wk, np.float32).T),
        "WvT": np.ascontiguousarray(np.asarray(L0_attn_Wv, np.float32).T),
        "WoT": np.ascontiguousarray(np.asarray(L0_attn_Wo, np.float32).T),
        "bq": np.ascontiguousarray(np.asarray(L0_attn_bq, np.float32).reshape(DC, P).T),
        "bk": np.ascontiguousarray(np.asarray(L0_attn_bk, np.float32).reshape(DC, P).T),
        "bv": np.asarray(L0_attn_bv, np.float32),
        "bo": np.ascontiguousarray(np.asarray(L0_attn_bo, np.float32).reshape(DC, P).T),
        "rW": np.asarray(L0_ffn_router_W, np.float32),
        "rbias": np.asarray(L0_ffn_router_b, np.float32),
        "sW1": np.asarray(L0_ffn_sW1, np.float32),
        "sW2": np.asarray(L0_ffn_sW2, np.float32),
        "sb1": np.ascontiguousarray(np.asarray(L0_ffn_sb1, np.float32).reshape(DC, P).T),
        "sb2": np.asarray(L0_ffn_sb2, np.float32),
        "eW1": np.asarray(L0_ffn_eW1, np.float32),
        "eW2": np.asarray(L0_ffn_eW2, np.float32),
        "eb1T": np.ascontiguousarray(np.asarray(L0_ffn_eb1, np.float32).T),
        "eb2": np.asarray(L0_ffn_eb2, np.float32),
    }
    in_maps = []
    for c in range(NCORE):
        m = dict(common)
        st = c * TOWN
        m["xT"] = np.ascontiguousarray(xTpad[:, st:st + TX])
        m["masks"] = make_masks(c)
        in_maps.append(m)
    return in_maps


def kernel(**inputs):
    in_maps = prep_inputs(**inputs)
    nc, _ = build_program()
    res = run_bass_kernel_spmd(nc, in_maps, core_ids=list(range(NCORE)))
    parts = [np.asarray(r["outT"]) for r in res.results]
    full = np.concatenate(parts, axis=1)
    return np.ascontiguousarray(full.T).reshape(1, SEQ, D).astype(np.float32)


# version 26
# speedup vs baseline: 1.2044x; 1.0031x over previous
import numpy as np
from contextlib import ExitStack

import concourse.bass as bass
import concourse.mybir as mybir
from concourse.bass_utils import run_bass_kernel_spmd

F32 = mybir.dt.float32
BF16 = mybir.dt.bfloat16
AF = mybir.ActivationFunctionType
ALU = mybir.AluOpType
AX = mybir.AxisListType

P = 128
D = 1024
DC = D // P
TOWN = 2048
HALO = 64
TX = TOWN + 2 * HALO
NCH = TX // P
NE = 15
NH = 16
HD = 64
EPS = 1e-5
NEG = -1.0e9
NCORE = 8
SEQ = 16384


def tiles(width, maxn=512):
    out = []
    o = 0
    while o < width:
        n = min(maxn, width - o)
        out.append((o, n))
        o += n
    return out


class Ctr:
    def __init__(self, es, nc, name):
        self.h = es.enter_context(nc.semaphore(name))
        self.n = 0


def build_program(upto=99, taps=(), aq=4, anit=16, aout=True, acut=99):
    nc = bass.Bass("TRN2", target_bir_lowering=False)
    taps = set(taps)
    tapnames = []
    uid = [0]

    xT = nc.dram_tensor("xT", [D, TX], F32, kind="ExternalInput").ap()
    masks = nc.dram_tensor("masks", [3, P, 1024], F32, kind="ExternalInput").ap()
    WqT = nc.dram_tensor("WqT", [D, D], F32, kind="ExternalInput").ap()
    WkT = nc.dram_tensor("WkT", [D, D], F32, kind="ExternalInput").ap()
    WvT = nc.dram_tensor("WvT", [D, D], F32, kind="ExternalInput").ap()
    WoT = nc.dram_tensor("WoT", [D, D], F32, kind="ExternalInput").ap()
    bq = nc.dram_tensor("bq", [P, DC], F32, kind="ExternalInput").ap()
    bk = nc.dram_tensor("bk", [P, DC], F32, kind="ExternalInput").ap()
    bv = nc.dram_tensor("bv", [D], F32, kind="ExternalInput").ap()
    bo = nc.dram_tensor("bo", [P, DC], F32, kind="ExternalInput").ap()
    rW = nc.dram_tensor("rW", [D, NE], F32, kind="ExternalInput").ap()
    rbias = nc.dram_tensor("rbias", [NE], F32, kind="ExternalInput").ap()
    sW1 = nc.dram_tensor("sW1", [D, D], F32, kind="ExternalInput").ap()
    sW2 = nc.dram_tensor("sW2", [D, D], F32, kind="ExternalInput").ap()
    sb1 = nc.dram_tensor("sb1", [P, DC], F32, kind="ExternalInput").ap()
    sb2 = nc.dram_tensor("sb2", [D], F32, kind="ExternalInput").ap()
    eW1 = nc.dram_tensor("eW1", [NE, D, D], F32, kind="ExternalInput").ap()
    eW2 = nc.dram_tensor("eW2", [NE, D, D], F32, kind="ExternalInput").ap()
    eb1T = nc.dram_tensor("eb1T", [D, NE], F32, kind="ExternalInput").ap()
    eb2 = nc.dram_tensor("eb2", [NE, D], F32, kind="ExternalInput").ap()

    outT = nc.dram_tensor("outT", [D, TOWN], F32, kind="ExternalOutput").ap()

    scr_rows = nc.dram_tensor("scr_rows", [2, TX], F32).ap()
    scr_cwt = nc.dram_tensor("scr_cwt", [NE, TX], BF16).ap()

    def mktap(name, shape, dtype=F32):
        if name in taps:
            tapnames.append(name)
            return nc.dram_tensor("tap_" + name, shape, dtype, kind="ExternalOutput").ap()
        return None

    t_z1 = mktap("z1", [P, DC, TX], BF16)
    t_cw1 = mktap("cw1", [P, NCH, NE], F32)
    t_y1 = mktap("y1", [P, DC, TX], F32)
    t_z2 = mktap("z2", [P, DC, TX], BF16)
    t_kT = mktap("kT", [P, DC, 640], BF16)
    t_v = mktap("v", [P, 5, D], BF16)
    t_qT = mktap("qT", [P, DC, 512], BF16)
    t_y2 = mktap("y2", [P, DC, TX], F32)
    t_z3 = mktap("z3", [P, DC, TX], BF16)
    t_cw3 = mktap("cw3", [P, NCH, NE], F32)

    es = ExitStack()
    with es:
        R = es.enter_context(nc.sbuf_tensor("R", [P, DC, TX], F32))
        zT = es.enter_context(nc.sbuf_tensor("zT", [P, DC, TX], BF16))
        bqc = es.enter_context(nc.sbuf_tensor("bqc", [P, DC], F32))
        bkc = es.enter_context(nc.sbuf_tensor("bkc", [P, DC], F32))
        boc = es.enter_context(nc.sbuf_tensor("boc", [P, DC], F32))
        bvb = es.enter_context(nc.sbuf_tensor("bvb", [P, D], BF16))
        sb1c = es.enter_context(nc.sbuf_tensor("sb1c", [P, DC], F32))
        eb1c = es.enter_context(nc.sbuf_tensor("eb1c", [P, DC, NE], F32))
        eb2s = es.enter_context(nc.sbuf_tensor("eb2s", [16, D], BF16))
        rWb = es.enter_context(nc.sbuf_tensor("rWb", [P, DC, NE], BF16))
        rbb = es.enter_context(nc.sbuf_tensor("rbb", [P, NE], F32))
        maskb = es.enter_context(nc.sbuf_tensor("maskb", [P, 3, 1024], BF16))
        identf = es.enter_context(nc.sbuf_tensor("identf", [P, P], F32))
        identb = es.enter_context(nc.sbuf_tensor("identb", [P, P], BF16))
        ones_col = es.enter_context(nc.sbuf_tensor("ones_col", [P, 1], BF16))
        ncol = es.enter_context(nc.sbuf_tensor("ncol", [P, 1], F32))
        qcol = es.enter_context(nc.sbuf_tensor("qcol", [P, 1], BF16))
        iota15 = es.enter_context(nc.sbuf_tensor("iota15", [P, NE], F32))
        cwf = es.enter_context(nc.sbuf_tensor("cwf", [P, NCH, NE], F32))
        cst = es.enter_context(nc.sbuf_tensor("cst", [P, 4], F32))
        onesr = es.enter_context(nc.sbuf_tensor("onesr", [1, TX], BF16))
        cwx = es.enter_context(nc.sbuf_tensor("cwx", [16, TX], BF16))

        s_in = Ctr(es, nc, "s_in")
        s_wm = [Ctr(es, nc, f"s_wm{i}") for i in range(3)]
        s_hw = Ctr(es, nc, "s_hw")
        s_cb = [Ctr(es, nc, f"s_cb{i}") for i in range(2)]
        s_pe = Ctr(es, nc, "s_pe")
        s_act = Ctr(es, nc, "s_act")
        s_dve = Ctr(es, nc, "s_dve")
        s_cfree = Ctr(es, nc, "s_cfree")
        s_gp = Ctr(es, nc, "s_gp")
        s_rpe = Ctr(es, nc, "s_rpe")
        s_rdv = Ctr(es, nc, "s_rdv")
        s_rac = Ctr(es, nc, "s_rac")

        def tap3(tapap, sbap):
            with nc.Block() as blk:
                @blk.sync
                def _(sy):
                    sy.dma_start(tapap, sbap).then_inc(s_hw.h, 16)
                    s_hw.n += 16
                    sy.wait_ge(s_hw.h, s_hw.n)

        with ExitStack() as ps:
            ioA = ps.enter_context(nc.sbuf_tensor("ioA", [P, P], F32))
            ioB = ps.enter_context(nc.sbuf_tensor("ioB", [P, P], F32))
            sb2row = ps.enter_context(nc.sbuf_tensor("sb2row", [1, D], F32))
            sb2h = ps.enter_context(nc.sbuf_tensor("sb2h", [1, D], BF16))
            with nc.Block() as blk:
                @blk.gpsimd
                def _(g):
                    def d(out, in_):
                        g.dma_start(out, in_).then_inc(s_in.h, 16)
                        s_in.n += 16
                    d(R[:, 3:5, :], xT.rearrange("(c p) t -> p c t", p=P)[:, 3:5, :])
                    d(maskb[:, :, :], masks.rearrange("m p t -> p m t"))
                    d(sb2row[:1, :], sb2.rearrange("(a d) -> a d", a=1))
                    d(eb2s[0:NE, :], eb2[:, :])
                    d(bvb[:, :], bv.partition_broadcast(P))
                    d(rWb[:, :, :], rW.rearrange("(c p) e -> p c e", p=P))
                    g.iota(iota15[:, :], [[1, NE]], channel_multiplier=0,
                           allow_small_or_imprecise_dtypes=True)
                    g.iota(ioA[:, :], [[1, P]], channel_multiplier=0,
                           allow_small_or_imprecise_dtypes=True)
                    g.iota(ioB[:, :], [[0, P]], channel_multiplier=1,
                           allow_small_or_imprecise_dtypes=True)
                    g.memset(ones_col[:, :], 1.0)
                    g.memset(ncol[:, :], -1.0 / D)
                    g.memset(qcol[:, :], 1.0 / D)
                    g.memset(cst[:, 3:4], -0.5)
                    g.memset(onesr[:1, :], 1.0)
                    g.memset(cst[:, 0:1], EPS)
                    g.memset(cst[:, 1:2], 0.5)
                    g.memset(cst[:, 2:3], 0.125)
                    g.wait_ge(s_in.h, s_in.n)

                @blk.sync
                def _(sy):
                    def d2_(out, in_):
                        sy.dma_start(out, in_).then_inc(s_hw.h, 16)
                        s_hw.n += 16
                    d2_(R[:, 0:3, :], xT.rearrange("(c p) t -> p c t", p=P)[:, 0:3, :])
                    d2_(bqc[:, :], bq[:, :])
                    d2_(bkc[:, :], bk[:, :])
                    d2_(boc[:, :], bo[:, :])
                    d2_(sb1c[:, :], sb1[:, :])
                    d2_(eb1c[:, :, :], eb1T.rearrange("(c p) e -> p c e", p=P))
                    d2_(rbb[:, :], rbias.partition_broadcast(P))
                    sy.wait_ge(s_hw.h, s_hw.n + 16)
                    s_hw.n += 16

                @blk.scalar
                def _(s):
                    s.dma_start(R[:, 5:8, :],
                                xT.rearrange("(c p) t -> p c t", p=P)[:, 5:8, :]).then_inc(s_hw.h, 16)
            with nc.Block() as blk:
                @blk.vector
                def _(v):
                    v.tensor_tensor(out=identf[:, :], in0=ioA[:, :], in1=ioB[:, :], op=ALU.is_equal)
                    v.drain()
                    v.tensor_copy(identb[:, :], identf[:, :])
                    v.tensor_scalar_mul(bqc[:, :], bqc[:, :], 0.125)
                    v.tensor_scalar_mul(sb1c[:, :], sb1c[:, :], 0.5)
                @blk.scalar
                def _(s):
                    s.activation(sb2h[:1, :], sb2row[:1, :], AF.Copy, scale=cst[0:1, 1:2]).then_inc(s_act.h, 1)
                    s_act.n += 1

            with nc.Block() as blk:
                @blk.gpsimd
                def _(g):
                    g.wait_ge(s_act.h, s_act.n)
                    g.dma_start(eb2s[NE:16, :], sb2h[:1, :]).then_inc(s_in.h, 16)
                    s_in.n += 16
                    g.wait_ge(s_in.h, s_in.n)

        def phase_ln(cl0, cln):
            uid[0] += 1
            u = uid[0]
            NSEG = 4
            sw = cln // NSEG
            NDV = 5
            with ExitStack() as ps:
                ssum = ps.enter_context(nc.sbuf_tensor(f"ln_su{u}", [1, TX], F32))
                ssq = ps.enter_context(nc.sbuf_tensor(f"ln_sq{u}", [1, TX], F32))
                svar = ps.enter_context(nc.sbuf_tensor(f"ln_va{u}", [1, TX], F32))
                slog = ps.enter_context(nc.sbuf_tensor(f"ln_lg{u}", [1, TX], F32))
                srstd = ps.enter_context(nc.sbuf_tensor(f"ln_sr{u}", [1, TX], F32))
                mB = ps.enter_context(nc.sbuf_tensor(f"ln_mb{u}", [P, TX], F32))
                rB = ps.enter_context(nc.sbuf_tensor(f"ln_rb{u}", [P, TX], F32))
                gsq = ps.enter_context(nc.sbuf_tensor(f"ln_gsq{u}", [P, DC, 2 * sw], BF16))
                tl = tiles(sw, 512)
                nt = len(tl)
                psA = [ps.enter_context(nc.psum_tensor(f"ln_psA{u}_{i}", [1, 512], F32))
                       for i in range(nt)]
                psB = [ps.enter_context(nc.psum_tensor(f"ln_psB{u}_{i}", [1, 512], F32))
                       for i in range(nt)]
                act0, pe0, dve0 = s_act.n, s_pe.n, s_dve.n
                hw0n = s_hw.n
                order = [("q", 0)]
                for sg in range(1, NSEG):
                    order.append(("q", sg))
                    order.append(("e", sg - 1))
                order.append(("e", NSEG - 1))
                sq_inc = {sg: order.index(("q", sg)) + 1 for sg in range(NSEG)}
                exp_inc = {sg: order.index(("e", sg)) + 1 for sg in range(NSEG)}
                with nc.Block() as blk:
                    @blk.scalar
                    def _(s):
                        for kind, sg in order:
                            if kind == "q":
                                c0 = cl0 + sg * sw
                                go = (sg % 2) * sw
                                if sg >= 2:
                                    s.wait_ge(s_pe.h, pe0 + (2 * (sg - 2) + 2) * nt)
                                for c in range(DC):
                                    mm = s.activation(gsq[:, c, go:go + sw],
                                                      R[:, c, c0:c0 + sw], AF.Square)
                                    if c == DC - 1:
                                        mm.then_inc(s_act.h, 1)
                            else:
                                o0 = sg * sw
                                s.wait_ge(s_dve.h, dve0 + 3 * sg + 3)
                                s.activation(slog[:1, o0:o0 + sw], svar[:1, o0:o0 + sw],
                                             AF.Ln, bias=cst[0:1, 0:1])
                                s.drain()
                                s.activation(srstd[:1, o0:o0 + sw], slog[:1, o0:o0 + sw],
                                             AF.Exp, scale=cst[0:1, 3:4]).then_inc(s_act.h, 1)
                    s_act.n += 2 * NSEG

                    @blk.tensor
                    def _(t):
                        for sg in range(NSEG):
                            c0 = cl0 + sg * sw
                            go = (sg % 2) * sw
                            for ti, (to, tn) in enumerate(tl):
                                if sg >= 1 and ti == 0:
                                    t.wait_ge(s_dve.h, dve0 + 3 * (sg - 1) + 1)
                                for c in range(DC):
                                    st, sp = (c == 0), (c == DC - 1)
                                    mm = t.matmul(psA[ti][0:1, 0:tn], ncol[:, :],
                                                  R[:, c, c0 + to: c0 + to + tn],
                                                  start=st, stop=sp)
                                    if sp:
                                        mm.then_inc(s_pe.h, 1)
                            t.wait_ge(s_act.h, act0 + sq_inc[sg])
                            for ti, (to, tn) in enumerate(tl):
                                if sg >= 1 and ti == 0:
                                    t.wait_ge(s_dve.h, dve0 + 3 * (sg - 1) + 2)
                                for c in range(DC):
                                    st, sp = (c == 0), (c == DC - 1)
                                    mm = t.matmul(psB[ti][0:1, 0:tn], qcol[:, :],
                                                  gsq[:, c, go + to:go + to + tn],
                                                  start=st, stop=sp)
                                    if sp:
                                        mm.then_inc(s_pe.h, 1)
                    s_pe.n += 2 * NSEG * nt

                    @blk.vector
                    def _(v):
                        def evacs(sg, sq, ptensors, dst):
                            o0 = sg * sw
                            base = pe0 + (2 * sg + sq) * nt
                            for ti, (to, tn) in enumerate(tl):
                                v.wait_ge(s_pe.h, base + ti + 1)
                                v.tensor_copy(dst[:1, o0 + to:o0 + to + tn],
                                              ptensors[ti][0:1, 0:tn])
                            v.drain()
                            v.nop().then_inc(s_dve.h, 1)

                        def rowmath(sg):
                            rs = slice(sg * sw, (sg + 1) * sw)
                            v.tensor_tensor(out=svar[:1, rs], in0=ssum[:1, rs],
                                            in1=ssum[:1, rs], op=ALU.mult)
                            v.drain()
                            v.tensor_tensor(out=svar[:1, rs], in0=ssq[:1, rs],
                                            in1=svar[:1, rs], op=ALU.subtract)
                            v.drain()
                            v.nop().then_inc(s_dve.h, 1)

                        def npass(sg, p2):
                            c0 = cl0 + sg * sw
                            o0 = sg * sw
                            sl = slice(c0, c0 + sw)
                            v.wait_ge(s_hw.h, hw0n + 64 * sg + 32 * (p2 + 1))
                            for c in range(NDV):
                                if p2:
                                    v.tensor_tensor(out=zT[:, c, sl], in0=zT[:, c, sl],
                                                    in1=rB[:, o0:o0 + sw], op=ALU.mult)
                                else:
                                    v.tensor_tensor(out=zT[:, c, sl], in0=R[:, c, sl],
                                                    in1=mB[:, o0:o0 + sw], op=ALU.add)
                            v.drain()

                        for sg in range(NSEG):
                            evacs(sg, 0, psA, ssum)
                            evacs(sg, 1, psB, ssq)
                            rowmath(sg)
                            if sg >= 1:
                                npass(sg - 1, 1)
                            npass(sg, 0)
                        npass(NSEG - 1, 1)
                    s_dve.n += 3 * NSEG

                    @blk.sync
                    def _(sy):
                        cnt = hw0n
                        for sg in range(NSEG):
                            rs = slice(sg * sw, (sg + 1) * sw)
                            sy.wait_ge(s_dve.h, dve0 + 3 * sg + 1)
                            sy.dma_start(scr_rows[0:1, rs], ssum[:1, rs]).then_inc(s_hw.h, 16)
                            cnt += 16
                            sy.wait_ge(s_hw.h, cnt)
                            sy.dma_start(mB[:, rs],
                                         scr_rows[0, rs].partition_broadcast(P)).then_inc(s_hw.h, 16)
                            cnt += 16
                            sy.wait_ge(s_hw.h, cnt)
                            sy.wait_ge(s_act.h, act0 + exp_inc[sg])
                            sy.dma_start(scr_rows[1:2, rs], srstd[:1, rs]).then_inc(s_hw.h, 16)
                            cnt += 16
                            sy.wait_ge(s_hw.h, cnt)
                            sy.dma_start(rB[:, rs],
                                         scr_rows[1, rs].partition_broadcast(P)).then_inc(s_hw.h, 16)
                            cnt += 16
                            if sg < NSEG - 1:
                                sy.wait_ge(s_hw.h, cnt)
                        s_hw.n += 64 * NSEG

                    @blk.gpsimd
                    def _(g):
                        for sg in range(NSEG):
                            c0 = cl0 + sg * sw
                            o0 = sg * sw
                            sl = slice(c0, c0 + sw)
                            g.wait_ge(s_hw.h, hw0n + 64 * sg + 32)
                            for c in range(NDV, DC):
                                g.tensor_tensor(out=zT[:, c, sl], in0=R[:, c, sl],
                                                in1=mB[:, o0:o0 + sw], op=ALU.add)
                            g.drain()
                            g.wait_ge(s_hw.h, hw0n + 64 * sg + 64)
                            for c in range(NDV, DC):
                                g.tensor_tensor(out=zT[:, c, sl], in0=zT[:, c, sl],
                                                in1=rB[:, o0:o0 + sw], op=ALU.mult)
                            g.drain()

        def make_router(cl0, cln, cwf, cwx, ps):
            uid[0] += 1
            u = uid[0]
            nch = cln // P
            nf = nch * NE
            prt = ps.enter_context(nc.psum_tensor(f"rt_p{u}", [P, NE], F32))
            ptw = ps.enter_context(nc.psum_tensor(f"rt_t{u}", [NE, P], F32))
            logt = ps.enter_context(nc.sbuf_tensor(f"rt_lg{u}", [P, NCH, NE], F32))
            probs = ps.enter_context(nc.sbuf_tensor(f"rt_pr{u}", [P, NCH, NE], F32))
            tt1 = ps.enter_context(nc.sbuf_tensor(f"rt_t1{u}", [P, NCH, NE], F32))
            nmx = ps.enter_context(nc.sbuf_tensor(f"rt_nm{u}", [P, 1], F32))
            f17 = ps.enter_context(nc.sbuf_tensor(f"rt_f17{u}", [P, NCH], F32))
            r17 = ps.enter_context(nc.sbuf_tensor(f"rt_r17{u}", [P, NCH], F32))
            iota15m = ps.enter_context(nc.sbuf_tensor(f"rt_iom{u}", [P, NE], F32))
            pe0r, dve0r, act0r = s_rpe.n, s_rdv.n, s_rac.n
            dve1r = dve0r + nch
            dve2r = dve1r + 2
            pe1r = pe0r + nch
            st = {"cwx_ev": dve2r + nch + 1}

            L = logt.ap().rearrange("p a b -> p (a b)")[:, 0:nf].rearrange("p (a b) -> p a b", b=NE)
            PR = probs.ap().rearrange("p a b -> p (a b)")[:, 0:nf].rearrange("p (a b) -> p a b", b=NE)
            T1 = tt1.ap().rearrange("p a b -> p (a b)")[:, 0:nf].rearrange("p (a b) -> p a b", b=NE)
            CW = cwf.ap().rearrange("p a b -> p (a b)")[:, 0:nf].rearrange("p (a b) -> p a b", b=NE)
            Lf = logt.ap().rearrange("p a b -> p (a b)")[:, 0:nf]
            PRf = probs.ap().rearrange("p a b -> p (a b)")[:, 0:nf]
            T1f = tt1.ap().rearrange("p a b -> p (a b)")[:, 0:nf]
            CWf = cwf.ap().rearrange("p a b -> p (a b)")[:, 0:nf]
            io_b = iota15.ap().unsqueeze(1).broadcast_to((P, nch, NE))
            iom_b = iota15m.ap().unsqueeze(1).broadcast_to((P, nch, NE))

            def pe_logits(t):
                for k in range(nch):
                    if k >= 1:
                        t.wait_ge(s_rdv.h, dve0r + k)
                    for c in range(DC):
                        mm = t.matmul(prt[:, :], zT[:, c, cl0 + k * P: cl0 + (k + 1) * P],
                                      rWb[:, c, :], start=(c == 0), stop=(c == DC - 1))
                        if c == DC - 1:
                            mm.then_inc(s_rpe.h, 1)
                s_rpe.n += nch

            def pe_trans(t):
                t.wait_ge(s_rdv.h, dve1r + 2)
                for k in range(nch):
                    if k >= 1:
                        t.wait_ge(s_rdv.h, dve2r + k)
                    t.transpose(ptw[0:NE, :], cwf[:, k, :], identf[:, :]).then_inc(s_rpe.h, 1)
                s_rpe.n += nch

            def dve(v):
                v.tensor_scalar_mul(iota15m[:, :], iota15[:, :], 1.0)
                v.drain()
                v.tensor_scalar_add(out=iota15m[:, :], in0=iota15m[:, :], scalar1=-100.0)
                v.drain()
                for k in range(nch):
                    v.wait_ge(s_rpe.h, pe0r + k + 1)
                    v.tensor_tensor(out=logt[:, k, :], in0=prt[:, :], in1=rbb[:, :], op=ALU.add)
                    v.drain()
                    v.nop().then_inc(s_rdv.h, 1)
                v.tensor_reduce(out=nmx[:, 0:1], in_=Lf, axis=AX.X, op=ALU.max, negate=True)
                v.drain()
                v.nop().then_inc(s_rdv.h, 1)
                v.wait_ge(s_rac.h, act0r + 1)
                v.tensor_reduce(out=f17[:, 0:nch], in_=PR, axis=AX.X, op=ALU.add)
                v.drain()
                v.reciprocal(r17[:, 0:nch], f17[:, 0:nch])
                v.drain()
                v.tensor_tensor(out=PR, in0=PR,
                                in1=r17[:, 0:nch].unsqueeze(2).broadcast_to((P, nch, NE)), op=ALU.mult)
                v.drain()
                v.memset(CWf, 0.0)
                v.drain()
                for it in range(3):
                    v.tensor_reduce(out=f17[:, 0:nch], in_=PR, axis=AX.X, op=ALU.max, negate=True)
                    v.drain()
                    fb = f17[:, 0:nch].unsqueeze(2).broadcast_to((P, nch, NE))
                    v.tensor_tensor(out=T1, in0=PR, in1=fb, op=ALU.add)
                    v.drain()
                    v.tensor_single_scalar(out=T1f, in_=T1f, scalar=0.0, op=ALU.is_equal)
                    v.drain()
                    v.tensor_tensor(out=T1, in0=T1, in1=iom_b, op=ALU.mult)
                    v.drain()
                    v.tensor_scalar_add(out=T1f, in0=T1f, scalar1=100.0)
                    v.drain()
                    v.tensor_reduce(out=f17[:, 0:nch], in_=T1, axis=AX.X, op=ALU.min)
                    v.drain()
                    fb = f17[:, 0:nch].unsqueeze(2).broadcast_to((P, nch, NE))
                    v.tensor_tensor(out=T1, in0=io_b, in1=fb, op=ALU.subtract)
                    v.drain()
                    v.tensor_single_scalar(out=T1f, in_=T1f, scalar=0.0, op=ALU.is_equal)
                    v.drain()
                    v.tensor_reduce(out=r17[:, 0:nch], in_=PR, axis=AX.X, op=ALU.max, negate=True)
                    v.drain()
                    rb2 = r17[:, 0:nch].unsqueeze(2).broadcast_to((P, nch, NE))
                    v.tensor_tensor(out=T1, in0=T1, in1=rb2, op=ALU.mult)
                    v.drain()
                    v.tensor_tensor(out=CW, in0=CW, in1=T1, op=ALU.subtract)
                    v.drain()
                    v.tensor_tensor(out=PR, in0=PR, in1=T1, op=ALU.add)
                    v.drain()
                v.tensor_scalar_mul(CWf, CWf, 0.5)
                v.drain()
                v.nop().then_inc(s_rdv.h, 1)
                for k in range(nch):
                    v.wait_ge(s_rpe.h, pe1r + k + 1)
                    v.tensor_copy(cwx[0:NE, cl0 + k * P: cl0 + (k + 1) * P], ptw[0:NE, :])
                    v.drain()
                    v.nop().then_inc(s_rdv.h, 1)
                v.nop().then_inc(s_rdv.h, 1)
                s_rdv.n += nch + 2 + nch + 1

            def act(sc):
                sc.wait_ge(s_rdv.h, dve1r + 1)
                sc.activation(PRf, Lf, AF.Exp, bias=nmx[:, 0:1]).then_inc(s_rac.h, 1)
                s_rac.n += 1

            def sync(sy):
                sy.wait_ge(s_rdv.h, st["cwx_ev"])
                sy.dma_start(scr_cwt[:, cl0:cl0 + cln], cwx[0:NE, cl0:cl0 + cln]).then_inc(s_hw.h, 16)
                s_hw.n += 16
                sy.wait_ge(s_hw.h, s_hw.n)

            return {"pe_logits": pe_logits, "pe_trans": pe_trans, "dve": dve,
                    "act": act, "sync": sync, "cwx_ev": st["cwx_ev"]}

        def phase_ffn(cl0, cln, halves, cwf, cwx, stream_out=False):
            uid[0] += 1
            u = uid[0]
            nmat = 2 * (NE + 1)
            nh = len(halves)
            with ExitStack() as ps:
                pA = ps.enter_context(nc.psum_tensor(f"ff_pA{u}", [P, 1152], F32))
                pB = ps.enter_context(nc.psum_tensor(f"ff_pB{u}", [P, 1152], F32))
                w0 = ps.enter_context(nc.sbuf_tensor(f"ff_w0{u}", [P, DC, D], BF16))
                w1 = ps.enter_context(nc.sbuf_tensor(f"ff_w1{u}", [P, DC, D], BF16))
                w2 = ps.enter_context(nc.sbuf_tensor(f"ff_w2{u}", [P, DC, D], BF16))
                wb = [w0, w1, w2]
                gbuf = ps.enter_context(nc.sbuf_tensor(f"ff_g{u}", [P, DC, 1152], BF16))
                rb0 = ps.enter_context(nc.sbuf_tensor(f"ff_r0{u}", [P, 1152], BF16))
                rb1 = ps.enter_context(nc.sbuf_tensor(f"ff_r1{u}", [P, 1152], BF16))
                rbufs = [rb0, rb1]
                cb0 = ps.enter_context(nc.sbuf_tensor(f"ff_c0{u}", [P, 1152], BF16))
                cb1 = ps.enter_context(nc.sbuf_tensor(f"ff_c1{u}", [P, 1152], BF16))
                cbufs = [cb0, cb1]

                rt = make_router(cl0, cln, cwf, cwx, ps)
                pe0, act0, dve0 = s_pe.n, s_act.n, s_dve.n
                cf0 = s_cfree.n
                wtgt = []
                for m in range(nmat):
                    c = s_wm[m % 3]
                    c.n += 16
                    wtgt.append(c.n)
                ctgt = {}
                for e in range(NE):
                    for hf in range(nh):
                        c = s_cb[e % 2]
                        c.n += 16
                        ctgt[(e, hf)] = c.n

                def pe_h(ei, hf, dh):
                    return pe0 + ei * 32 + hf * 16 + dh + 1

                def pe_ye(ei, hf, do):
                    return pe0 + ei * 32 + hf * 16 + 8 + do + 1

                def act_relu(ei, hf, dh):
                    return act0 + ei * 16 + hf * 8 + dh + 1

                def dve_g(ei, hf, dh):
                    return dve0 + 16 + (ei - 1) * 32 + hf * 16 + dh + 1

                def dve_R(ei, hf, do):
                    if ei == 0:
                        return dve0 + hf * 8 + do + 1
                    return dve0 + 16 + (ei - 1) * 32 + hf * 16 + 8 + do + 1

                GSEQ = []
                for ei in range(NE + 1):
                    for hf in range(nh):
                        for dh in range(8):
                            GSEQ.append(("h", ei, hf, dh))
                        for do in range(8):
                            GSEQ.append(("ye", ei, hf, do))

                def free_count(gidx):
                    kind, ei, hf, x = GSEQ[gidx]
                    if kind == "h":
                        return (s_act.h, act_relu(ei, hf, x))
                    return (s_dve.h, dve_R(ei, hf, x))

                with nc.Block() as blk:
                    @blk.gpsimd
                    def _(g):
                        for m in range(nmat):
                            ei, is2 = divmod(m, 2)
                            if ei == 0:
                                src = sW2 if is2 else sW1
                            else:
                                src = (eW2 if is2 else eW1)[ei - 1]
                            if m >= 3:
                                mp = m - 3
                                eip, is2p = divmod(mp, 2)
                                if is2p == 0:
                                    g.wait_ge(s_pe.h, pe_h(eip, nh - 1, 7))
                                else:
                                    g.wait_ge(s_pe.h, pe_ye(eip, nh - 1, 7))
                            g.dma_start(wb[m % 3][:, :, :],
                                        src.rearrange("(c p) n -> p c n", p=P)).then_inc(s_wm[m % 3].h, 16)
                        for i in range(3):
                            g.wait_ge(s_wm[i].h, s_wm[i].n)

                    @blk.sync
                    def _(sy):
                        rt["sync"](sy)
                        for e in range(NE):
                            for hf, (ho, hn) in enumerate(halves):
                                if hf >= 1:
                                    sy.wait_ge(s_cfree.h, cf0 + 2 * e + 1)
                                elif e >= 2:
                                    sy.wait_ge(s_cfree.h, cf0 + 2 * (e - 2) + 2)
                                sy.dma_start(cbufs[e % 2][:, 0:hn],
                                             scr_cwt[e, cl0 + ho: cl0 + ho + hn].partition_broadcast(P)
                                             ).then_inc(s_cb[e % 2].h, 16)
                        for i in range(2):
                            sy.wait_ge(s_cb[i].h, s_cb[i].n)
                        if stream_out:
                            for hf, (ho, hn) in enumerate(halves):
                                sy.wait_ge(s_dve.h, dve_R(NE, hf, 7))
                                sy.dma_start(
                                    outT.rearrange("(c p) t -> p c t", p=P)[:, :, ho: ho + hn],
                                    R[:, :, cl0 + ho: cl0 + ho + hn]).then_inc(s_hw.h, 16)
                                s_hw.n += 16
                            sy.wait_ge(s_hw.h, s_hw.n)

                    @blk.tensor
                    def _(t):
                        rt["pe_logits"](t)
                        ebwait = [True]
                        gidx = 0
                        for ei in range(NE + 1):
                            m1, m2 = 2 * ei, 2 * ei + 1
                            for hf, (ho, hn) in enumerate(halves):
                                tl = tiles(hn, 512)
                                if hf == 0:
                                    t.wait_ge(s_wm[m1 % 3].h, wtgt[m1])
                                for dh in range(8):
                                    bank = pA if gidx % 2 == 0 else pB
                                    if gidx >= 2:
                                        sh, cn = free_count(gidx - 2)
                                        t.wait_ge(sh, cn)
                                    for c in range(DC):
                                        for ti, (to, tn) in enumerate(tl):
                                            mm = t.matmul(bank[:, to:to + tn],
                                                          wb[m1 % 3][:, c, dh * P:(dh + 1) * P],
                                                          zT[:, c, cl0 + ho + to: cl0 + ho + to + tn],
                                                          start=(c == 0), stop=(c == DC - 1))
                                            if c == DC - 1 and ti == len(tl) - 1:
                                                mm.then_inc(s_pe.h, 1)
                                    gidx += 1
                                if ei == 0 and hf == 0:
                                    rt["pe_trans"](t)
                                if hf == 0:
                                    t.wait_ge(s_wm[m2 % 3].h, wtgt[m2])
                                for do in range(8):
                                    bank = pA if gidx % 2 == 0 else pB
                                    if gidx >= 2:
                                        sh, cn = free_count(gidx - 2)
                                        t.wait_ge(sh, cn)
                                    for dh in range(8):
                                        if do == 0:
                                            if ei == 0:
                                                t.wait_ge(s_act.h, act_relu(0, hf, dh))
                                            else:
                                                t.wait_ge(s_dve.h, dve_g(ei, hf, dh))
                                        for ti, (to, tn) in enumerate(tl):
                                            last = (dh == DC - 1) and (ei != 0)
                                            mm = t.matmul(bank[:, to:to + tn],
                                                          wb[m2 % 3][:, dh, do * P:(do + 1) * P],
                                                          gbuf[:, dh, to:to + tn],
                                                          start=(dh == 0), stop=last)
                                            if last and ti == len(tl) - 1:
                                                mm.then_inc(s_pe.h, 1)
                                    if ei == 0:
                                        if ebwait[0]:
                                            t.wait_ge(s_rdv.h, rt["cwx_ev"])
                                            ebwait[0] = False
                                        for ti, (to, tn) in enumerate(tl):
                                            mm = t.matmul(bank[:, to:to + tn],
                                                          eb2s[:, do * P:(do + 1) * P],
                                                          cwx[:, cl0 + ho + to: cl0 + ho + to + tn],
                                                          start=False, stop=True)
                                            if ti == len(tl) - 1:
                                                mm.then_inc(s_pe.h, 1)
                                    gidx += 1
                    s_pe.n += (NE + 1) * nh * 16

                    @blk.scalar
                    def _(s):
                        rt["act"](s)
                        for ei in range(NE + 1):
                            for hf, (ho, hn) in enumerate(halves):
                                for dh in range(8):
                                    gidx = ei * 32 + hf * 16 + dh
                                    bank = pA if gidx % 2 == 0 else pB
                                    s.wait_ge(s_pe.h, pe_h(ei, hf, dh))
                                    if ei == 0:
                                        s.activation(gbuf[:, dh, 0:hn], bank[:, 0:hn], AF.Relu,
                                                     bias=sb1c[:, dh:dh + 1],
                                                     scale=cst[:, 1:2]).then_inc(s_act.h, 1)
                                    else:
                                        ridx = (ei - 1) * 16 + hf * 8 + dh
                                        if ridx >= 2:
                                            e2, r2 = divmod(ridx - 2, 16)
                                            hf2, dh2 = divmod(r2, 8)
                                            s.wait_ge(s_dve.h, dve_g(e2 + 1, hf2, dh2))
                                        s.activation(rbufs[ridx % 2][:, 0:hn], bank[:, 0:hn], AF.Relu,
                                                     bias=eb1c[:, dh, ei - 1:ei]).then_inc(s_act.h, 1)
                    s_act.n += (NE + 1) * nh * 8

                    @blk.vector
                    def _(v):
                        rt["dve"](v)
                        for ei in range(NE + 1):
                            for hf, (ho, hn) in enumerate(halves):
                                if ei >= 1:
                                    e = ei - 1
                                    v.wait_ge(s_cb[e % 2].h, ctgt[(e, hf)])
                                    for dh in range(8):
                                        ridx = e * 16 + hf * 8 + dh
                                        v.wait_ge(s_act.h, act_relu(ei, hf, dh))
                                        v.tensor_tensor(out=gbuf[:, dh, 0:hn], in0=rbufs[ridx % 2][:, 0:hn],
                                                        in1=cbufs[e % 2][:, 0:hn], op=ALU.mult)
                                        v.drain()
                                        v.nop().then_inc(s_dve.h, 1)
                                    v.nop().then_inc(s_cfree.h, 1)
                                for do in range(8):
                                    v.wait_ge(s_pe.h, pe_ye(ei, hf, do))
                                    gidx = ei * 32 + hf * 16 + 8 + do
                                    bank = pA if gidx % 2 == 0 else pB
                                    v.tensor_tensor(out=R[:, do, cl0 + ho: cl0 + ho + hn],
                                                    in0=bank[:, 0:hn],
                                                    in1=R[:, do, cl0 + ho: cl0 + ho + hn], op=ALU.add)
                                    v.drain()
                                    v.nop().then_inc(s_dve.h, 1)
                    s_dve.n += NE * nh * 8 + (NE + 1) * nh * 8
                    s_cfree.n += NE * nh

        def phase_attn():
            uid[0] += 1
            u = uid[0]
            with ExitStack() as ps:
                pQ = [[ps.enter_context(nc.psum_tensor(f"at_S{i}{k}{u}", [P, 512], F32))
                       for k in range(2)] for i in range(2)]
                pT2 = [ps.enter_context(nc.psum_tensor(f"at_T{i}{u}", [P, 8, P], BF16)) for i in range(2)]
                pC2 = [ps.enter_context(nc.psum_tensor(f"at_C{i}{u}", [P, 2, P], F32)) for i in range(2)]
                pSa = pQ[0][0].ap()
                pSb = pQ[0][1].ap()
                wA = ps.enter_context(nc.sbuf_tensor(f"at_wA{u}", [P, DC, D], BF16))
                wB = ps.enter_context(nc.sbuf_tensor(f"at_wB{u}", [P, DC, D], BF16))
                kTb = ps.enter_context(nc.sbuf_tensor(f"at_kT{u}", [P, DC, 640], BF16))
                vbuf = ps.enter_context(nc.sbuf_tensor(f"at_v{u}", [P, 5, D], BF16))
                qTb = ps.enter_context(nc.sbuf_tensor(f"at_qT{u}", [P, DC, 512], BF16))
                ctxT = ps.enter_context(nc.sbuf_tensor(f"at_cT{u}", [P, DC, 512], BF16))
                esc = [ps.enter_context(nc.sbuf_tensor(f"at_es{i}{u}", [P, 4, 256], BF16)) for i in range(3)]
                nmx2 = [ps.enter_context(nc.sbuf_tensor(f"at_n2{i}{u}", [P, 2], F32)) for i in range(2)]
                atb = [ps.enter_context(nc.sbuf_tensor(f"at_ab{i}{u}", [P, 4, 256], BF16)) for i in range(2)]
                atT = [ps.enter_context(nc.sbuf_tensor(f"at_aT{i}{u}", [P, 8, P], BF16)) for i in range(2)]
                nmx = [ps.enter_context(nc.sbuf_tensor(f"at_nm{i}{u}", [P, 1], F32)) for i in range(2)]
                sro = [ps.enter_context(nc.sbuf_tensor(f"at_sr{i}{u}", [P, 4], F32)) for i in range(2)]
                rec = [ps.enter_context(nc.sbuf_tensor(f"at_rc{i}{u}", [P, 4], F32)) for i in range(2)]

                prev_e3 = [None]
                prev_e4 = [None]
                prev_pe_s_last = [None]
                prev_pe_c_last = [None]
                prev_dve_out = [None]
                qblk = nc.Block()
                blk = qblk.__enter__()
                for qh in range(aq):
                    kc0 = 512 * qh
                    qc0 = HALO + 512 * qh
                    awt = []
                    for mi in range(4):
                        c = s_wm[mi % 2]
                        c.n += 16
                        awt.append(c.n)
                    pe0, act0, dve0 = s_pe.n, s_act.n, s_dve.n
                    ktl = tiles(640, 512)

                    def prj_free(gi, act0=act0, dve0=dve0):
                        if gi < 16:
                            return (s_act.h, act0 + gi + 1)
                        if gi < 26:
                            return (s_dve.h, dve0 + (gi - 16) + 1)
                        return (s_act.h, act0 + 16 + (gi - 26) + 1)

                    if True:
                        @blk.gpsimd
                        def _(g):
                            waits = [prev_e3[0], prev_e4[0], pe0 + 16, pe0 + 26]
                            for mi, src in enumerate((WkT, WvT, WqT, WoT)):
                                if waits[mi] is not None:
                                    g.wait_ge(s_pe.h, waits[mi])
                                g.dma_start((wA if mi % 2 == 0 else wB)[:, :, :],
                                            src.rearrange("(c p) n -> p c n", p=P)).then_inc(s_wm[mi % 2].h, 16)
                            g.wait_ge(s_wm[0].h, awt[2])
                            g.wait_ge(s_wm[1].h, awt[1])

                        @blk.tensor
                        def _(t):
                            gi = 0
                            t.wait_ge(s_wm[0].h, awt[0])
                            if prev_dve_out[0] is not None:
                                t.wait_ge(s_dve.h, prev_dve_out[0])
                            for do in range(8):
                                for (to, tn) in ktl:
                                    bank = pSa if gi % 2 == 0 else pSb
                                    if gi >= 2:
                                        sh, cn = prj_free(gi - 2)
                                        t.wait_ge(sh, cn)
                                    for c in range(DC):
                                        mm = t.matmul(bank[:, 0:tn], wA[:, c, do * P:(do + 1) * P],
                                                      zT[:, c, kc0 + to: kc0 + to + tn],
                                                      start=(c == 0), stop=(c == DC - 1))
                                        if c == DC - 1:
                                            mm.then_inc(s_pe.h, 1)
                                    gi += 1
                            t.wait_ge(s_wm[1].h, awt[1])
                            for kc in range(5):
                                for nt in range(2):
                                    bank = pSa if gi % 2 == 0 else pSb
                                    if gi >= 2:
                                        sh, cn = prj_free(gi - 2)
                                        t.wait_ge(sh, cn)
                                    for c in range(DC):
                                        mm = t.matmul(bank[:, 0:512],
                                                      zT[:, c, kc0 + kc * P: kc0 + (kc + 1) * P],
                                                      wB[:, c, nt * 512:(nt + 1) * 512],
                                                      start=(c == 0), stop=(c == DC - 1))
                                        if c == DC - 1:
                                            mm.then_inc(s_pe.h, 1)
                                    gi += 1
                            t.wait_ge(s_wm[0].h, awt[2])
                            for do in range(8):
                                bank = pSa if gi % 2 == 0 else pSb
                                if gi >= 2:
                                    sh, cn = prj_free(gi - 2)
                                    t.wait_ge(sh, cn)
                                for c in range(DC):
                                    mm = t.matmul(bank[:, 0:512], wA[:, c, do * P:(do + 1) * P],
                                                  zT[:, c, qc0: qc0 + 512],
                                                  start=(c == 0), stop=(c == DC - 1))
                                    if c == DC - 1:
                                        mm.then_inc(s_pe.h, 1)
                                gi += 1
                        s_pe.n += 34
                        prev_e3[0] = pe0 + 34

                        @blk.scalar
                        def _(s):
                            if prev_pe_s_last[0] is not None:
                                s.wait_ge(s_pe.h, prev_pe_s_last[0])
                            gi = 0
                            for do in range(8):
                                for (to, tn) in ktl:
                                    bank = pSa if gi % 2 == 0 else pSb
                                    s.wait_ge(s_pe.h, pe0 + gi + 1)
                                    s.activation(kTb[:, do, to:to + tn], bank[:, 0:tn], AF.Identity,
                                                 bias=bkc[:, do:do + 1]).then_inc(s_act.h, 1)
                                    gi += 1
                            gi = 26
                            for do in range(8):
                                bank = pSa if gi % 2 == 0 else pSb
                                s.wait_ge(s_pe.h, pe0 + gi + 1)
                                s.activation(qTb[:, do, 0:512], bank[:, 0:512], AF.Identity,
                                             bias=bqc[:, do:do + 1], scale=cst[:, 2:3]).then_inc(s_act.h, 1)
                                gi += 1
                        s_act.n += 24

                        @blk.vector
                        def _(v):
                            if prev_pe_c_last[0] is not None:
                                v.wait_ge(s_pe.h, prev_pe_c_last[0])
                            gi = 16
                            for kc in range(5):
                                for nt in range(2):
                                    bank = pSa if gi % 2 == 0 else pSb
                                    v.wait_ge(s_pe.h, pe0 + gi + 1)
                                    v.tensor_tensor(out=vbuf[:, kc, nt * 512:(nt + 1) * 512],
                                                    in0=bank[:, 0:512],
                                                    in1=bvb[:, nt * 512:(nt + 1) * 512], op=ALU.add)
                                    v.drain()
                                    v.nop().then_inc(s_dve.h, 1)
                                    gi += 1
                        s_dve.n += 10

                    pe1, act1, dve1, gp1 = s_pe.n, s_act.n, s_dve.n, s_gp.n
                    NIT = anit

                    def pe_s(j):
                        return pe1 + 3 * j + 1

                    def pe_t(j):
                        return pe1 + 3 * (j + 3) + 2

                    def pe_c(j):
                        return pe1 + 3 * (j + 4) + 3

                    def d1(j):
                        return dve1 + 2 * j + 1

                    def d2(j):
                        return dve1 + 2 * (j + 1) + 2

                    def g_n(j):
                        return gp1 + j + 3

                    def a2(j):
                        return act1 + 3 * (j + 3) + 1

                    def a3(j):
                        return act1 + 3 * (j + 5) + 2

                    def a1(j):
                        return act1 + 3 * j + 3

                    NC2 = NIT + 5
                    act1v, dve1v = s_act.n, s_dve.n

                    if True:
                        @blk.tensor
                        def _(t):
                            t.wait_ge(s_act.h, act1v)
                            t.wait_ge(s_dve.h, dve1v)
                            for i in range(NC2):
                                if 0 <= i < NIT:
                                    j = i
                                    pl, hq = divmod(j, 4)
                                    p_glob = 4 * qh + pl
                                    midx = 0 if p_glob == 0 else (2 if p_glob == 15 else 1)
                                    if j >= 2:
                                        t.wait_ge(s_act.h, a1(j - 2))

                                    def score_mm(e, stop):
                                        h = 4 * hq + e
                                        hp, dc = h % 2, h // 2
                                        return t.matmul(pQ[j % 2][e % 2][:, (e // 2) * 256:(e // 2 + 1) * 256],
                                                        qTb[hp * HD:(hp + 1) * HD, dc, pl * P:(pl + 1) * P],
                                                        kTb[hp * HD:(hp + 1) * HD, dc, pl * P: pl * P + 256],
                                                        start=False, stop=stop)
                                    t.matmul(pQ[j % 2][0][:, :], identb[:, :],
                                             maskb[:, midx, 0:512], start=True, stop=False)
                                    score_mm(0, False)
                                    score_mm(2, True)
                                    t.matmul(pQ[j % 2][1][:, :], identb[:, :],
                                             maskb[:, midx, 512:1024], start=True, stop=False)
                                    score_mm(1, False)
                                    score_mm(3, True).then_inc(s_pe.h, 1)
                                else:
                                    t.drain()
                                    t.nop().then_inc(s_pe.h, 1)
                                if 0 <= i - 3 < NIT:
                                    j = i - 3
                                    t.wait_ge(s_gp.h, g_n(j))
                                    if j >= 2:
                                        t.wait_ge(s_act.h, a2(j - 2))
                                    pT = pT2[j % 2]
                                    for sidx in range(4):
                                        t.transpose(pT[:, 2 * sidx, :], atb[j % 2][:, sidx, 0:128], identb[:, :])
                                        mm = t.transpose(pT[:, 2 * sidx + 1, :], atb[j % 2][:, sidx, 128:256], identb[:, :])
                                        if sidx == 3:
                                            mm.then_inc(s_pe.h, 1)
                                else:
                                    t.drain()
                                    t.nop().then_inc(s_pe.h, 1)
                                if 0 <= i - 4 < NIT:
                                    j = i - 4
                                    pl, hq = divmod(j, 4)
                                    t.wait_ge(s_act.h, a2(j))
                                    if j >= 2:
                                        t.wait_ge(s_act.h, a3(j - 2))
                                    pC = pC2[j % 2]
                                    for sidx, e in enumerate((0, 2, 1, 3)):
                                        h = 4 * hq + e
                                        hp = h % 2
                                        t.matmul(pC[hp * HD:(hp + 1) * HD, e // 2, :],
                                                 vbuf[:, pl, h * HD:(h + 1) * HD],
                                                 atT[j % 2][:, 2 * sidx, :], start=True, stop=False)
                                        mm = t.matmul(pC[hp * HD:(hp + 1) * HD, e // 2, :],
                                                      vbuf[:, pl + 1, h * HD:(h + 1) * HD],
                                                      atT[j % 2][:, 2 * sidx + 1, :], start=False, stop=True)
                                        if sidx == 3:
                                            mm.then_inc(s_pe.h, 1)
                                else:
                                    t.drain()
                                    t.nop().then_inc(s_pe.h, 1)
                        s_pe.n += 3 * NC2

                        @blk.vector
                        def _(v):
                            for i in range(NC2):
                                if 0 <= i < NIT:
                                    j = i
                                    v.wait_ge(s_pe.h, pe_s(j))
                                    if j >= 2:
                                        v.wait_ge(s_act.h, a1(j - 2))
                                    v.tensor_reduce(out=nmx2[j % 2][:, 0:1], in_=pQ[j % 2][0][:, :],
                                                    axis=AX.X, op=ALU.max, negate=True)
                                    mm = v.tensor_reduce(out=nmx2[j % 2][:, 1:2], in_=pQ[j % 2][1][:, :],
                                                         axis=AX.X, op=ALU.max, negate=True)
                                    mm.then_inc(s_dve.h, 1)
                                else:
                                    v.drain()
                                    v.nop().then_inc(s_dve.h, 1)
                                if 0 <= i - 1 < NIT:
                                    j = i - 1
                                    v.wait_ge(s_act.h, a1(j))
                                    if j >= 2:
                                        v.wait_ge(s_gp.h, g_n(j - 2))
                                    v.tensor_reduce(out=sro[j % 2][:, 0:2], in_=esc[j % 3][:, 0:2, :],
                                                    axis=AX.X, op=ALU.add)
                                    v.tensor_reduce(out=sro[j % 2][:, 2:4], in_=esc[j % 3][:, 2:4, :],
                                                    axis=AX.X, op=ALU.add)
                                    v.drain()
                                    v.reciprocal(rec[j % 2][:, :], sro[j % 2][:, :])
                                    v.drain()
                                    v.nop().then_inc(s_dve.h, 1)
                                else:
                                    v.drain()
                                    v.nop().then_inc(s_dve.h, 1)
                        s_dve.n += 2 * NC2

                        @blk.gpsimd
                        def _(g):
                            for i in range(NC2):
                                if 0 <= i - 2 < NIT:
                                    j = i - 2
                                    g.wait_ge(s_dve.h, d2(j))
                                    if j >= 2:
                                        g.wait_ge(s_pe.h, pe_t(j - 2))
                                    b_ap = rec[j % 2][:, :].unsqueeze(2).broadcast_to((P, 4, 256))
                                    g.tensor_tensor(out=atb[j % 2][:, :, :], in0=esc[j % 3][:, :, :],
                                                    in1=b_ap, op=ALU.mult)
                                    g.drain()
                                    g.nop().then_inc(s_gp.h, 1)
                                else:
                                    g.drain()
                                    g.nop().then_inc(s_gp.h, 1)
                        s_gp.n += NC2

                        @blk.scalar
                        def _(s):
                            for i in range(NC2):
                                if 0 <= i - 3 < NIT:
                                    j = i - 3
                                    s.wait_ge(s_pe.h, pe_t(j))
                                    if j >= 2:
                                        s.wait_ge(s_pe.h, pe_c(j - 2))
                                    s.copy(atT[j % 2][:, :, :], pT2[j % 2][:, :, :]).then_inc(s_act.h, 1)
                                else:
                                    s.drain()
                                    s.nop().then_inc(s_act.h, 1)
                                if 0 <= i - 5 < NIT:
                                    j = i - 5
                                    pl, hq = divmod(j, 4)
                                    s.wait_ge(s_pe.h, pe_c(j))
                                    s.copy(ctxT[:, 2 * hq:2 * hq + 2, pl * P:(pl + 1) * P],
                                           pC2[j % 2][:, :, :]).then_inc(s_act.h, 1)
                                else:
                                    s.drain()
                                    s.nop().then_inc(s_act.h, 1)
                                if 0 <= i < NIT:
                                    j = i
                                    s.wait_ge(s_dve.h, d1(j))
                                    if j >= 3:
                                        s.wait_ge(s_gp.h, g_n(j - 3))
                                        s.wait_ge(s_dve.h, d2(j - 3))
                                    s.activation(esc[j % 3][:, 0:2, :], pQ[j % 2][0][:, :],
                                                 AF.Exp, bias=nmx2[j % 2][:, 0:1])
                                    s.activation(esc[j % 3][:, 2:4, :], pQ[j % 2][1][:, :],
                                                 AF.Exp, bias=nmx2[j % 2][:, 1:2]).then_inc(s_act.h, 1)
                                else:
                                    s.drain()
                                    s.nop().then_inc(s_act.h, 1)
                        s_act.n += 3 * NC2

                    prev_pe_s_last[0] = pe_s(NIT - 1)
                    prev_pe_c_last[0] = pe_c(NIT - 1)
                    if not aout:
                        prev_e3[0] = None
                        prev_e4[0] = None
                        continue
                    pe2, dve2 = s_pe.n, s_dve.n
                    a3last = a3(NIT - 1)
                    if True:
                        @blk.tensor
                        def _(t):
                            t.wait_ge(s_wm[1].h, awt[3])
                            t.wait_ge(s_act.h, a3last)
                            for do in range(8):
                                bank = pSa if do % 2 == 0 else pSb
                                if do >= 2:
                                    t.wait_ge(s_dve.h, dve2 + do - 1)
                                for c in range(DC):
                                    mm = t.matmul(bank[:, 0:512], wB[:, c, do * P:(do + 1) * P],
                                                  ctxT[:, c, 0:512], start=(c == 0), stop=(c == DC - 1))
                                    if c == DC - 1:
                                        mm.then_inc(s_pe.h, 1)
                        s_pe.n += 8
                        prev_e4[0] = pe2 + 8

                        @blk.vector
                        def _(v):
                            for do in range(8):
                                bank = pSa if do % 2 == 0 else pSb
                                v.wait_ge(s_pe.h, pe2 + do + 1)
                                v.scalar_tensor_tensor(out=R[:, do, qc0:qc0 + 512],
                                                       in0=bank[:, 0:512], scalar=boc[:, do:do + 1],
                                                       in1=R[:, do, qc0:qc0 + 512],
                                                       op0=ALU.add, op1=ALU.add)
                                v.drain()
                                v.nop().then_inc(s_dve.h, 1)
                        s_dve.n += 8
                    prev_dve_out[0] = s_dve.n
                qblk.__exit__(None, None, None)

        if upto >= 1:
            phase_ln(0, TX)
            if t_z1 is not None:
                tap3(t_z1, zT[:, :, :])
        if upto >= 2:
            with ExitStack() as cs:
                cwf = cs.enter_context(nc.sbuf_tensor("cwf_1", [P, NCH, NE], F32))
                cwx = cs.enter_context(nc.sbuf_tensor("cwx_1", [16, TX], BF16))
                with nc.Block() as blk:
                    @blk.gpsimd
                    def _(g):
                        g.dma_start(cwx[NE:16, :], onesr[:1, :]).then_inc(s_in.h, 16)
                        s_in.n += 16
                        g.wait_ge(s_in.h, s_in.n)
                if upto >= 3:
                    phase_ffn(0, TX, [(0, 1152), (1152, 1024)], cwf, cwx)
                    if t_cw1 is not None:
                        tap3(t_cw1, cwf[:, :, :])
            if upto >= 3 and t_y1 is not None:
                tap3(t_y1, R[:, :, :])
        if upto >= 4:
            phase_ln(0, TX)
            if t_z2 is not None:
                tap3(t_z2, zT[:, :, :])
        if upto >= 5:
            phase_attn()
            if t_y2 is not None:
                tap3(t_y2, R[:, :, :])
        if upto >= 6:
            phase_ln(HALO, TOWN)
            if t_z3 is not None:
                tap3(t_z3, zT[:, :, :])
            with ExitStack() as cs:
                cwf = cs.enter_context(nc.sbuf_tensor("cwf_2", [P, NCH, NE], F32))
                cwx = cs.enter_context(nc.sbuf_tensor("cwx_2", [16, TX], BF16))
                with nc.Block() as blk:
                    @blk.gpsimd
                    def _(g):
                        g.dma_start(cwx[NE:16, :], onesr[:1, :]).then_inc(s_in.h, 16)
                        s_in.n += 16
                        g.wait_ge(s_in.h, s_in.n)
                phase_ffn(HALO, TOWN, [(0, 1024), (1024, 1024)], cwf, cwx, stream_out=True)
                if t_cw3 is not None:
                    tap3(t_cw3, cwf[:, :, :])

        if upto < 6:
            with nc.Block() as blk:
                @blk.sync
                def _(sy):
                    sy.dma_start(outT.rearrange("(c p) t -> p c t", p=P),
                                 R[:, :, HALO:HALO + TOWN]).then_inc(s_hw.h, 16)
                    s_hw.n += 16
                    sy.wait_ge(s_hw.h, s_hw.n)

    return nc, tapnames


def make_masks(core):
    r = np.arange(P)[:, None]
    jj = np.arange(256)[None, :]
    band = (jj - r >= 32) & (jj - r <= 96)
    mi = np.where(band, 0.0, NEG).astype(np.float32)
    ma = np.where(band & (jj >= 64), 0.0, NEG).astype(np.float32) if core == 0 else mi
    mc = np.where(band & (jj < 192), 0.0, NEG).astype(np.float32) if core == NCORE - 1 else mi
    st = np.stack([ma, mi, mc]).astype(np.float32)
    return np.ascontiguousarray(np.tile(st, (1, 1, 4)))


def prep_inputs(L0_attn_x, L0_attn_Wq, L0_attn_bq, L0_attn_Wk, L0_attn_bk,
                L0_attn_Wv, L0_attn_bv, L0_attn_Wo, L0_attn_bo,
                L0_ffn_router_W, L0_ffn_router_b, L0_ffn_sW1, L0_ffn_sb1,
                L0_ffn_sW2, L0_ffn_sb2, L0_ffn_eW1, L0_ffn_eb1,
                L0_ffn_eW2, L0_ffn_eb2):
    x = np.asarray(L0_attn_x, dtype=np.float32).reshape(SEQ, D)
    xTf = np.ascontiguousarray(x.T)
    xTpad = np.zeros((D, SEQ + 2 * HALO), np.float32)
    xTpad[:, HALO:HALO + SEQ] = xTf
    common = {
        "WqT": np.ascontiguousarray(np.asarray(L0_attn_Wq, np.float32).T),
        "WkT": np.ascontiguousarray(np.asarray(L0_attn_Wk, np.float32).T),
        "WvT": np.ascontiguousarray(np.asarray(L0_attn_Wv, np.float32).T),
        "WoT": np.ascontiguousarray(np.asarray(L0_attn_Wo, np.float32).T),
        "bq": np.ascontiguousarray(np.asarray(L0_attn_bq, np.float32).reshape(DC, P).T),
        "bk": np.ascontiguousarray(np.asarray(L0_attn_bk, np.float32).reshape(DC, P).T),
        "bv": np.asarray(L0_attn_bv, np.float32),
        "bo": np.ascontiguousarray(np.asarray(L0_attn_bo, np.float32).reshape(DC, P).T),
        "rW": np.asarray(L0_ffn_router_W, np.float32),
        "rbias": np.asarray(L0_ffn_router_b, np.float32),
        "sW1": np.asarray(L0_ffn_sW1, np.float32),
        "sW2": np.asarray(L0_ffn_sW2, np.float32),
        "sb1": np.ascontiguousarray(np.asarray(L0_ffn_sb1, np.float32).reshape(DC, P).T),
        "sb2": np.asarray(L0_ffn_sb2, np.float32),
        "eW1": np.asarray(L0_ffn_eW1, np.float32),
        "eW2": np.asarray(L0_ffn_eW2, np.float32),
        "eb1T": np.ascontiguousarray(np.asarray(L0_ffn_eb1, np.float32).T),
        "eb2": np.asarray(L0_ffn_eb2, np.float32),
    }
    in_maps = []
    for c in range(NCORE):
        m = dict(common)
        st = c * TOWN
        m["xT"] = np.ascontiguousarray(xTpad[:, st:st + TX])
        m["masks"] = make_masks(c)
        in_maps.append(m)
    return in_maps


def kernel(**inputs):
    in_maps = prep_inputs(**inputs)
    nc, _ = build_program()
    res = run_bass_kernel_spmd(nc, in_maps, core_ids=list(range(NCORE)))
    parts = [np.asarray(r["outT"]) for r in res.results]
    full = np.concatenate(parts, axis=1)
    return np.ascontiguousarray(full.T).reshape(1, SEQ, D).astype(np.float32)


# version 27 (final)
# speedup vs baseline: 1.2045x; 1.0000x over previous
import numpy as np
from contextlib import ExitStack

import concourse.bass as bass
import concourse.mybir as mybir
from concourse.bass_utils import run_bass_kernel_spmd

F32 = mybir.dt.float32
BF16 = mybir.dt.bfloat16
AF = mybir.ActivationFunctionType
ALU = mybir.AluOpType
AX = mybir.AxisListType

P = 128
D = 1024
DC = D // P
TOWN = 2048
HALO = 64
TX = TOWN + 2 * HALO
NCH = TX // P
NE = 15
NH = 16
HD = 64
EPS = 1e-5
NEG = -1.0e9
NCORE = 8
SEQ = 16384


def tiles(width, maxn=512):
    out = []
    o = 0
    while o < width:
        n = min(maxn, width - o)
        out.append((o, n))
        o += n
    return out


class Ctr:
    def __init__(self, es, nc, name):
        self.h = es.enter_context(nc.semaphore(name))
        self.n = 0


def build_program(upto=99, taps=(), aq=4, anit=16, aout=True, acut=99):
    nc = bass.Bass("TRN2", target_bir_lowering=False)
    taps = set(taps)
    tapnames = []
    uid = [0]

    xT = nc.dram_tensor("xT", [D, TX], F32, kind="ExternalInput").ap()
    masks = nc.dram_tensor("masks", [3, P, 1024], F32, kind="ExternalInput").ap()
    WqT = nc.dram_tensor("WqT", [D, D], F32, kind="ExternalInput").ap()
    WkT = nc.dram_tensor("WkT", [D, D], F32, kind="ExternalInput").ap()
    WvT = nc.dram_tensor("WvT", [D, D], F32, kind="ExternalInput").ap()
    WoT = nc.dram_tensor("WoT", [D, D], F32, kind="ExternalInput").ap()
    bq = nc.dram_tensor("bq", [P, DC], F32, kind="ExternalInput").ap()
    bk = nc.dram_tensor("bk", [P, DC], F32, kind="ExternalInput").ap()
    bv = nc.dram_tensor("bv", [D], F32, kind="ExternalInput").ap()
    bo = nc.dram_tensor("bo", [P, DC], F32, kind="ExternalInput").ap()
    rW = nc.dram_tensor("rW", [D, NE], F32, kind="ExternalInput").ap()
    rbias = nc.dram_tensor("rbias", [NE], F32, kind="ExternalInput").ap()
    sW1 = nc.dram_tensor("sW1", [D, D], F32, kind="ExternalInput").ap()
    sW2 = nc.dram_tensor("sW2", [D, D], F32, kind="ExternalInput").ap()
    sb1 = nc.dram_tensor("sb1", [P, DC], F32, kind="ExternalInput").ap()
    sb2 = nc.dram_tensor("sb2", [D], F32, kind="ExternalInput").ap()
    eW1 = nc.dram_tensor("eW1", [NE, D, D], F32, kind="ExternalInput").ap()
    eW2 = nc.dram_tensor("eW2", [NE, D, D], F32, kind="ExternalInput").ap()
    eb1T = nc.dram_tensor("eb1T", [D, NE], F32, kind="ExternalInput").ap()
    eb2 = nc.dram_tensor("eb2", [NE, D], F32, kind="ExternalInput").ap()

    outT = nc.dram_tensor("outT", [D, TOWN], F32, kind="ExternalOutput").ap()

    scr_rows = nc.dram_tensor("scr_rows", [2, TX], F32).ap()
    scr_cwt = nc.dram_tensor("scr_cwt", [NE, TX], BF16).ap()

    def mktap(name, shape, dtype=F32):
        if name in taps:
            tapnames.append(name)
            return nc.dram_tensor("tap_" + name, shape, dtype, kind="ExternalOutput").ap()
        return None

    t_z1 = mktap("z1", [P, DC, TX], BF16)
    t_cw1 = mktap("cw1", [P, NCH, NE], F32)
    t_y1 = mktap("y1", [P, DC, TX], F32)
    t_z2 = mktap("z2", [P, DC, TX], BF16)
    t_kT = mktap("kT", [P, DC, 640], BF16)
    t_v = mktap("v", [P, 5, D], BF16)
    t_qT = mktap("qT", [P, DC, 512], BF16)
    t_y2 = mktap("y2", [P, DC, TX], F32)
    t_z3 = mktap("z3", [P, DC, TX], BF16)
    t_cw3 = mktap("cw3", [P, NCH, NE], F32)

    es = ExitStack()
    with es:
        R = es.enter_context(nc.sbuf_tensor("R", [P, DC, TX], F32))
        zT = es.enter_context(nc.sbuf_tensor("zT", [P, DC, TX], BF16))
        bqc = es.enter_context(nc.sbuf_tensor("bqc", [P, DC], F32))
        bkc = es.enter_context(nc.sbuf_tensor("bkc", [P, DC], F32))
        boc = es.enter_context(nc.sbuf_tensor("boc", [P, DC], F32))
        bvb = es.enter_context(nc.sbuf_tensor("bvb", [P, D], BF16))
        sb1c = es.enter_context(nc.sbuf_tensor("sb1c", [P, DC], F32))
        eb1c = es.enter_context(nc.sbuf_tensor("eb1c", [P, DC, NE], F32))
        eb2s = es.enter_context(nc.sbuf_tensor("eb2s", [16, D], BF16))
        rWb = es.enter_context(nc.sbuf_tensor("rWb", [P, DC, NE], BF16))
        rbb = es.enter_context(nc.sbuf_tensor("rbb", [P, NE], F32))
        maskb = es.enter_context(nc.sbuf_tensor("maskb", [P, 3, 1024], BF16))
        identf = es.enter_context(nc.sbuf_tensor("identf", [P, P], F32))
        identb = es.enter_context(nc.sbuf_tensor("identb", [P, P], BF16))
        ones_col = es.enter_context(nc.sbuf_tensor("ones_col", [P, 1], BF16))
        ncol = es.enter_context(nc.sbuf_tensor("ncol", [P, 1], F32))
        qcol = es.enter_context(nc.sbuf_tensor("qcol", [P, 1], BF16))
        iota15 = es.enter_context(nc.sbuf_tensor("iota15", [P, NE], F32))
        cwf = es.enter_context(nc.sbuf_tensor("cwf", [P, NCH, NE], F32))
        cst = es.enter_context(nc.sbuf_tensor("cst", [P, 4], F32))
        onesr = es.enter_context(nc.sbuf_tensor("onesr", [1, TX], BF16))
        cwx = es.enter_context(nc.sbuf_tensor("cwx", [16, TX], BF16))

        s_in = Ctr(es, nc, "s_in")
        s_wm = [Ctr(es, nc, f"s_wm{i}") for i in range(3)]
        s_hw = Ctr(es, nc, "s_hw")
        s_cb = [Ctr(es, nc, f"s_cb{i}") for i in range(2)]
        s_pe = Ctr(es, nc, "s_pe")
        s_act = Ctr(es, nc, "s_act")
        s_dve = Ctr(es, nc, "s_dve")
        s_cfree = Ctr(es, nc, "s_cfree")
        s_gp = Ctr(es, nc, "s_gp")
        s_rpe = Ctr(es, nc, "s_rpe")
        s_rdv = Ctr(es, nc, "s_rdv")
        s_rac = Ctr(es, nc, "s_rac")

        def tap3(tapap, sbap):
            with nc.Block() as blk:
                @blk.sync
                def _(sy):
                    sy.dma_start(tapap, sbap).then_inc(s_hw.h, 16)
                    s_hw.n += 16
                    sy.wait_ge(s_hw.h, s_hw.n)

        with ExitStack() as ps:
            ioA = ps.enter_context(nc.sbuf_tensor("ioA", [P, P], F32))
            ioB = ps.enter_context(nc.sbuf_tensor("ioB", [P, P], F32))
            sb2row = ps.enter_context(nc.sbuf_tensor("sb2row", [1, D], F32))
            sb2h = ps.enter_context(nc.sbuf_tensor("sb2h", [1, D], BF16))
            with nc.Block() as blk:
                @blk.gpsimd
                def _(g):
                    def d(out, in_):
                        g.dma_start(out, in_).then_inc(s_in.h, 16)
                        s_in.n += 16
                    d(R[:, 3:5, :], xT.rearrange("(c p) t -> p c t", p=P)[:, 3:5, :])
                    d(maskb[:, :, :], masks.rearrange("m p t -> p m t"))
                    d(sb2row[:1, :], sb2.rearrange("(a d) -> a d", a=1))
                    d(eb2s[0:NE, :], eb2[:, :])
                    d(bvb[:, :], bv.partition_broadcast(P))
                    d(rWb[:, :, :], rW.rearrange("(c p) e -> p c e", p=P))
                    g.iota(iota15[:, :], [[1, NE]], channel_multiplier=0,
                           allow_small_or_imprecise_dtypes=True)
                    g.iota(ioA[:, :], [[1, P]], channel_multiplier=0,
                           allow_small_or_imprecise_dtypes=True)
                    g.iota(ioB[:, :], [[0, P]], channel_multiplier=1,
                           allow_small_or_imprecise_dtypes=True)
                    g.memset(ones_col[:, :], 1.0)
                    g.memset(ncol[:, :], -1.0 / D)
                    g.memset(qcol[:, :], 1.0 / D)
                    g.memset(cst[:, 3:4], -0.5)
                    g.memset(onesr[:1, :], 1.0)
                    g.memset(cst[:, 0:1], EPS)
                    g.memset(cst[:, 1:2], 0.5)
                    g.memset(cst[:, 2:3], 0.125)
                    g.wait_ge(s_in.h, s_in.n)

                @blk.sync
                def _(sy):
                    def d2_(out, in_):
                        sy.dma_start(out, in_).then_inc(s_hw.h, 16)
                        s_hw.n += 16
                    d2_(R[:, 0:3, :], xT.rearrange("(c p) t -> p c t", p=P)[:, 0:3, :])
                    d2_(bqc[:, :], bq[:, :])
                    d2_(bkc[:, :], bk[:, :])
                    d2_(boc[:, :], bo[:, :])
                    d2_(sb1c[:, :], sb1[:, :])
                    d2_(eb1c[:, :, :], eb1T.rearrange("(c p) e -> p c e", p=P))
                    d2_(rbb[:, :], rbias.partition_broadcast(P))
                    sy.wait_ge(s_hw.h, s_hw.n + 16)
                    s_hw.n += 16

                @blk.scalar
                def _(s):
                    s.dma_start(R[:, 5:8, :],
                                xT.rearrange("(c p) t -> p c t", p=P)[:, 5:8, :]).then_inc(s_hw.h, 16)
            with nc.Block() as blk:
                @blk.vector
                def _(v):
                    v.tensor_tensor(out=identf[:, :], in0=ioA[:, :], in1=ioB[:, :], op=ALU.is_equal)
                    v.drain()
                    v.tensor_copy(identb[:, :], identf[:, :])
                    v.tensor_scalar_mul(bqc[:, :], bqc[:, :], 0.125)
                    v.tensor_scalar_mul(sb1c[:, :], sb1c[:, :], 0.5)
                @blk.scalar
                def _(s):
                    s.activation(sb2h[:1, :], sb2row[:1, :], AF.Copy, scale=cst[0:1, 1:2]).then_inc(s_act.h, 1)
                    s_act.n += 1

            with nc.Block() as blk:
                @blk.gpsimd
                def _(g):
                    g.wait_ge(s_act.h, s_act.n)
                    g.dma_start(eb2s[NE:16, :], sb2h[:1, :]).then_inc(s_in.h, 16)
                    s_in.n += 16
                    g.wait_ge(s_in.h, s_in.n)

        def phase_ln(cl0, cln):
            uid[0] += 1
            u = uid[0]
            NSEG = 4
            sw = cln // NSEG
            NDV = 5
            with ExitStack() as ps:
                ssum = ps.enter_context(nc.sbuf_tensor(f"ln_su{u}", [1, TX], F32))
                ssq = ps.enter_context(nc.sbuf_tensor(f"ln_sq{u}", [1, TX], F32))
                svar = ps.enter_context(nc.sbuf_tensor(f"ln_va{u}", [1, TX], F32))
                slog = ps.enter_context(nc.sbuf_tensor(f"ln_lg{u}", [1, TX], F32))
                srstd = ps.enter_context(nc.sbuf_tensor(f"ln_sr{u}", [1, TX], F32))
                mB = ps.enter_context(nc.sbuf_tensor(f"ln_mb{u}", [P, TX], F32))
                rB = ps.enter_context(nc.sbuf_tensor(f"ln_rb{u}", [P, TX], F32))
                gsq = ps.enter_context(nc.sbuf_tensor(f"ln_gsq{u}", [P, DC, 2 * sw], BF16))
                tl = tiles(sw, 512)
                nt = len(tl)
                psA = [ps.enter_context(nc.psum_tensor(f"ln_psA{u}_{i}", [1, 512], F32))
                       for i in range(nt)]
                psB = [ps.enter_context(nc.psum_tensor(f"ln_psB{u}_{i}", [1, 512], F32))
                       for i in range(nt)]
                act0, pe0, dve0 = s_act.n, s_pe.n, s_dve.n
                hw0n = s_hw.n
                order = [("q", 0)]
                for sg in range(1, NSEG):
                    order.append(("q", sg))
                    order.append(("e", sg - 1))
                order.append(("e", NSEG - 1))
                sq_inc = {sg: order.index(("q", sg)) + 1 for sg in range(NSEG)}
                exp_inc = {sg: order.index(("e", sg)) + 1 for sg in range(NSEG)}
                with nc.Block() as blk:
                    @blk.scalar
                    def _(s):
                        for kind, sg in order:
                            if kind == "q":
                                c0 = cl0 + sg * sw
                                go = (sg % 2) * sw
                                if sg >= 2:
                                    s.wait_ge(s_pe.h, pe0 + (2 * (sg - 2) + 2) * nt)
                                for c in range(DC):
                                    mm = s.activation(gsq[:, c, go:go + sw],
                                                      R[:, c, c0:c0 + sw], AF.Square)
                                    if c == DC - 1:
                                        mm.then_inc(s_act.h, 1)
                            else:
                                o0 = sg * sw
                                s.wait_ge(s_dve.h, dve0 + 3 * sg + 3)
                                s.activation(slog[:1, o0:o0 + sw], svar[:1, o0:o0 + sw],
                                             AF.Ln, bias=cst[0:1, 0:1])
                                s.drain()
                                s.activation(srstd[:1, o0:o0 + sw], slog[:1, o0:o0 + sw],
                                             AF.Exp, scale=cst[0:1, 3:4]).then_inc(s_act.h, 1)
                    s_act.n += 2 * NSEG

                    @blk.tensor
                    def _(t):
                        for sg in range(NSEG):
                            c0 = cl0 + sg * sw
                            go = (sg % 2) * sw
                            for ti, (to, tn) in enumerate(tl):
                                if sg >= 1 and ti == 0:
                                    t.wait_ge(s_dve.h, dve0 + 3 * (sg - 1) + 1)
                                for c in range(DC):
                                    st, sp = (c == 0), (c == DC - 1)
                                    mm = t.matmul(psA[ti][0:1, 0:tn], ncol[:, :],
                                                  R[:, c, c0 + to: c0 + to + tn],
                                                  start=st, stop=sp)
                                    if sp:
                                        mm.then_inc(s_pe.h, 1)
                            t.wait_ge(s_act.h, act0 + sq_inc[sg])
                            for ti, (to, tn) in enumerate(tl):
                                if sg >= 1 and ti == 0:
                                    t.wait_ge(s_dve.h, dve0 + 3 * (sg - 1) + 2)
                                for c in range(DC):
                                    st, sp = (c == 0), (c == DC - 1)
                                    mm = t.matmul(psB[ti][0:1, 0:tn], qcol[:, :],
                                                  gsq[:, c, go + to:go + to + tn],
                                                  start=st, stop=sp)
                                    if sp:
                                        mm.then_inc(s_pe.h, 1)
                    s_pe.n += 2 * NSEG * nt

                    @blk.vector
                    def _(v):
                        def evacs(sg, sq, ptensors, dst):
                            o0 = sg * sw
                            base = pe0 + (2 * sg + sq) * nt
                            for ti, (to, tn) in enumerate(tl):
                                v.wait_ge(s_pe.h, base + ti + 1)
                                v.tensor_copy(dst[:1, o0 + to:o0 + to + tn],
                                              ptensors[ti][0:1, 0:tn])
                            v.drain()
                            v.nop().then_inc(s_dve.h, 1)

                        def rowmath(sg):
                            rs = slice(sg * sw, (sg + 1) * sw)
                            v.tensor_tensor(out=svar[:1, rs], in0=ssum[:1, rs],
                                            in1=ssum[:1, rs], op=ALU.mult)
                            v.drain()
                            v.tensor_tensor(out=svar[:1, rs], in0=ssq[:1, rs],
                                            in1=svar[:1, rs], op=ALU.subtract)
                            v.drain()
                            v.nop().then_inc(s_dve.h, 1)

                        def npass(sg, p2):
                            c0 = cl0 + sg * sw
                            o0 = sg * sw
                            sl = slice(c0, c0 + sw)
                            v.wait_ge(s_hw.h, hw0n + 64 * sg + 32 * (p2 + 1))
                            for c in range(NDV):
                                if p2:
                                    v.tensor_tensor(out=zT[:, c, sl], in0=zT[:, c, sl],
                                                    in1=rB[:, o0:o0 + sw], op=ALU.mult)
                                else:
                                    v.tensor_tensor(out=zT[:, c, sl], in0=R[:, c, sl],
                                                    in1=mB[:, o0:o0 + sw], op=ALU.add)
                            v.drain()

                        for sg in range(NSEG):
                            evacs(sg, 0, psA, ssum)
                            evacs(sg, 1, psB, ssq)
                            rowmath(sg)
                            if sg >= 1:
                                npass(sg - 1, 1)
                            npass(sg, 0)
                        npass(NSEG - 1, 1)
                    s_dve.n += 3 * NSEG

                    @blk.sync
                    def _(sy):
                        cnt = hw0n
                        for sg in range(NSEG):
                            rs = slice(sg * sw, (sg + 1) * sw)
                            sy.wait_ge(s_dve.h, dve0 + 3 * sg + 1)
                            sy.dma_start(scr_rows[0:1, rs], ssum[:1, rs]).then_inc(s_hw.h, 16)
                            cnt += 16
                            sy.wait_ge(s_hw.h, cnt)
                            sy.dma_start(mB[:, rs],
                                         scr_rows[0, rs].partition_broadcast(P)).then_inc(s_hw.h, 16)
                            cnt += 16
                            sy.wait_ge(s_hw.h, cnt)
                            sy.wait_ge(s_act.h, act0 + exp_inc[sg])
                            sy.dma_start(scr_rows[1:2, rs], srstd[:1, rs]).then_inc(s_hw.h, 16)
                            cnt += 16
                            sy.wait_ge(s_hw.h, cnt)
                            sy.dma_start(rB[:, rs],
                                         scr_rows[1, rs].partition_broadcast(P)).then_inc(s_hw.h, 16)
                            cnt += 16
                            if sg < NSEG - 1:
                                sy.wait_ge(s_hw.h, cnt)
                        s_hw.n += 64 * NSEG

                    @blk.gpsimd
                    def _(g):
                        for sg in range(NSEG):
                            c0 = cl0 + sg * sw
                            o0 = sg * sw
                            sl = slice(c0, c0 + sw)
                            g.wait_ge(s_hw.h, hw0n + 64 * sg + 32)
                            for c in range(NDV, DC):
                                g.tensor_tensor(out=zT[:, c, sl], in0=R[:, c, sl],
                                                in1=mB[:, o0:o0 + sw], op=ALU.add)
                            g.drain()
                            g.wait_ge(s_hw.h, hw0n + 64 * sg + 64)
                            for c in range(NDV, DC):
                                g.tensor_tensor(out=zT[:, c, sl], in0=zT[:, c, sl],
                                                in1=rB[:, o0:o0 + sw], op=ALU.mult)
                            g.drain()

        def make_router(cl0, cln, cwf, cwx, ps):
            uid[0] += 1
            u = uid[0]
            nch = cln // P
            nf = nch * NE
            prt = ps.enter_context(nc.psum_tensor(f"rt_p{u}", [P, NE], F32))
            ptw = ps.enter_context(nc.psum_tensor(f"rt_t{u}", [NE, P], F32))
            logt = ps.enter_context(nc.sbuf_tensor(f"rt_lg{u}", [P, NCH, NE], F32))
            probs = ps.enter_context(nc.sbuf_tensor(f"rt_pr{u}", [P, NCH, NE], F32))
            tt1 = ps.enter_context(nc.sbuf_tensor(f"rt_t1{u}", [P, NCH, NE], F32))
            nmx = ps.enter_context(nc.sbuf_tensor(f"rt_nm{u}", [P, 1], F32))
            f17 = ps.enter_context(nc.sbuf_tensor(f"rt_f17{u}", [P, NCH], F32))
            r17 = ps.enter_context(nc.sbuf_tensor(f"rt_r17{u}", [P, NCH], F32))
            iota15m = ps.enter_context(nc.sbuf_tensor(f"rt_iom{u}", [P, NE], F32))
            pe0r, dve0r, act0r = s_rpe.n, s_rdv.n, s_rac.n
            dve1r = dve0r + nch
            dve2r = dve1r + 2
            pe1r = pe0r + nch
            st = {"cwx_ev": dve2r + nch + 1}

            L = logt.ap().rearrange("p a b -> p (a b)")[:, 0:nf].rearrange("p (a b) -> p a b", b=NE)
            PR = probs.ap().rearrange("p a b -> p (a b)")[:, 0:nf].rearrange("p (a b) -> p a b", b=NE)
            T1 = tt1.ap().rearrange("p a b -> p (a b)")[:, 0:nf].rearrange("p (a b) -> p a b", b=NE)
            CW = cwf.ap().rearrange("p a b -> p (a b)")[:, 0:nf].rearrange("p (a b) -> p a b", b=NE)
            Lf = logt.ap().rearrange("p a b -> p (a b)")[:, 0:nf]
            PRf = probs.ap().rearrange("p a b -> p (a b)")[:, 0:nf]
            T1f = tt1.ap().rearrange("p a b -> p (a b)")[:, 0:nf]
            CWf = cwf.ap().rearrange("p a b -> p (a b)")[:, 0:nf]
            io_b = iota15.ap().unsqueeze(1).broadcast_to((P, nch, NE))
            iom_b = iota15m.ap().unsqueeze(1).broadcast_to((P, nch, NE))

            def pe_logits(t):
                for k in range(nch):
                    if k >= 1:
                        t.wait_ge(s_rdv.h, dve0r + k)
                    for c in range(DC):
                        mm = t.matmul(prt[:, :], zT[:, c, cl0 + k * P: cl0 + (k + 1) * P],
                                      rWb[:, c, :], start=(c == 0), stop=(c == DC - 1))
                        if c == DC - 1:
                            mm.then_inc(s_rpe.h, 1)
                s_rpe.n += nch

            def pe_trans(t):
                t.wait_ge(s_rdv.h, dve1r + 2)
                for k in range(nch):
                    if k >= 1:
                        t.wait_ge(s_rdv.h, dve2r + k)
                    t.transpose(ptw[0:NE, :], cwf[:, k, :], identf[:, :]).then_inc(s_rpe.h, 1)
                s_rpe.n += nch

            def dve(v):
                v.tensor_scalar_mul(iota15m[:, :], iota15[:, :], 1.0)
                v.drain()
                v.tensor_scalar_add(out=iota15m[:, :], in0=iota15m[:, :], scalar1=-100.0)
                v.drain()
                for k in range(nch):
                    v.wait_ge(s_rpe.h, pe0r + k + 1)
                    v.tensor_tensor(out=logt[:, k, :], in0=prt[:, :], in1=rbb[:, :], op=ALU.add)
                    v.drain()
                    v.nop().then_inc(s_rdv.h, 1)
                v.tensor_reduce(out=nmx[:, 0:1], in_=Lf, axis=AX.X, op=ALU.max, negate=True)
                v.drain()
                v.nop().then_inc(s_rdv.h, 1)
                v.wait_ge(s_rac.h, act0r + 1)
                v.tensor_reduce(out=f17[:, 0:nch], in_=PR, axis=AX.X, op=ALU.add)
                v.drain()
                v.reciprocal(r17[:, 0:nch], f17[:, 0:nch])
                v.drain()
                v.tensor_tensor(out=PR, in0=PR,
                                in1=r17[:, 0:nch].unsqueeze(2).broadcast_to((P, nch, NE)), op=ALU.mult)
                v.drain()
                v.memset(CWf, 0.0)
                v.drain()
                for it in range(3):
                    v.tensor_reduce(out=f17[:, 0:nch], in_=PR, axis=AX.X, op=ALU.max, negate=True)
                    v.drain()
                    fb = f17[:, 0:nch].unsqueeze(2).broadcast_to((P, nch, NE))
                    v.tensor_tensor(out=T1, in0=PR, in1=fb, op=ALU.add)
                    v.drain()
                    v.tensor_single_scalar(out=T1f, in_=T1f, scalar=0.0, op=ALU.is_equal)
                    v.drain()
                    v.tensor_tensor(out=T1, in0=T1, in1=iom_b, op=ALU.mult)
                    v.drain()
                    v.tensor_scalar_add(out=T1f, in0=T1f, scalar1=100.0)
                    v.drain()
                    v.tensor_reduce(out=f17[:, 0:nch], in_=T1, axis=AX.X, op=ALU.min)
                    v.drain()
                    fb = f17[:, 0:nch].unsqueeze(2).broadcast_to((P, nch, NE))
                    v.tensor_tensor(out=T1, in0=io_b, in1=fb, op=ALU.subtract)
                    v.drain()
                    v.tensor_single_scalar(out=T1f, in_=T1f, scalar=0.0, op=ALU.is_equal)
                    v.drain()
                    v.tensor_reduce(out=r17[:, 0:nch], in_=PR, axis=AX.X, op=ALU.max, negate=True)
                    v.drain()
                    rb2 = r17[:, 0:nch].unsqueeze(2).broadcast_to((P, nch, NE))
                    v.tensor_tensor(out=T1, in0=T1, in1=rb2, op=ALU.mult)
                    v.drain()
                    v.tensor_tensor(out=CW, in0=CW, in1=T1, op=ALU.subtract)
                    v.drain()
                    v.tensor_tensor(out=PR, in0=PR, in1=T1, op=ALU.add)
                    v.drain()
                v.tensor_scalar_mul(CWf, CWf, 0.5)
                v.drain()
                v.nop().then_inc(s_rdv.h, 1)
                for k in range(nch):
                    v.wait_ge(s_rpe.h, pe1r + k + 1)
                    v.tensor_copy(cwx[0:NE, cl0 + k * P: cl0 + (k + 1) * P], ptw[0:NE, :])
                    v.drain()
                    v.nop().then_inc(s_rdv.h, 1)
                v.nop().then_inc(s_rdv.h, 1)
                s_rdv.n += nch + 2 + nch + 1

            def act(sc):
                sc.wait_ge(s_rdv.h, dve1r + 1)
                sc.activation(PRf, Lf, AF.Exp, bias=nmx[:, 0:1]).then_inc(s_rac.h, 1)
                s_rac.n += 1

            def sync(sy):
                sy.wait_ge(s_rdv.h, st["cwx_ev"])
                sy.dma_start(scr_cwt[:, cl0:cl0 + cln], cwx[0:NE, cl0:cl0 + cln]).then_inc(s_hw.h, 16)
                s_hw.n += 16
                sy.wait_ge(s_hw.h, s_hw.n)

            return {"pe_logits": pe_logits, "pe_trans": pe_trans, "dve": dve,
                    "act": act, "sync": sync, "cwx_ev": st["cwx_ev"]}

        def phase_ffn(cl0, cln, halves, cwf, cwx, stream_out=False):
            uid[0] += 1
            u = uid[0]
            nmat = 2 * (NE + 1)
            nh = len(halves)
            with ExitStack() as ps:
                pA = ps.enter_context(nc.psum_tensor(f"ff_pA{u}", [P, 1152], F32))
                pB = ps.enter_context(nc.psum_tensor(f"ff_pB{u}", [P, 1152], F32))
                w0 = ps.enter_context(nc.sbuf_tensor(f"ff_w0{u}", [P, DC, D], BF16))
                w1 = ps.enter_context(nc.sbuf_tensor(f"ff_w1{u}", [P, DC, D], BF16))
                w2 = ps.enter_context(nc.sbuf_tensor(f"ff_w2{u}", [P, DC, D], BF16))
                wb = [w0, w1, w2]
                gbuf = ps.enter_context(nc.sbuf_tensor(f"ff_g{u}", [P, DC, 1152], BF16))
                rb0 = ps.enter_context(nc.sbuf_tensor(f"ff_r0{u}", [P, 1152], BF16))
                rb1 = ps.enter_context(nc.sbuf_tensor(f"ff_r1{u}", [P, 1152], BF16))
                rbufs = [rb0, rb1]
                cb0 = ps.enter_context(nc.sbuf_tensor(f"ff_c0{u}", [P, 1152], BF16))
                cb1 = ps.enter_context(nc.sbuf_tensor(f"ff_c1{u}", [P, 1152], BF16))
                cbufs = [cb0, cb1]

                rt = make_router(cl0, cln, cwf, cwx, ps)
                pe0, act0, dve0 = s_pe.n, s_act.n, s_dve.n
                cf0 = s_cfree.n
                wtgt = []
                for m in range(nmat):
                    c = s_wm[m % 3]
                    c.n += 16
                    wtgt.append(c.n)
                ctgt = {}
                for e in range(NE):
                    for hf in range(nh):
                        c = s_cb[e % 2]
                        c.n += 16
                        ctgt[(e, hf)] = c.n

                def pe_h(ei, hf, dh):
                    return pe0 + ei * 32 + hf * 16 + dh + 1

                def pe_ye(ei, hf, do):
                    return pe0 + ei * 32 + hf * 16 + 8 + do + 1

                def act_relu(ei, hf, dh):
                    return act0 + ei * 16 + hf * 8 + dh + 1

                def dve_g(ei, hf, dh):
                    return dve0 + 16 + (ei - 1) * 32 + hf * 16 + dh + 1

                def dve_R(ei, hf, do):
                    if ei == 0:
                        return dve0 + hf * 8 + do + 1
                    return dve0 + 16 + (ei - 1) * 32 + hf * 16 + 8 + do + 1

                GSEQ = []
                for ei in range(NE + 1):
                    for hf in range(nh):
                        for dh in range(8):
                            GSEQ.append(("h", ei, hf, dh))
                        for do in range(8):
                            GSEQ.append(("ye", ei, hf, do))

                def free_count(gidx):
                    kind, ei, hf, x = GSEQ[gidx]
                    if kind == "h":
                        return (s_act.h, act_relu(ei, hf, x))
                    return (s_dve.h, dve_R(ei, hf, x))

                with nc.Block() as blk:
                    @blk.gpsimd
                    def _(g):
                        for m in range(nmat):
                            ei, is2 = divmod(m, 2)
                            if ei == 0:
                                src = sW2 if is2 else sW1
                            else:
                                src = (eW2 if is2 else eW1)[ei - 1]
                            if m >= 3:
                                mp = m - 3
                                eip, is2p = divmod(mp, 2)
                                if is2p == 0:
                                    g.wait_ge(s_pe.h, pe_h(eip, nh - 1, 7))
                                else:
                                    g.wait_ge(s_pe.h, pe_ye(eip, nh - 1, 7))
                            g.dma_start(wb[m % 3][:, :, :],
                                        src.rearrange("(c p) n -> p c n", p=P)).then_inc(s_wm[m % 3].h, 16)
                        for i in range(3):
                            g.wait_ge(s_wm[i].h, s_wm[i].n)

                    @blk.sync
                    def _(sy):
                        rt["sync"](sy)
                        for e in range(NE):
                            for hf, (ho, hn) in enumerate(halves):
                                if hf >= 1:
                                    sy.wait_ge(s_cfree.h, cf0 + 2 * e + 1)
                                elif e >= 2:
                                    sy.wait_ge(s_cfree.h, cf0 + 2 * (e - 2) + 2)
                                sy.dma_start(cbufs[e % 2][:, 0:hn],
                                             scr_cwt[e, cl0 + ho: cl0 + ho + hn].partition_broadcast(P)
                                             ).then_inc(s_cb[e % 2].h, 16)
                        for i in range(2):
                            sy.wait_ge(s_cb[i].h, s_cb[i].n)
                        if stream_out:
                            for hf, (ho, hn) in enumerate(halves):
                                for do in range(8):
                                    sy.wait_ge(s_dve.h, dve_R(NE, hf, do))
                                    sy.dma_start(
                                        outT.rearrange("(c p) t -> p c t", p=P)[:, do, ho: ho + hn],
                                        R[:, do, cl0 + ho: cl0 + ho + hn]).then_inc(s_hw.h, 16)
                                    s_hw.n += 16
                            sy.wait_ge(s_hw.h, s_hw.n)

                    @blk.tensor
                    def _(t):
                        rt["pe_logits"](t)
                        ebwait = [True]
                        gidx = 0
                        for ei in range(NE + 1):
                            m1, m2 = 2 * ei, 2 * ei + 1
                            for hf, (ho, hn) in enumerate(halves):
                                tl = tiles(hn, 512)
                                if hf == 0:
                                    t.wait_ge(s_wm[m1 % 3].h, wtgt[m1])
                                for dh in range(8):
                                    bank = pA if gidx % 2 == 0 else pB
                                    if gidx >= 2:
                                        sh, cn = free_count(gidx - 2)
                                        t.wait_ge(sh, cn)
                                    for c in range(DC):
                                        for ti, (to, tn) in enumerate(tl):
                                            mm = t.matmul(bank[:, to:to + tn],
                                                          wb[m1 % 3][:, c, dh * P:(dh + 1) * P],
                                                          zT[:, c, cl0 + ho + to: cl0 + ho + to + tn],
                                                          start=(c == 0), stop=(c == DC - 1))
                                            if c == DC - 1 and ti == len(tl) - 1:
                                                mm.then_inc(s_pe.h, 1)
                                    gidx += 1
                                if ei == 0 and hf == 0:
                                    rt["pe_trans"](t)
                                if hf == 0:
                                    t.wait_ge(s_wm[m2 % 3].h, wtgt[m2])
                                for do in range(8):
                                    bank = pA if gidx % 2 == 0 else pB
                                    if gidx >= 2:
                                        sh, cn = free_count(gidx - 2)
                                        t.wait_ge(sh, cn)
                                    for dh in range(8):
                                        if do == 0:
                                            if ei == 0:
                                                t.wait_ge(s_act.h, act_relu(0, hf, dh))
                                            else:
                                                t.wait_ge(s_dve.h, dve_g(ei, hf, dh))
                                        for ti, (to, tn) in enumerate(tl):
                                            last = (dh == DC - 1) and (ei != 0)
                                            mm = t.matmul(bank[:, to:to + tn],
                                                          wb[m2 % 3][:, dh, do * P:(do + 1) * P],
                                                          gbuf[:, dh, to:to + tn],
                                                          start=(dh == 0), stop=last)
                                            if last and ti == len(tl) - 1:
                                                mm.then_inc(s_pe.h, 1)
                                    if ei == 0:
                                        if ebwait[0]:
                                            t.wait_ge(s_rdv.h, rt["cwx_ev"])
                                            ebwait[0] = False
                                        for ti, (to, tn) in enumerate(tl):
                                            mm = t.matmul(bank[:, to:to + tn],
                                                          eb2s[:, do * P:(do + 1) * P],
                                                          cwx[:, cl0 + ho + to: cl0 + ho + to + tn],
                                                          start=False, stop=True)
                                            if ti == len(tl) - 1:
                                                mm.then_inc(s_pe.h, 1)
                                    gidx += 1
                    s_pe.n += (NE + 1) * nh * 16

                    @blk.scalar
                    def _(s):
                        rt["act"](s)
                        for ei in range(NE + 1):
                            for hf, (ho, hn) in enumerate(halves):
                                for dh in range(8):
                                    gidx = ei * 32 + hf * 16 + dh
                                    bank = pA if gidx % 2 == 0 else pB
                                    s.wait_ge(s_pe.h, pe_h(ei, hf, dh))
                                    if ei == 0:
                                        s.activation(gbuf[:, dh, 0:hn], bank[:, 0:hn], AF.Relu,
                                                     bias=sb1c[:, dh:dh + 1],
                                                     scale=cst[:, 1:2]).then_inc(s_act.h, 1)
                                    else:
                                        ridx = (ei - 1) * 16 + hf * 8 + dh
                                        if ridx >= 2:
                                            e2, r2 = divmod(ridx - 2, 16)
                                            hf2, dh2 = divmod(r2, 8)
                                            s.wait_ge(s_dve.h, dve_g(e2 + 1, hf2, dh2))
                                        s.activation(rbufs[ridx % 2][:, 0:hn], bank[:, 0:hn], AF.Relu,
                                                     bias=eb1c[:, dh, ei - 1:ei]).then_inc(s_act.h, 1)
                    s_act.n += (NE + 1) * nh * 8

                    @blk.vector
                    def _(v):
                        rt["dve"](v)
                        for ei in range(NE + 1):
                            for hf, (ho, hn) in enumerate(halves):
                                if ei >= 1:
                                    e = ei - 1
                                    v.wait_ge(s_cb[e % 2].h, ctgt[(e, hf)])
                                    for dh in range(8):
                                        ridx = e * 16 + hf * 8 + dh
                                        v.wait_ge(s_act.h, act_relu(ei, hf, dh))
                                        v.tensor_tensor(out=gbuf[:, dh, 0:hn], in0=rbufs[ridx % 2][:, 0:hn],
                                                        in1=cbufs[e % 2][:, 0:hn], op=ALU.mult)
                                        v.drain()
                                        v.nop().then_inc(s_dve.h, 1)
                                    v.nop().then_inc(s_cfree.h, 1)
                                for do in range(8):
                                    v.wait_ge(s_pe.h, pe_ye(ei, hf, do))
                                    gidx = ei * 32 + hf * 16 + 8 + do
                                    bank = pA if gidx % 2 == 0 else pB
                                    v.tensor_tensor(out=R[:, do, cl0 + ho: cl0 + ho + hn],
                                                    in0=bank[:, 0:hn],
                                                    in1=R[:, do, cl0 + ho: cl0 + ho + hn], op=ALU.add)
                                    v.drain()
                                    v.nop().then_inc(s_dve.h, 1)
                    s_dve.n += NE * nh * 8 + (NE + 1) * nh * 8
                    s_cfree.n += NE * nh

        def phase_attn():
            uid[0] += 1
            u = uid[0]
            with ExitStack() as ps:
                pQ = [[ps.enter_context(nc.psum_tensor(f"at_S{i}{k}{u}", [P, 512], F32))
                       for k in range(2)] for i in range(2)]
                pT2 = [ps.enter_context(nc.psum_tensor(f"at_T{i}{u}", [P, 8, P], BF16)) for i in range(2)]
                pC2 = [ps.enter_context(nc.psum_tensor(f"at_C{i}{u}", [P, 2, P], F32)) for i in range(2)]
                pSa = pQ[0][0].ap()
                pSb = pQ[0][1].ap()
                wA = ps.enter_context(nc.sbuf_tensor(f"at_wA{u}", [P, DC, D], BF16))
                wB = ps.enter_context(nc.sbuf_tensor(f"at_wB{u}", [P, DC, D], BF16))
                kTb = ps.enter_context(nc.sbuf_tensor(f"at_kT{u}", [P, DC, 640], BF16))
                vbuf = ps.enter_context(nc.sbuf_tensor(f"at_v{u}", [P, 5, D], BF16))
                qTb = ps.enter_context(nc.sbuf_tensor(f"at_qT{u}", [P, DC, 512], BF16))
                ctxT = ps.enter_context(nc.sbuf_tensor(f"at_cT{u}", [P, DC, 512], BF16))
                esc = [ps.enter_context(nc.sbuf_tensor(f"at_es{i}{u}", [P, 4, 256], BF16)) for i in range(3)]
                nmx2 = [ps.enter_context(nc.sbuf_tensor(f"at_n2{i}{u}", [P, 2], F32)) for i in range(2)]
                atb = [ps.enter_context(nc.sbuf_tensor(f"at_ab{i}{u}", [P, 4, 256], BF16)) for i in range(2)]
                atT = [ps.enter_context(nc.sbuf_tensor(f"at_aT{i}{u}", [P, 8, P], BF16)) for i in range(2)]
                nmx = [ps.enter_context(nc.sbuf_tensor(f"at_nm{i}{u}", [P, 1], F32)) for i in range(2)]
                sro = [ps.enter_context(nc.sbuf_tensor(f"at_sr{i}{u}", [P, 4], F32)) for i in range(2)]
                rec = [ps.enter_context(nc.sbuf_tensor(f"at_rc{i}{u}", [P, 4], F32)) for i in range(2)]

                prev_e3 = [None]
                prev_e4 = [None]
                prev_pe_s_last = [None]
                prev_pe_c_last = [None]
                prev_dve_out = [None]
                qblk = nc.Block()
                blk = qblk.__enter__()
                for qh in range(aq):
                    kc0 = 512 * qh
                    qc0 = HALO + 512 * qh
                    awt = []
                    for mi in range(4):
                        c = s_wm[mi % 2]
                        c.n += 16
                        awt.append(c.n)
                    pe0, act0, dve0 = s_pe.n, s_act.n, s_dve.n
                    ktl = tiles(640, 512)

                    def prj_free(gi, act0=act0, dve0=dve0):
                        if gi < 16:
                            return (s_act.h, act0 + gi + 1)
                        if gi < 26:
                            return (s_dve.h, dve0 + (gi - 16) + 1)
                        return (s_act.h, act0 + 16 + (gi - 26) + 1)

                    if True:
                        @blk.gpsimd
                        def _(g):
                            waits = [prev_e3[0], prev_e4[0], pe0 + 16, pe0 + 26]
                            for mi, src in enumerate((WkT, WvT, WqT, WoT)):
                                if waits[mi] is not None:
                                    g.wait_ge(s_pe.h, waits[mi])
                                g.dma_start((wA if mi % 2 == 0 else wB)[:, :, :],
                                            src.rearrange("(c p) n -> p c n", p=P)).then_inc(s_wm[mi % 2].h, 16)
                            g.wait_ge(s_wm[0].h, awt[2])
                            g.wait_ge(s_wm[1].h, awt[1])

                        @blk.tensor
                        def _(t):
                            gi = 0
                            t.wait_ge(s_wm[0].h, awt[0])
                            if prev_dve_out[0] is not None:
                                t.wait_ge(s_dve.h, prev_dve_out[0])
                            for do in range(8):
                                for (to, tn) in ktl:
                                    bank = pSa if gi % 2 == 0 else pSb
                                    if gi >= 2:
                                        sh, cn = prj_free(gi - 2)
                                        t.wait_ge(sh, cn)
                                    for c in range(DC):
                                        mm = t.matmul(bank[:, 0:tn], wA[:, c, do * P:(do + 1) * P],
                                                      zT[:, c, kc0 + to: kc0 + to + tn],
                                                      start=(c == 0), stop=(c == DC - 1))
                                        if c == DC - 1:
                                            mm.then_inc(s_pe.h, 1)
                                    gi += 1
                            t.wait_ge(s_wm[1].h, awt[1])
                            for kc in range(5):
                                for nt in range(2):
                                    bank = pSa if gi % 2 == 0 else pSb
                                    if gi >= 2:
                                        sh, cn = prj_free(gi - 2)
                                        t.wait_ge(sh, cn)
                                    for c in range(DC):
                                        mm = t.matmul(bank[:, 0:512],
                                                      zT[:, c, kc0 + kc * P: kc0 + (kc + 1) * P],
                                                      wB[:, c, nt * 512:(nt + 1) * 512],
                                                      start=(c == 0), stop=(c == DC - 1))
                                        if c == DC - 1:
                                            mm.then_inc(s_pe.h, 1)
                                    gi += 1
                            t.wait_ge(s_wm[0].h, awt[2])
                            for do in range(8):
                                bank = pSa if gi % 2 == 0 else pSb
                                if gi >= 2:
                                    sh, cn = prj_free(gi - 2)
                                    t.wait_ge(sh, cn)
                                for c in range(DC):
                                    mm = t.matmul(bank[:, 0:512], wA[:, c, do * P:(do + 1) * P],
                                                  zT[:, c, qc0: qc0 + 512],
                                                  start=(c == 0), stop=(c == DC - 1))
                                    if c == DC - 1:
                                        mm.then_inc(s_pe.h, 1)
                                gi += 1
                        s_pe.n += 34
                        prev_e3[0] = pe0 + 34

                        @blk.scalar
                        def _(s):
                            if prev_pe_s_last[0] is not None:
                                s.wait_ge(s_pe.h, prev_pe_s_last[0])
                            gi = 0
                            for do in range(8):
                                for (to, tn) in ktl:
                                    bank = pSa if gi % 2 == 0 else pSb
                                    s.wait_ge(s_pe.h, pe0 + gi + 1)
                                    s.activation(kTb[:, do, to:to + tn], bank[:, 0:tn], AF.Identity,
                                                 bias=bkc[:, do:do + 1]).then_inc(s_act.h, 1)
                                    gi += 1
                            gi = 26
                            for do in range(8):
                                bank = pSa if gi % 2 == 0 else pSb
                                s.wait_ge(s_pe.h, pe0 + gi + 1)
                                s.activation(qTb[:, do, 0:512], bank[:, 0:512], AF.Identity,
                                             bias=bqc[:, do:do + 1], scale=cst[:, 2:3]).then_inc(s_act.h, 1)
                                gi += 1
                        s_act.n += 24

                        @blk.vector
                        def _(v):
                            if prev_pe_c_last[0] is not None:
                                v.wait_ge(s_pe.h, prev_pe_c_last[0])
                            gi = 16
                            for kc in range(5):
                                for nt in range(2):
                                    bank = pSa if gi % 2 == 0 else pSb
                                    v.wait_ge(s_pe.h, pe0 + gi + 1)
                                    v.tensor_tensor(out=vbuf[:, kc, nt * 512:(nt + 1) * 512],
                                                    in0=bank[:, 0:512],
                                                    in1=bvb[:, nt * 512:(nt + 1) * 512], op=ALU.add)
                                    v.drain()
                                    v.nop().then_inc(s_dve.h, 1)
                                    gi += 1
                        s_dve.n += 10

                    pe1, act1, dve1, gp1 = s_pe.n, s_act.n, s_dve.n, s_gp.n
                    NIT = anit

                    def pe_s(j):
                        return pe1 + 3 * j + 1

                    def pe_t(j):
                        return pe1 + 3 * (j + 3) + 2

                    def pe_c(j):
                        return pe1 + 3 * (j + 4) + 3

                    def d1(j):
                        return dve1 + 2 * j + 1

                    def d2(j):
                        return dve1 + 2 * (j + 1) + 2

                    def g_n(j):
                        return gp1 + j + 3

                    def a2(j):
                        return act1 + 3 * (j + 3) + 1

                    def a3(j):
                        return act1 + 3 * (j + 5) + 2

                    def a1(j):
                        return act1 + 3 * j + 3

                    NC2 = NIT + 5
                    act1v, dve1v = s_act.n, s_dve.n

                    if True:
                        @blk.tensor
                        def _(t):
                            t.wait_ge(s_act.h, act1v)
                            t.wait_ge(s_dve.h, dve1v)
                            for i in range(NC2):
                                if 0 <= i < NIT:
                                    j = i
                                    pl, hq = divmod(j, 4)
                                    p_glob = 4 * qh + pl
                                    midx = 0 if p_glob == 0 else (2 if p_glob == 15 else 1)
                                    if j >= 2:
                                        t.wait_ge(s_act.h, a1(j - 2))

                                    def score_mm(e, stop):
                                        h = 4 * hq + e
                                        hp, dc = h % 2, h // 2
                                        return t.matmul(pQ[j % 2][e % 2][:, (e // 2) * 256:(e // 2 + 1) * 256],
                                                        qTb[hp * HD:(hp + 1) * HD, dc, pl * P:(pl + 1) * P],
                                                        kTb[hp * HD:(hp + 1) * HD, dc, pl * P: pl * P + 256],
                                                        start=False, stop=stop)
                                    t.matmul(pQ[j % 2][0][:, :], identb[:, :],
                                             maskb[:, midx, 0:512], start=True, stop=False)
                                    score_mm(0, False)
                                    score_mm(2, True)
                                    t.matmul(pQ[j % 2][1][:, :], identb[:, :],
                                             maskb[:, midx, 512:1024], start=True, stop=False)
                                    score_mm(1, False)
                                    score_mm(3, True).then_inc(s_pe.h, 1)
                                else:
                                    t.drain()
                                    t.nop().then_inc(s_pe.h, 1)
                                if 0 <= i - 3 < NIT:
                                    j = i - 3
                                    t.wait_ge(s_gp.h, g_n(j))
                                    if j >= 2:
                                        t.wait_ge(s_act.h, a2(j - 2))
                                    pT = pT2[j % 2]
                                    for sidx in range(4):
                                        t.transpose(pT[:, 2 * sidx, :], atb[j % 2][:, sidx, 0:128], identb[:, :])
                                        mm = t.transpose(pT[:, 2 * sidx + 1, :], atb[j % 2][:, sidx, 128:256], identb[:, :])
                                        if sidx == 3:
                                            mm.then_inc(s_pe.h, 1)
                                else:
                                    t.drain()
                                    t.nop().then_inc(s_pe.h, 1)
                                if 0 <= i - 4 < NIT:
                                    j = i - 4
                                    pl, hq = divmod(j, 4)
                                    t.wait_ge(s_act.h, a2(j))
                                    if j >= 2:
                                        t.wait_ge(s_act.h, a3(j - 2))
                                    pC = pC2[j % 2]
                                    for sidx, e in enumerate((0, 2, 1, 3)):
                                        h = 4 * hq + e
                                        hp = h % 2
                                        t.matmul(pC[hp * HD:(hp + 1) * HD, e // 2, :],
                                                 vbuf[:, pl, h * HD:(h + 1) * HD],
                                                 atT[j % 2][:, 2 * sidx, :], start=True, stop=False)
                                        mm = t.matmul(pC[hp * HD:(hp + 1) * HD, e // 2, :],
                                                      vbuf[:, pl + 1, h * HD:(h + 1) * HD],
                                                      atT[j % 2][:, 2 * sidx + 1, :], start=False, stop=True)
                                        if sidx == 3:
                                            mm.then_inc(s_pe.h, 1)
                                else:
                                    t.drain()
                                    t.nop().then_inc(s_pe.h, 1)
                        s_pe.n += 3 * NC2

                        @blk.vector
                        def _(v):
                            for i in range(NC2):
                                if 0 <= i < NIT:
                                    j = i
                                    v.wait_ge(s_pe.h, pe_s(j))
                                    if j >= 2:
                                        v.wait_ge(s_act.h, a1(j - 2))
                                    v.tensor_reduce(out=nmx2[j % 2][:, 0:1], in_=pQ[j % 2][0][:, :],
                                                    axis=AX.X, op=ALU.max, negate=True)
                                    mm = v.tensor_reduce(out=nmx2[j % 2][:, 1:2], in_=pQ[j % 2][1][:, :],
                                                         axis=AX.X, op=ALU.max, negate=True)
                                    mm.then_inc(s_dve.h, 1)
                                else:
                                    v.drain()
                                    v.nop().then_inc(s_dve.h, 1)
                                if 0 <= i - 1 < NIT:
                                    j = i - 1
                                    v.wait_ge(s_act.h, a1(j))
                                    if j >= 2:
                                        v.wait_ge(s_gp.h, g_n(j - 2))
                                    v.tensor_reduce(out=sro[j % 2][:, 0:2], in_=esc[j % 3][:, 0:2, :],
                                                    axis=AX.X, op=ALU.add)
                                    v.tensor_reduce(out=sro[j % 2][:, 2:4], in_=esc[j % 3][:, 2:4, :],
                                                    axis=AX.X, op=ALU.add)
                                    v.drain()
                                    v.reciprocal(rec[j % 2][:, :], sro[j % 2][:, :])
                                    v.drain()
                                    v.nop().then_inc(s_dve.h, 1)
                                else:
                                    v.drain()
                                    v.nop().then_inc(s_dve.h, 1)
                        s_dve.n += 2 * NC2

                        @blk.gpsimd
                        def _(g):
                            for i in range(NC2):
                                if 0 <= i - 2 < NIT:
                                    j = i - 2
                                    g.wait_ge(s_dve.h, d2(j))
                                    if j >= 2:
                                        g.wait_ge(s_pe.h, pe_t(j - 2))
                                    b_ap = rec[j % 2][:, :].unsqueeze(2).broadcast_to((P, 4, 256))
                                    g.tensor_tensor(out=atb[j % 2][:, :, :], in0=esc[j % 3][:, :, :],
                                                    in1=b_ap, op=ALU.mult)
                                    g.drain()
                                    g.nop().then_inc(s_gp.h, 1)
                                else:
                                    g.drain()
                                    g.nop().then_inc(s_gp.h, 1)
                        s_gp.n += NC2

                        @blk.scalar
                        def _(s):
                            for i in range(NC2):
                                if 0 <= i - 3 < NIT:
                                    j = i - 3
                                    s.wait_ge(s_pe.h, pe_t(j))
                                    if j >= 2:
                                        s.wait_ge(s_pe.h, pe_c(j - 2))
                                    s.copy(atT[j % 2][:, :, :], pT2[j % 2][:, :, :]).then_inc(s_act.h, 1)
                                else:
                                    s.drain()
                                    s.nop().then_inc(s_act.h, 1)
                                if 0 <= i - 5 < NIT:
                                    j = i - 5
                                    pl, hq = divmod(j, 4)
                                    s.wait_ge(s_pe.h, pe_c(j))
                                    s.copy(ctxT[:, 2 * hq:2 * hq + 2, pl * P:(pl + 1) * P],
                                           pC2[j % 2][:, :, :]).then_inc(s_act.h, 1)
                                else:
                                    s.drain()
                                    s.nop().then_inc(s_act.h, 1)
                                if 0 <= i < NIT:
                                    j = i
                                    s.wait_ge(s_dve.h, d1(j))
                                    if j >= 3:
                                        s.wait_ge(s_gp.h, g_n(j - 3))
                                        s.wait_ge(s_dve.h, d2(j - 3))
                                    s.activation(esc[j % 3][:, 0:2, :], pQ[j % 2][0][:, :],
                                                 AF.Exp, bias=nmx2[j % 2][:, 0:1])
                                    s.activation(esc[j % 3][:, 2:4, :], pQ[j % 2][1][:, :],
                                                 AF.Exp, bias=nmx2[j % 2][:, 1:2]).then_inc(s_act.h, 1)
                                else:
                                    s.drain()
                                    s.nop().then_inc(s_act.h, 1)
                        s_act.n += 3 * NC2

                    prev_pe_s_last[0] = pe_s(NIT - 1)
                    prev_pe_c_last[0] = pe_c(NIT - 1)
                    if not aout:
                        prev_e3[0] = None
                        prev_e4[0] = None
                        continue
                    pe2, dve2 = s_pe.n, s_dve.n
                    a3last = a3(NIT - 1)
                    if True:
                        @blk.tensor
                        def _(t):
                            t.wait_ge(s_wm[1].h, awt[3])
                            t.wait_ge(s_act.h, a3last)
                            for do in range(8):
                                bank = pSa if do % 2 == 0 else pSb
                                if do >= 2:
                                    t.wait_ge(s_dve.h, dve2 + do - 1)
                                for c in range(DC):
                                    mm = t.matmul(bank[:, 0:512], wB[:, c, do * P:(do + 1) * P],
                                                  ctxT[:, c, 0:512], start=(c == 0), stop=(c == DC - 1))
                                    if c == DC - 1:
                                        mm.then_inc(s_pe.h, 1)
                        s_pe.n += 8
                        prev_e4[0] = pe2 + 8

                        @blk.vector
                        def _(v):
                            for do in range(8):
                                bank = pSa if do % 2 == 0 else pSb
                                v.wait_ge(s_pe.h, pe2 + do + 1)
                                v.scalar_tensor_tensor(out=R[:, do, qc0:qc0 + 512],
                                                       in0=bank[:, 0:512], scalar=boc[:, do:do + 1],
                                                       in1=R[:, do, qc0:qc0 + 512],
                                                       op0=ALU.add, op1=ALU.add)
                                v.drain()
                                v.nop().then_inc(s_dve.h, 1)
                        s_dve.n += 8
                    prev_dve_out[0] = s_dve.n
                qblk.__exit__(None, None, None)

        if upto >= 1:
            phase_ln(0, TX)
            if t_z1 is not None:
                tap3(t_z1, zT[:, :, :])
        if upto >= 2:
            with ExitStack() as cs:
                cwf = cs.enter_context(nc.sbuf_tensor("cwf_1", [P, NCH, NE], F32))
                cwx = cs.enter_context(nc.sbuf_tensor("cwx_1", [16, TX], BF16))
                with nc.Block() as blk:
                    @blk.gpsimd
                    def _(g):
                        g.dma_start(cwx[NE:16, :], onesr[:1, :]).then_inc(s_in.h, 16)
                        s_in.n += 16
                        g.wait_ge(s_in.h, s_in.n)
                if upto >= 3:
                    phase_ffn(0, TX, [(0, 1152), (1152, 1024)], cwf, cwx)
                    if t_cw1 is not None:
                        tap3(t_cw1, cwf[:, :, :])
            if upto >= 3 and t_y1 is not None:
                tap3(t_y1, R[:, :, :])
        if upto >= 4:
            phase_ln(0, TX)
            if t_z2 is not None:
                tap3(t_z2, zT[:, :, :])
        if upto >= 5:
            phase_attn()
            if t_y2 is not None:
                tap3(t_y2, R[:, :, :])
        if upto >= 6:
            phase_ln(HALO, TOWN)
            if t_z3 is not None:
                tap3(t_z3, zT[:, :, :])
            with ExitStack() as cs:
                cwf = cs.enter_context(nc.sbuf_tensor("cwf_2", [P, NCH, NE], F32))
                cwx = cs.enter_context(nc.sbuf_tensor("cwx_2", [16, TX], BF16))
                with nc.Block() as blk:
                    @blk.gpsimd
                    def _(g):
                        g.dma_start(cwx[NE:16, :], onesr[:1, :]).then_inc(s_in.h, 16)
                        s_in.n += 16
                        g.wait_ge(s_in.h, s_in.n)
                phase_ffn(HALO, TOWN, [(0, 1024), (1024, 1024)], cwf, cwx, stream_out=True)
                if t_cw3 is not None:
                    tap3(t_cw3, cwf[:, :, :])

        if upto < 6:
            with nc.Block() as blk:
                @blk.sync
                def _(sy):
                    sy.dma_start(outT.rearrange("(c p) t -> p c t", p=P),
                                 R[:, :, HALO:HALO + TOWN]).then_inc(s_hw.h, 16)
                    s_hw.n += 16
                    sy.wait_ge(s_hw.h, s_hw.n)

    return nc, tapnames


def make_masks(core):
    r = np.arange(P)[:, None]
    jj = np.arange(256)[None, :]
    band = (jj - r >= 32) & (jj - r <= 96)
    mi = np.where(band, 0.0, NEG).astype(np.float32)
    ma = np.where(band & (jj >= 64), 0.0, NEG).astype(np.float32) if core == 0 else mi
    mc = np.where(band & (jj < 192), 0.0, NEG).astype(np.float32) if core == NCORE - 1 else mi
    st = np.stack([ma, mi, mc]).astype(np.float32)
    return np.ascontiguousarray(np.tile(st, (1, 1, 4)))


def prep_inputs(L0_attn_x, L0_attn_Wq, L0_attn_bq, L0_attn_Wk, L0_attn_bk,
                L0_attn_Wv, L0_attn_bv, L0_attn_Wo, L0_attn_bo,
                L0_ffn_router_W, L0_ffn_router_b, L0_ffn_sW1, L0_ffn_sb1,
                L0_ffn_sW2, L0_ffn_sb2, L0_ffn_eW1, L0_ffn_eb1,
                L0_ffn_eW2, L0_ffn_eb2):
    x = np.asarray(L0_attn_x, dtype=np.float32).reshape(SEQ, D)
    xTf = np.ascontiguousarray(x.T)
    xTpad = np.zeros((D, SEQ + 2 * HALO), np.float32)
    xTpad[:, HALO:HALO + SEQ] = xTf
    common = {
        "WqT": np.ascontiguousarray(np.asarray(L0_attn_Wq, np.float32).T),
        "WkT": np.ascontiguousarray(np.asarray(L0_attn_Wk, np.float32).T),
        "WvT": np.ascontiguousarray(np.asarray(L0_attn_Wv, np.float32).T),
        "WoT": np.ascontiguousarray(np.asarray(L0_attn_Wo, np.float32).T),
        "bq": np.ascontiguousarray(np.asarray(L0_attn_bq, np.float32).reshape(DC, P).T),
        "bk": np.ascontiguousarray(np.asarray(L0_attn_bk, np.float32).reshape(DC, P).T),
        "bv": np.asarray(L0_attn_bv, np.float32),
        "bo": np.ascontiguousarray(np.asarray(L0_attn_bo, np.float32).reshape(DC, P).T),
        "rW": np.asarray(L0_ffn_router_W, np.float32),
        "rbias": np.asarray(L0_ffn_router_b, np.float32),
        "sW1": np.asarray(L0_ffn_sW1, np.float32),
        "sW2": np.asarray(L0_ffn_sW2, np.float32),
        "sb1": np.ascontiguousarray(np.asarray(L0_ffn_sb1, np.float32).reshape(DC, P).T),
        "sb2": np.asarray(L0_ffn_sb2, np.float32),
        "eW1": np.asarray(L0_ffn_eW1, np.float32),
        "eW2": np.asarray(L0_ffn_eW2, np.float32),
        "eb1T": np.ascontiguousarray(np.asarray(L0_ffn_eb1, np.float32).T),
        "eb2": np.asarray(L0_ffn_eb2, np.float32),
    }
    in_maps = []
    for c in range(NCORE):
        m = dict(common)
        st = c * TOWN
        m["xT"] = np.ascontiguousarray(xTpad[:, st:st + TX])
        m["masks"] = make_masks(c)
        in_maps.append(m)
    return in_maps


def kernel(**inputs):
    in_maps = prep_inputs(**inputs)
    nc, _ = build_program()
    res = run_bass_kernel_spmd(nc, in_maps, core_ids=list(range(NCORE)))
    parts = [np.asarray(r["outT"]) for r in res.results]
    full = np.concatenate(parts, axis=1)
    return np.ascontiguousarray(full.T).reshape(1, SEQ, D).astype(np.float32)
